# version 34
# baseline (speedup 1.0000x reference)
import os
import numpy as np
import concourse.bass as bass
import concourse.mybir as mybir
from concourse.bass_utils import run_bass_kernel_spmd
from contextlib import ExitStack

F32 = mybir.dt.float32
BF16 = mybir.dt.bfloat16
AF = mybir.ActivationFunctionType
OP = mybir.AluOpType

B, T, D = 2, 8192, 1024
NCORES = 8
GRP = 4
RPC = (B * T) // NCORES
L = 128
PST = 512
FST = 256
C = 256
EARLY_A = int(os.environ.get("EARLY_A", "1"))
CHASE = int(os.environ.get("CHASE", "1"))


class Sem:
    def __init__(self, ctx, nc, name):
        self.h = ctx.enter_context(nc.semaphore(name))
        self.n = 0

    def inc(self, instr, k=1):
        instr.then_inc(self.h, k)
        self.n += k
        return self.n


def build_nc(rpc=RPC, debug=False):
    nch = rpc // L
    ntiles = rpc // 128
    npst = rpc // PST
    nfst = rpc // FST
    cpst = FST // L
    seq_t = GRP * rpc

    nc = bass.Bass("TRN2", target_bir_lowering=False, debug=debug)

    dx = nc.dram_tensor("x", [rpc, D], F32, kind="ExternalInput")
    dy = nc.dram_tensor("y", [rpc, D], F32, kind="ExternalOutput")
    din = {}
    for nm, shape in (("basis", [D, C]), ("w1", [D, D]), ("b1", [D]), ("w2", [D, 512]),
                      ("b2", [512]), ("w3", [512, 256]), ("b3", [256]), ("w4", [256, 1]),
                      ("spack", [1, 5]), ("gw1", [D, 512]), ("gb1", [512]), ("gw2", [512, 256]),
                      ("gb2", [256]), ("gw3", [256, 1]), ("ident", [128, 128]),
                      ("cpack", [128, 386]), ("rpack", [1, nch + 2 * GRP + 128]),
                      ("cfs", [4, 256, 256]), ("onesr", [1, 128])):
        din[nm] = nc.dram_tensor(nm, shape, F32, kind="ExternalInput")

    CCN = C * C
    cc_in = nc.dram_tensor("cc_in", [1, CCN], BF16)
    cc_out = nc.dram_tensor("cc_out", [GRP, CCN], BF16)
    ccx_in = nc.dram_tensor("ccx_in", [1, D], BF16)
    ccx_out = nc.dram_tensor("ccx_out", [GRP, D], BF16)
    bstage = nc.dram_tensor("bstage", [1, 32], F32)
    ab_dram = nc.dram_tensor("ab_dram", [1, 768], F32)

    ctx = ExitStack()
    with ctx:
        sb = lambda name, shape, dt: ctx.enter_context(nc.sbuf_tensor(name, shape, dt))
        ps = lambda name, shape, dt: ctx.enter_context(nc.psum_tensor(name, shape, dt))

        class View:
            def __init__(self, t, base, width):
                self.t, self.base, self.w = t, base, width

            def __getitem__(self, idx):
                p, c = idx
                c0 = 0 if c.start is None else c.start
                c1 = self.w if c.stop is None else c.stop
                return self.t[p, self.base + c0: self.base + c1]


        xn_fm = sb("xn_fm", [128, 8 * rpc], BF16)
        q_fm = sb("q_fm", [128, 2 * rpc], BF16)
        k_fm = sb("k_fm", [128, 2 * rpc], BF16)
        kd_tm = sb("kd_tm", [128, nch * C], BF16)
        v_tm = sb("v_tm", [128, nch * C], BF16)
        t_st = sb("t_st", [128, nch * 512], BF16)
        w1_sb = sb("w1_sb", [128, 8 * D], BF16)
        w2_sb = sb("w2_sb", [128, 8 * 512], BF16)
        w3_sb = sb("w3_sb", [128, 4 * 256], BF16)
        w4_sb = sb("w4_sb", [128, 2], BF16)
        g1_sb = sb("g1_sb", [128, 8 * 512], BF16)
        g2_sb = sb("g2_sb", [128, 4 * 256], BF16)
        g3_sb = sb("g3_sb", [128, 2], BF16)
        owt_sb = sb("owt_sb", [128, 2 * D], BF16)
        b1_sb = sb("b1_sb", [128, 8], F32)
        b2_sb = sb("b2_sb", [128, 4], F32)
        b3_sb = sb("b3_sb", [128, 2], F32)
        gb1_sb = sb("gb1_sb", [128, 4], F32)
        gb2_sb = sb("gb2_sb", [128, 2], F32)
        sc_in = sb("sc_in", [1, 8], F32)
        id_sb = sb("id_sb", [128, 128], BF16)
        rpack_sb = sb("rpack_sb", [1, nch + 2 * GRP + 128], F32)
        EJ_sb = View(rpack_sb, 0, nch)
        E4_sb = View(rpack_sb, nch, GRP)
        M4_sb = View(rpack_sb, nch + GRP, GRP)
        onesr_bf = sb("onesr_bf", [1, 128], BF16)
        onesc_bf = sb("onesc_bf", [128, 1], BF16)
        DT_sb = sb("DT_sb", [128, 128], BF16)
        DQ_sb = sb("DQ_sb", [128, 128], BF16)
        dk_col = sb("dk_col", [128, 1], F32)
        dL_col = sb("dL_col", [128, 1], F32)
        bcast = sb("bcast", [128, 32], F32)
        row_sb = sb("row_sb", [1, 32], F32)
        sclr = sb("sclr", [1, 16], F32)
        c1_col = sb("c1_col", [128, 1], F32)
        ceps_col = sb("ceps_col", [128, 1], F32)
        xnsum_sb = sb("xnsum_sb", [128, 8], BF16)
        xnt_sb = sb("xnt_sb", [128, 8], F32)
        xnt_bf = sb("xnt_bf", [128, 8], BF16)
        sin_sb = sb("sin_sb", [128, 512], F32)
        gath_sb = sb("gath_sb", [128, 512], BF16)
        xg_sb = [sb(f"xg_sb{r}", [128, 8], BF16) for r in range(GRP)]
        gg1_sb = sb("gg1_sb", [128, 4], BF16)
        gg2_sb = sb("gg2_sb", [128, 2], BF16)
        dsrow_sb = sb("dsrow_sb", [1, nch], F32)
        wrrow_sb = sb("wrrow_sb", [1, GRP], F32)
        xin = [sb(f"xin{i}", [128, D], F32) for i in range(4)]
        stats = [sb(f"stats{i}", [128, 12], F32) for i in range(4)]
        mv = [sb(f"mv{i}", [128, 2], F32) for i in range(4)]
        rcol = [sb(f"rcol{i}", [128, 1], F32) for i in range(4)]
        mrcol = [sb(f"mrcol{i}", [128, 1], F32) for i in range(2)]
        xn_tm = [sb(f"xn_tm{i}", [128, D], BF16) for i in range(2)]
        sj_bf = [sb(f"sj_bf{i}", [128, 512], BF16) for i in range(2)]
        qd_tmp = [sb(f"qd_tmp{i}", [128, 256], BF16) for i in range(2)]
        sco_sb = [sb(f"sco_sb{i}", [128, 128], BF16) for i in range(2)]
        retr_sb = [sb(f"retr_sb{i}", [128, C], BF16) for i in range(2)]
        attn_sb = sb("attn_sb", [128, 2 * D], BF16)
        attn_sb_b = sb("attn_sb_b", [128, 2 * D], BF16)
        attn2 = [attn_sb, attn_sb_b]
        A_sb = sb("A_sb", [128, 8 * FST], BF16)
        B_sb = sb("B_sb", [128, 8 * FST], BF16)
        comb_sb = sb("comb_sb", [128, 8 * FST], BF16)
        h1_bf = sb("h1_bf", [128, 8 * FST], BF16)
        h2_bf = sb("h2_bf", [128, 4 * FST], BF16)
        h3_bf = sb("h3_bf", [128, 2 * FST], BF16)
        th4_row = sb("th4_row", [1, FST], F32)
        crows = sb("crows", [1, 3 * FST], F32)
        abrow = sb("abrow", [1, 2 * FST], F32)
        arow_bf = sb("arow_bf", [1, FST], BF16)
        brow_bf = sb("brow_bf", [1, FST], BF16)
        c2_bf = sb("c2_bf", [1, FST], BF16)
        abcol = sb("abcol", [128, 6], F32)
        y_sb = [sb(f"y_sb{i}", [128, D], F32) for i in range(2)]
        E1_sb = View(y_sb[0], 0, 128)
        M1_sb = View(y_sb[0], 128, 128)
        EQ_sb = View(y_sb[0], 256, 128)
        CK_sb = View(y_sb[0], 384, 2)
        cpack_sb = View(y_sb[0], 0, 386)
        basis_bf = View(A_sb, 0, 8 * C)
        qw_sb = View(B_sb, 0, 8 * C)
        kw_sb = View(h1_bf, 0, 8 * C)
        vw_sb = View(attn_sb_b, 0, 8 * C)
        basT = View(comb_sb, 0, 2 * D)
        qc_bf = View(h1_bf, 0, 2 * C)
        kc_bf = View(h1_bf, 512, 2 * C)
        vc_bf = View(h1_bf, 1024, 2 * C)
        oc_bf = View(h1_bf, 1536, 2 * C)
        qcT = View(attn_sb, 0, 2 * C)
        kcT = View(attn_sb, 512, 2 * C)
        vcT = View(attn_sb, 1024, 2 * C)
        ocT = View(attn_sb, 1536, 2 * C)

        pbfa = ps("pbfa", [128, 1024], BF16)
        pbfb = ps("pbfb", [128, 1024], BF16)
        pf = [ps(f"pf{i}", [128, 512], F32) for i in range(6)]

        sem_dx = [Sem(ctx, nc, f"dx{i}") for i in range(4)]
        sem_dwa = Sem(ctx, nc, "dwa")
        sem_dw0 = Sem(ctx, nc, "dw0")
        sem_dw00 = Sem(ctx, nc, "dw00")
        sem_dwb = Sem(ctx, nc, "dwb")
        sem_dbc = Sem(ctx, nc, "dbc")
        sem_dcc = Sem(ctx, nc, "dcc")
        sem_dy = [Sem(ctx, nc, "dy0"), Sem(ctx, nc, "dy1")]
        sem_dab = Sem(ctx, nc, "dab")
        sem_vr = Sem(ctx, nc, "vr")
        sem_dxg = Sem(ctx, nc, "dxg")
        sem_ttr = Sem(ctx, nc, "ttr")
        sem_tpj = Sem(ctx, nc, "tpj")
        sem_tv = Sem(ctx, nc, "tv")
        sem_tkt = Sem(ctx, nc, "tkt")
        sem_cc = Sem(ctx, nc, "scc")
        sem_ccx = Sem(ctx, nc, "sccx")
        sem_nrm = Sem(ctx, nc, "snrm")
        sem_fin = Sem(ctx, nc, "sfin")
        sv = Sem(ctx, nc, "sv")
        ss = Sem(ctx, nc, "ss")
        sq = Sem(ctx, nc, "sq")
        sem_vsu = Sem(ctx, nc, "svsu")
        sem_vbn = Sem(ctx, nc, "svbn")
        sem_vmr = Sem(ctx, nc, "svmr")
        sem_txs = Sem(ctx, nc, "stxs")
        sem_s3 = Sem(ctx, nc, "ss3")
        sem_v3 = Sem(ctx, nc, "sv3")

        from concourse.hw_specs import get_activation_tables
        _tabs = list(get_activation_tables(nc.m.arch))
        SET_LNEXP = _tabs.index("natural_log_exp_and_others")
        SET_GELU = _tabs.index("gelu_and_others")
        SET_RSQRT = _tabs.index("reciprocal_sqrt_and_small")

        def preload(e, set_idx):
            e.add_instruction(mybir.InstLoadActFuncSet(
                name=nc.get_next_instruction_name(), act_func_set_id=set_idx,
                ins=[], outs=[]))

        at = {}
        nsp_ap = lambda: bcast[:, 0:1]
        osc_ap = lambda: bcast[:, 1:2]
        dS_ap = lambda j: bcast[:, 2 + j:3 + j]
        wr_ap = lambda r: bcast[:, 2 + nch + r:3 + nch + r]
        blk = ctx.enter_context(nc.Block())

        def sync_all(e):
            for i in range(ntiles):
                if i >= 4:
                    e.wait_ge(sem_nrm.h, i - 3)
                e.dma_start(out=xin[i % 4][:, :], in_=dx[i * 128:(i + 1) * 128, :]).then_inc(sem_dx[i % 4].h, 16)
                sem_dx[i % 4].n += 16
            for gt in range(ntiles):
                if gt < 4:
                    e.wait_ge(sem_nrm.h, ntiles)
                else:
                    e.wait_ge(sem_fin.h, gt - 3)
                e.dma_start(out=xin[gt % 4][:, :], in_=dx[gt * 128:(gt + 1) * 128, :]).then_inc(sem_dx[gt % 4].h, 16)
                sem_dx[gt % 4].n += 16
                k = gt - 4
                if k >= 0:
                    e.wait_ge(sem_fin.h, k + 1)
                    ins = e.dma_start(out=dy[k * 128:(k + 1) * 128, :], in_=y_sb[k % 2][:, :])
                    sem_dy[k % 2].inc(ins, 16)
            for k in range(ntiles - 4, ntiles):
                e.wait_ge(sem_fin.h, k + 1)
                ins = e.dma_start(out=dy[k * 128:(k + 1) * 128, :], in_=y_sb[k % 2][:, :])
                sem_dy[k % 2].inc(ins, 16)
            e.wait_ge(sem_dy[0].h, sem_dy[0].n)
            e.wait_ge(sem_dy[1].h, sem_dy[1].n)

        blk.sync(sync_all)

        dw_at = {}

        def g_loads(e):
            items = [
                (id_sb[:, :], din["ident"][:, :], "id"),
                (onesr_bf[:, :], din["onesr"][:, :], "onesr"),
                (onesc_bf[:, :], din["onesr"].ap().rearrange("a (p b) -> (a p) b", b=1), "onesc"),
                (sc_in[0:1, 0:5], din["spack"][:, :], "spack"),
                (cpack_sb[:, :], din["cpack"][:, :], "cpack"),
                (rpack_sb[0:1, :], din["rpack"][:, :], "rpack"),
                (basis_bf[:, :], din["basis"].ap().rearrange("(c p) n -> p c n", p=128), "basis"),
                (h1_bf[:, 0:2048], din["cfs"].ap().rearrange("w (c p) n -> p w c n", p=128), "cfs"),
                (b1_sb[:, :], din["b1"].ap().rearrange("(c p) -> p c", p=128), "b1"),
                (b2_sb[:, :], din["b2"].ap().rearrange("(c p) -> p c", p=128), "b2"),
                (b3_sb[:, :], din["b3"].ap().rearrange("(c p) -> p c", p=128), "b3"),
                (gb1_sb[:, :], din["gb1"].ap().rearrange("(c p) -> p c", p=128), "gb1"),
                (gb2_sb[:, :], din["gb2"].ap().rearrange("(c p) -> p c", p=128), "gb2"),
                (w1_sb[:, :], din["w1"].ap().rearrange("(c p) n -> p c n", p=128), "w1"),
                (w2_sb[:, :], din["w2"].ap().rearrange("(c p) n -> p c n", p=128), "w2"),
                (w3_sb[:, :], din["w3"].ap().rearrange("(c p) n -> p c n", p=128), "w3"),
                (w4_sb[:, :], din["w4"].ap().rearrange("(c p) n -> p (c n)", p=128), "w4"),
                (g1_sb[:, :], din["gw1"].ap().rearrange("(c p) n -> p c n", p=128), "g1"),
                (g2_sb[:, :], din["gw2"].ap().rearrange("(c p) n -> p c n", p=128), "g2"),
                (g3_sb[:, :], din["gw3"].ap().rearrange("(c p) n -> p (c n)", p=128), "g3"),
            ]
            slow_ok = {"b1", "b2", "b3", "gb1", "gb2", "onesc", "w4", "g3"}
            grp_b = {"w1", "w2", "w3", "w4", "g1", "g2", "g3",
                     "b1", "b2", "b3", "gb1", "gb2"}
            grp_00 = {"id", "basis", "cfs"}
            grp_0 = {"onesr", "onesc", "spack", "cpack", "rpack"}
            items.sort(key=lambda it: 0 if it[2] in grp_00 else
                       (1 if it[2] in grp_0 else (3 if it[2] in grp_b else 2)))
            for out_ap, in_ap, key in items:
                ins = e.dma_start(out=out_ap, in_=in_ap,
                                  allow_slow_non_contiguous=(key in slow_ok))
                sem = (sem_dw00 if key in grp_00 else
                       (sem_dw0 if key in grp_0 else (sem_dwb if key in grp_b else sem_dwa)))
                dw_at[key] = sem.inc(ins, 16)

        blk.gpsimd(g_loads)

        def s_proj_ev(e, st):
            e.wait_ge(sem_tpj.h, st + 1)
            for nm, banks, dst in (("q", (pf[1], pf[2]), q_fm), ("k", (pf[3], pf[4]), k_fm)):
                for mt in range(2):
                    ins = e.activation(dst[:, mt * rpc + st * PST: mt * rpc + st * PST + PST],
                                       banks[mt][:, 0:512], AF.Copy)
                    at[f"s_{nm}ev{st}_{mt}"] = ss.inc(ins)
            for tt in range(4):
                gt = 4 * st + tt
                e.wait_ge(sem_tv.h, gt + 1)
                ins = e.activation(v_tm[:, gt * C: gt * C + C], pf[5][:, 0:C], AF.Copy)
                at[f"s_vev{gt}"] = ss.inc(ins)
                e.wait_ge(sem_tkt.h, gt + 1)
                ins = e.activation(kd_tm[:, gt * C: gt * C + C], pbfb[:, 0:C],
                                   AF.Identity, scale=dk_col[:, 0:1])
                at[f"s_kdev{gt}"] = ss.inc(ins)

        def scalar_mega1(e):
            preload(e, SET_LNEXP)
            e.wait_ge(sem_vsu.h, 2)
            ins = e.activation(sclr[0:1, 1:2], sclr[0:1, 0:1], AF.Exp)
            k = ss.inc(ins)
            e.wait_ge(ss.h, k)
            ins = e.activation(sclr[0:1, 2:3], sclr[0:1, 1:2], AF.Ln, bias=c1_col[0:1, 0:1])
            at["s_sp"] = ss.inc(ins)
            e.wait_ge(sem_vsu.h, 3)
            ins = e.activation(dsrow_sb[0:1, :], EJ_sb[0:1, :], AF.Exp, scale=sclr[0:1, 3:4])
            k = ss.inc(ins)
            ins = e.activation(wrrow_sb[0:1, :], E4_sb[0:1, :], AF.Exp, scale=sclr[0:1, 3:4])
            at["s_rows"] = ss.inc(ins)
            e.wait_ge(sem_dbc.h, 32)
            ins = e.activation(gath_sb[:, 0:128], E1_sb[:, :], AF.Exp, scale=nsp_ap())
            at["s_dtraw"] = ss.inc(ins)
            ins = e.activation(DQ_sb[:, :], EQ_sb[:, :], AF.Exp, scale=nsp_ap())
            at["s_dq"] = ss.inc(ins)
            e.activation(dk_col[:, 0:1], CK_sb[:, 0:1], AF.Exp, scale=nsp_ap())
            ins = e.activation(dL_col[:, 0:1], CK_sb[:, 1:2], AF.Exp, scale=nsp_ap())
            at["s_dk"] = ss.inc(ins)
            for i in range(ntiles):
                e.wait_ge(sem_vbn.h, i + 1)
                ins = e.activation(rcol[i % 4][:, 0:1], mv[i % 4][:, 1:2], AF.Ln, bias=ceps_col[:, 0:1])
                k = ss.inc(ins)
                e.wait_ge(ss.h, k)
                ins = e.activation(rcol[i % 4][:, 0:1], rcol[i % 4][:, 0:1], AF.Exp, scale=-0.5)
                at[f"s_r{i}"] = ss.inc(ins)
                if i >= 1:
                    e.wait_ge(sem_ttr.h, i)
                    src = bass.AP(pbfa, 0, [[1024, 128], [128, 8], [1, 128]])
                    dst = bass.AP(xn_fm, (i - 1) * 128, [[8 * rpc, 128], [rpc, 8], [1, 128]])
                    ins = e.activation(dst, src, AF.Copy)
                    at[f"s_xnev{i - 1}"] = ss.inc(ins)
                    if (i - 1) % 4 == 3:
                        s_proj_ev(e, (i - 1) // 4)
            e.wait_ge(sem_ttr.h, ntiles)
            src = bass.AP(pbfa, 0, [[1024, 128], [128, 8], [1, 128]])
            dst = bass.AP(xn_fm, (ntiles - 1) * 128, [[8 * rpc, 128], [rpc, 8], [1, 128]])
            ins = e.activation(dst, src, AF.Copy)
            at[f"s_xnev{ntiles - 1}"] = ss.inc(ins)
            s_proj_ev(e, (ntiles - 1) // 4)

        blk.scalar(scalar_mega1)

        def v_setup1(e):
            e.memset(row_sb[0:1, :], 0.0)
            e.memset(c1_col[:, :], 1.0)
            e.memset(ceps_col[:, :], 1e-5)
            e.wait_ge(sem_dw0.h, sem_dw0.n)
            ins = e.tensor_scalar(sclr[0:1, 0:1], sc_in[0:1, 3:4], -1.0, 80.0, OP.mult, OP.min)
            ins.then_inc(sem_vsu.h, 2)
            sem_vsu.n += 2

        blk.vector(v_setup1)

        def v_setup2(e):
            e.wait_ge(ss.h, at["s_sp"])
            ins = e.tensor_scalar(sclr[0:1, 3:4], sclr[0:1, 2:3], -1.0, None, OP.mult)
            ins.then_inc(sem_vsu.h, 1)
            sem_vsu.n += 1
            e.tensor_scalar(sclr[0:1, 4:5], sc_in[0:1, 0:1], 0.5, None, OP.mult)
            e.tensor_scalar(sclr[0:1, 5:6], sc_in[0:1, 1:2], 0.5, None, OP.mult)
            e.tensor_scalar(sclr[0:1, 6:7], sc_in[0:1, 2:3], 0.3, None, OP.mult)
            ins = e.tensor_scalar(sclr[0:1, 7:8], sc_in[0:1, 2:3], 0.7, None, OP.mult)
            at["v_scl"] = sv.inc(ins)

        blk.vector(v_setup2)

        def v_setup3(e):
            e.wait_ge(ss.h, at["s_rows"])
            ins = e.tensor_tensor(wrrow_sb[0:1, :], wrrow_sb[0:1, :], M4_sb[0:1, :], OP.mult)
            k = sv.inc(ins)
            e.wait_ge(sv.h, k)
            e.tensor_copy(row_sb[0:1, 0:1], sclr[0:1, 3:4])
            e.tensor_copy(row_sb[0:1, 1:2], sc_in[0:1, 4:5])
            e.tensor_copy(row_sb[0:1, 2:2 + nch], dsrow_sb[0:1, :])
            ins = e.tensor_copy(row_sb[0:1, 2 + nch:2 + nch + GRP], wrrow_sb[0:1, :])
            at["v_row"] = sv.inc(ins)

        blk.vector(v_setup3)

        def g_bcast(e):
            e.wait_ge(sv.h, at["v_row"])
            ins = e.dma_start(out=bstage[:, :], in_=row_sb[0:1, :])
            k = sem_dbc.inc(ins, 16)
            e.wait_ge(sem_dbc.h, k)
            ins = e.dma_start(out=bcast[:, :], in_=bass.AP(bstage, 0, [[0, 128], [1, 32]]))
            dw_at["bcast"] = sem_dbc.inc(ins, 16)

        blk.gpsimd(g_bcast)

        def v_decay(e):
            e.wait_ge(ss.h, at["s_dtraw"])
            ins = e.tensor_tensor(DT_sb[:, :], gath_sb[:, 0:128], M1_sb[:, :], OP.mult)
            at["v_dt"] = sv.inc(ins)

        blk.vector(v_decay)

        trn = {"n": 0}

        def emit_tr_batch(specs):
            i = trn["n"]
            trn["n"] += 1
            dst_sb, dst_col = specs[0][0], specs[0][1]
            nb = len(specs)

            def t_f(e):
                if i == 0:
                    e.wait_ge(sem_dw00.h, sem_dw00.n)
                else:
                    e.wait_ge(sv.h, at[f"v_tr{i - 1}"])
                last = None
                for bi, (_, _, src_sb, src_col) in enumerate(specs):
                    last = e.transpose(pbfb[:, bi * 128:(bi + 1) * 128],
                                       src_sb[:, src_col:src_col + 128], id_sb[:, :])
                at[f"t_tr{i}"] = sq.inc(last)

            def v_f(e):
                e.wait_ge(sq.h, at[f"t_tr{i}"])
                ins = e.tensor_copy(dst_sb[:, dst_col:dst_col + nb * 128], pbfb[:, 0:nb * 128])
                at[f"v_tr{i}"] = sv.inc(ins)

            blk.tensor(t_f)
            blk.vector(v_f)

        for jck in range(2):
            for vh in range(2):
                emit_tr_batch([(basT, jck * D + (vh * 4) * 128, basis_bf, vck * C + jck * 128)
                               for vck in range(vh * 4, vh * 4 + 4)])
        for w_bf, w_T in ((qc_bf, qcT), (kc_bf, kcT), (vc_bf, vcT), (oc_bf, ocT)):
            emit_tr_batch([(w_T, 0, w_bf, cck * C + jck * 128)
                           for jck in range(2) for cck in range(2)])
        n_tr_total = trn["n"]

        prev_w = {"q": None, "k": "q", "v": "k"}

        def emit_w_pair(wT, dst, nm):
            for mv_ in range(8):
                bank = pf[4] if mv_ % 2 == 0 else pf[5]

                def t_f(e, mv_=mv_, wT=wT, bank=bank, nm=nm):
                    if mv_ == 0 and nm == "q":
                        e.wait_ge(sv.h, at[f"v_tr{n_tr_total - 1}"])
                    if mv_ < 2 and prev_w[nm] is not None:
                        e.wait_ge(sv.h, at[f"v_w{prev_w[nm]}{mv_ + 6}"])
                    prev = at.get(f"v_w{nm}{mv_ - 2}")
                    if prev is not None:
                        e.wait_ge(sv.h, prev)
                    last = None
                    for kj in range(2):
                        last = e.matmul(bank[:, 0:C],
                                        basT[:, kj * D + mv_ * 128: kj * D + mv_ * 128 + 128],
                                        wT[:, kj * C: kj * C + C],
                                        start=(kj == 0), stop=(kj == 1))
                    at[f"t_w{nm}{mv_}"] = sq.inc(last)

                def v_f(e, mv_=mv_, dst=dst, bank=bank, nm=nm):
                    e.wait_ge(sq.h, at[f"t_w{nm}{mv_}"])
                    ins = e.tensor_copy(dst[:, mv_ * C: mv_ * C + C], bank[:, 0:C])
                    at[f"v_w{nm}{mv_}"] = sv.inc(ins)

                blk.tensor(t_f)
                blk.vector(v_f)

        emit_w_pair(qcT, qw_sb, "q")
        emit_w_pair(kcT, kw_sb, "k")
        emit_w_pair(vcT, vw_sb, "v")

        for mc in range(2):
            for nh in range(2):
                bank = pf[4] if nh == 0 else pf[5]
                prev_key = (f"v_ow{mc}{nh - 1}" if nh == 1 else (f"v_ow{mc - 1}{1}" if mc > 0 else "v_wv7"))

                def t_f(e, mc=mc, nh=nh, bank=bank, prev_key=prev_key):
                    e.wait_ge(sv.h, at[prev_key])
                    last = None
                    for kj in range(2):
                        last = e.matmul(bank[:, 0:512],
                                        ocT[:, kj * C + mc * 128: kj * C + mc * 128 + 128],
                                        basT[:, kj * D + nh * 512: kj * D + nh * 512 + 512],
                                        start=(kj == 0), stop=(kj == 1))
                    at[f"t_ow{mc}{nh}"] = sq.inc(last)

                def v_f(e, mc=mc, nh=nh, bank=bank):
                    e.wait_ge(sq.h, at[f"t_ow{mc}{nh}"])
                    e.wait_ge(sem_dbc.h, dw_at["bcast"])
                    ins = e.tensor_scalar(owt_sb[:, mc * D + nh * 512: mc * D + nh * 512 + 512],
                                          bank[:, 0:512], osc_ap(), None, OP.mult)
                    at[f"v_ow{mc}{nh}"] = sv.inc(ins)

                blk.tensor(t_f)
                blk.vector(v_f)

        def v_tile(i):
            def f(e):
                e.wait_ge(sem_dx[i % 4].h, 16 * (i // 4 + 1))
                if i >= 4:
                    e.wait_ge(ss.h, at[f"s_r{i - 4}"])
                e.bn_stats(stats[i % 4][:, 0:6], xin[i % 4][:, 0:512])
                ins = e.bn_stats(stats[i % 4][:, 6:12], xin[i % 4][:, 512:1024])
                k = sv.inc(ins)
                e.wait_ge(sv.h, k)
                ins = e.bn_aggr(mv[i % 4][:, 0:2], stats[i % 4][:, 0:12])
                ins.then_inc(sem_vbn.h, 1)
                sem_vbn.n += 1
            return f

        def v_norm(i):
            def f(e):
                e.wait_ge(ss.h, at[f"s_r{i}"])
                if i >= 2:
                    e.wait_ge(sem_txs.h, i - 1)
                ins = e.tensor_scalar(xn_tm[i % 2][:, :], xin[i % 4][:, :],
                                      mv[i % 4][:, 0:1], rcol[i % 4][:, 0:1],
                                      OP.subtract, OP.mult)
                ins.then_inc(sem_nrm.h, 1)
                sem_nrm.n += 1
            return f

        def t_tile(i):
            def f(e):
                e.wait_ge(sem_nrm.h, i + 1)
                if i == 0:
                    e.wait_ge(sem_dw00.h, sem_dw00.n)
                    e.wait_ge(sem_dw0.h, sem_dw0.n)
                if i >= 1:
                    e.wait_ge(ss.h, at[f"s_xnev{i - 1}"])
                last = None
                for ck in range(8):
                    last = e.transpose(pbfa[:, ck * 128:(ck + 1) * 128],
                                       xn_tm[i % 2][:, ck * 128:(ck + 1) * 128], id_sb[:, :])
                last.then_inc(sem_ttr.h, 1)
                sem_ttr.n += 1
                last = None
                for ck in range(8):
                    last = e.matmul(pf[0][:, ck:ck + 1],
                                    xn_tm[i % 2][:, ck * 128:(ck + 1) * 128],
                                    onesc_bf[:, 0:1],
                                    start=(i == 0 and ck == 0), stop=(i == ntiles - 1 and ck == 7))
                last.then_inc(sem_txs.h, 1)
                sem_txs.n += 1
            return f

        def t_proj(st):
            def f(e):
                e.wait_ge(ss.h, at[f"s_xnev{4 * st + 3}"])
                e.wait_ge(sv.h, at["v_wv7"])
                if st >= 1:
                    e.wait_ge(ss.h, at[f"s_kev{st - 1}_1"])
                else:
                    e.wait_ge(sv.h, at["v_ow11"])
                last = None
                for (wsb, banks, nm) in ((qw_sb, (pf[1], pf[2]), "q"), (kw_sb, (pf[3], pf[4]), "k")):
                    for mt in range(2):
                        for kc in range(8):
                            last = e.matmul(banks[mt][:, 0:512],
                                            wsb[:, kc * C + mt * 128: kc * C + mt * 128 + 128],
                                            xn_fm[:, kc * rpc + st * PST: kc * rpc + st * PST + PST],
                                            start=(kc == 0), stop=(kc == 7))
                last.then_inc(sem_tpj.h, 1)
                sem_tpj.n += 1
            return f

        def t_v(gt, st):
            def f(e):
                prev = at.get(f"s_vev{gt - 1}")
                if prev is not None:
                    e.wait_ge(ss.h, prev)
                elif st == 0:
                    e.wait_ge(sv.h, at["v_ow11"])
                e.wait_ge(ss.h, at[f"s_xnev{gt}"])
                last = None
                for kc in range(8):
                    last = e.matmul(pf[5][:, 0:C],
                                    xn_fm[:, kc * rpc + gt * 128: kc * rpc + gt * 128 + 128],
                                    vw_sb[:, kc * C: kc * C + C],
                                    start=(kc == 0), stop=(kc == 7))
                last.then_inc(sem_tv.h, 1)
                sem_tv.n += 1
            return f

        def t_ktr(gt, st):
            def f(e):
                e.wait_ge(ss.h, at[f"s_kev{st}_1"])
                prev = at.get(f"s_kdev{gt - 1}")
                if prev is not None:
                    e.wait_ge(ss.h, prev)
                last = None
                for ct in range(2):
                    last = e.transpose(pbfb[:, ct * 128: ct * 128 + 128],
                                       k_fm[:, ct * rpc + gt * 128: ct * rpc + gt * 128 + 128],
                                       id_sb[:, :])
                last.then_inc(sem_tkt.h, 1)
                sem_tkt.n += 1
            return f

        for i in range(ntiles + 1):
            if i < ntiles:
                blk.vector(v_tile(i))
            if i >= 1:
                j0 = i - 1
                blk.vector(v_norm(j0))
                blk.tensor(t_tile(j0))
                if j0 % 4 == 3:
                    st = j0 // 4
                    blk.tensor(t_proj(st))
                    for tt in range(4):
                        gt = 4 * st + tt
                        blk.tensor(t_v(gt, st))
                        blk.tensor(t_ktr(gt, st))

        def v_xnsum(e):
            e.wait_ge(sem_txs.h, ntiles)
            ins = e.tensor_copy(xnsum_sb[:, :], pf[0][:, 0:8])
            at["v_xnsum"] = sv.inc(ins)

        blk.vector(v_xnsum)

        for jj in range(nch - 1, -1, -1):
            bank = pf[1] if jj % 2 else pf[2]

            def t_cj(e, jj=jj, bank=bank):
                if jj == nch - 1:
                    e.wait_ge(ss.h, at[f"s_kdev{nch - 1}"])
                    e.wait_ge(ss.h, at[f"s_vev{nch - 1}"])
                    e.wait_ge(ss.h, at[f"s_kev{npst - 1}_1"])
                prev = at.get(f"v_T{jj + 2}")
                if prev is not None:
                    e.wait_ge(sv.h, prev)
                last = None
                for mt in range(2):
                    last = e.matmul(bank[:, mt * 256: mt * 256 + 256],
                                    kd_tm[:, jj * C + mt * 128: jj * C + mt * 128 + 128],
                                    v_tm[:, jj * C: jj * C + C],
                                    start=True, stop=True)
                at[f"t_C{jj}"] = sq.inc(last)

            def v_tj(e, jj=jj, bank=bank):
                e.wait_ge(sq.h, at[f"t_C{jj}"])
                if jj == nch - 1:
                    ins = e.tensor_copy(t_st[:, jj * 512:(jj + 1) * 512], bank[:, 0:512])
                else:
                    e.wait_ge(sv.h, at[f"v_T{jj + 1}"])
                    ins = e.scalar_tensor_tensor(t_st[:, jj * 512:(jj + 1) * 512],
                                                 t_st[:, (jj + 1) * 512:(jj + 2) * 512],
                                                 dL_col[:, 0:1], bank[:, 0:512],
                                                 OP.mult, OP.add)
                at[f"v_T{jj}"] = sv.inc(ins)

            blk.tensor(t_cj)
            blk.vector(v_tj)

        PERM = [4, 5, 6, 7, 0, 1, 2, 3]
        SIGNP = [u for u in range(8) if [1, 1, -1, -1, -1, -1, 1, 1][u] > 0]
        SIGNM = [u for u in range(8) if [1, 1, -1, -1, -1, -1, 1, 1][u] < 0]

        s3sched = []
        for st_ in range(nfst):
            if st_ == 0:
                s3sched += [("g1_0_0", 8)]
            else:
                s3sched += [(f"g1_{st_}_0", 8), (f"atev_{st_}_0", 2),
                            (f"atev_{st_}_1", 2), (f"bpev_{st_}", 4)]
            s3sched += [(f"g2_{st_}_0", 4), (f"g3_{st_}_0", 2), (f"th4_{st_}_0", 1)]
            if st_ == 0:
                s3sched += [("gf1", 1), ("gf2", 1), ("gf3", 1)]
            s3sched += [(f"crow_{st_}_0", 1)]
            s3sched += [(f"g1_{st_}_1", 8), (f"g2_{st_}_1", 4), (f"g3_{st_}_1", 2),
                        (f"th4_{st_}_1", 1), (f"g1_{st_}_2", 8)]
            if st_ + 1 < nfst:
                s3sched += [(f"aev_{st_ + 1}", 4)]
            s3sched += [(f"g2_{st_}_2", 4), (f"g3_{st_}_2", 2), (f"th4_{st_}_2", 1)]
        s3pos = {}
        _c = 0
        for _n, _k in s3sched:
            s3pos[_n] = _c
            _c += _k

        def s3g1(st, cyc, k):
            return s3pos[f"g1_{st}_{cyc}"] + k
        def s3g2(st, cyc, k):
            return s3pos[f"g2_{st}_{cyc}"] + k
        def s3g3(st, cyc, k):
            return s3pos[f"g3_{st}_{cyc}"] + k
        def s3th4(st, cyc):
            return s3pos[f"th4_{st}_{cyc}"] + 1
        def s3aev(st):
            return s3pos[f"aev_{st}"] + 4
        def s3bpev(st):
            return s3pos[f"bpev_{st}"] + 4
        def s3atev(st, tt):
            return s3pos[f"atev_{st}_{tt}"] + 2

        def s3b(st, cyc):
            return s3g1(st, cyc, 0)

        def v_sj(j):
            def f(e):
                e.wait_ge(sv.h, at["v_A3"])
                e.wait_ge(sem_vr.h, GRP)
                prev = at.get(f"v_retr{j - 2}")
                if prev is not None:
                    e.wait_ge(sv.h, prev)
                if j == nch - 1:
                    ins = e.tensor_scalar(sj_bf[j % 2][:, :], sin_sb[:, :], dS_ap(j), None, OP.mult)
                else:
                    ins = e.scalar_tensor_tensor(sj_bf[j % 2][:, :], sin_sb[:, :], dS_ap(j),
                                                 t_st[:, (j + 1) * 512:(j + 2) * 512], OP.mult, OP.add)
                k = sv.inc(ins)
                for ck in range(2):
                    ins = e.tensor_tensor(qd_tmp[j % 2][:, ck * 128:(ck + 1) * 128],
                                          q_fm[:, ck * rpc + j * 128: ck * rpc + j * 128 + 128],
                                          DQ_sb[:, :], OP.mult)
                at[f"v_sj{j}"] = sv.inc(ins)
            return f

        def t_scores(j, st):
            def f(e):
                e.wait_ge(sv.h, at[f"v_sj{j}"])
                prev = at.get(f"v_sco{j - 2}")
                if prev is not None:
                    e.wait_ge(sv.h, prev)
                e.wait_ge(sem_s3.h, s3g3(max(st - 1, 0), 2, 2))
                rg = (j % 2) * 128
                last = None
                for kc in range(2):
                    last = e.matmul(pf[4][:, rg:rg + 128],
                                    k_fm[:, kc * rpc + j * 128: kc * rpc + j * 128 + 128],
                                    q_fm[:, kc * rpc + j * 128: kc * rpc + j * 128 + 128],
                                    start=(kc == 0), stop=(kc == 1))
                at[f"t_sco{j}"] = sq.inc(last)
            return f

        def v_scores(j):
            def f(e):
                e.wait_ge(sq.h, at[f"t_sco{j}"])
                prev = at.get(f"t_retr{j - 2}")
                if prev is not None:
                    e.wait_ge(sq.h, prev)
                rg = (j % 2) * 128
                ins = e.tensor_tensor(sco_sb[j % 2][:, :], pf[4][:, rg:rg + 128],
                                      DT_sb[:, :], OP.mult)
                at[f"v_sco{j}"] = sv.inc(ins)
            return f

        def t_retr(j, st):
            def f(e):
                e.wait_ge(sv.h, at[f"v_sco{j}"])
                pst = max(st - 1, 0)
                e.wait_ge(sem_v3.h, (pst * 2 + 2) * 8)
                e.wait_ge(sem_s3.h, s3th4(pst, 2))
                rg = (j % 2) * 256
                last = None
                for mc in range(2):
                    e.matmul(pf[5][:, rg + mc * 128: rg + mc * 128 + 128],
                             v_tm[:, j * C + mc * 128: j * C + mc * 128 + 128],
                             sco_sb[j % 2][:, :], start=True, stop=False)
                    e.matmul(pf[5][:, rg + mc * 128: rg + mc * 128 + 128],
                             sj_bf[j % 2][:, mc * 128: mc * 128 + 128],
                             qd_tmp[j % 2][:, 0:128], start=False, stop=False)
                    last = e.matmul(pf[5][:, rg + mc * 128: rg + mc * 128 + 128],
                                    sj_bf[j % 2][:, C + mc * 128: C + mc * 128 + 128],
                                    qd_tmp[j % 2][:, 128:256], start=False, stop=True)
                at[f"t_retr{j}"] = sq.inc(last)
            return f

        def v_retr(j):
            def f(e):
                e.wait_ge(sq.h, at[f"t_retr{j}"])
                rg = (j % 2) * 256
                ins = e.tensor_copy(retr_sb[j % 2][:, :], pf[5][:, rg:rg + C])
                at[f"v_retr{j}"] = sv.inc(ins)
            return f

        def t_oproj(j, st):
            def f(e):
                e.wait_ge(sv.h, at[f"v_retr{j}"])
                if st == 0:
                    prev = at.get(f"v_attnev{j - 1}")
                    if prev is not None:
                        e.wait_ge(sv.h, prev)
                    e.wait_ge(sv.h, at["v_Bev0"])
                    if nfst > 1:
                        e.wait_ge(sem_s3.h, s3aev(1))
                elif j % cpst == 0:
                    e.wait_ge(sem_s3.h, s3aev(st))
                else:
                    e.wait_ge(sem_s3.h, s3atev(st, j % cpst - 1))
                last = None
                for nh in range(2):
                    for kc in range(2):
                        last = e.matmul(pf[2 + nh][:, 0:512],
                                        retr_sb[j % 2][:, kc * 128: kc * 128 + 128],
                                        owt_sb[:, kc * D + nh * 512: kc * D + nh * 512 + 512],
                                        start=(kc == 0), stop=(kc == 1))
                at[f"t_o{j}"] = sq.inc(last)
            return f

        def v_oproj(j, st):
            tt = j - st * cpst

            def f(e):
                e.wait_ge(sq.h, at[f"t_o{j}"])
                ab = attn2[st % 2]
                e.tensor_copy(ab[:, tt * D: tt * D + 512], pf[2][:, 0:512])
                ins = e.tensor_copy(ab[:, tt * D + 512: tt * D + 1024], pf[3][:, 0:512])
                at[f"v_attnev{j}"] = sv.inc(ins)
            return f

        def emit_attn(st):
            for j in range(st * cpst, (st + 1) * cpst):
                blk.vector(v_sj(j))
                blk.tensor(t_scores(j, st))
                blk.vector(v_scores(j))
                blk.tensor(t_retr(j, st))
                blk.vector(v_retr(j))
                blk.tensor(t_oproj(j, st))
                if st == 0:
                    blk.vector(v_oproj(j, st))

        def xn_slice(ck, st):
            return xn_fm[:, ck * rpc + st * FST: ck * rpc + st * FST + FST]

        def t_A(st, half=None):
            mts = range(8) if half is None else range(half * 4, half * 4 + 4)
            def f(e):
                e.wait_ge(sem_dwb.h, sem_dwb.n)
                if half in (None, 0):
                    if st == 0:
                        e.wait_ge(sv.h, at["v_T0"])
                    else:
                        e.wait_ge(sv.h, at[f"v_Bev{st - 1}"])
                for mt in mts:
                    bank = pf[mt // 2]
                    cols = (mt % 2) * 256
                    last = None
                    for kc in range(8):
                        last = e.matmul(bank[:, cols:cols + 256],
                                        w1_sb[:, kc * D + mt * 128: kc * D + mt * 128 + 128],
                                        xn_slice(kc, st), start=(kc == 0), stop=(kc == 7))
                    at[f"t_h1_{st}_0_{mt}"] = sq.inc(last)
            return f

        def v_Aev(st):
            def f(e):
                if st >= 1:
                    e.wait_ge(sem_v3.h, ((st - 1) * 2 + 2) * 8)
                ins = None
                for b in range(4):
                    e.wait_ge(sq.h, at[f"t_h1_{st}_0_{2 * b + 1}"])
                    ins = e.tensor_copy(A_sb[:, b * 512:(b + 1) * 512], pf[b][:, 0:512])
                at[f"v_Aev{st}"] = sv.inc(ins)
            return f

        def t_Bpass(st, sign_set, tag, half=None):
            mts = range(8) if half is None else range(half * 4, half * 4 + 4)
            def f(e):
                if tag == "p":
                    if st == 0:
                        e.wait_ge(sv.h, at["v_Aev0"])
                    elif half == 0:
                        e.wait_ge(sem_s3.h, s3aev(st))
                    else:
                        e.wait_ge(sem_s3.h, s3atev(st, cpst - 1))
                elif half in (None, 0):
                    if st == 0:
                        e.wait_ge(sv.h, at["v_Bpev0"])
                    else:
                        e.wait_ge(sem_s3.h, s3bpev(st))
                for mt in mts:
                    bank = pf[mt // 2]
                    cols = (mt % 2) * 256
                    last = None
                    for ki, u in enumerate(sign_set):
                        last = e.matmul(bank[:, cols:cols + 256],
                                        w1_sb[:, u * D + mt * 128: u * D + mt * 128 + 128],
                                        xn_slice(PERM[u], st),
                                        start=(ki == 0), stop=(ki == len(sign_set) - 1))
                    at[f"t_B{tag}_{st}_{mt}"] = sq.inc(last)
            return f

        def v_Bpev(st):
            def f(e):
                e.wait_ge(sq.h, at[f"t_Bp_{st}_7"])
                if st >= 1:
                    e.wait_ge(sq.h, at[f"t_fT{st - 1}_1"])
                ins = None
                for b in range(4):
                    ins = e.tensor_copy(comb_sb[:, b * 512:(b + 1) * 512], pf[b][:, 0:512])
                at[f"v_Bpev{st}"] = sv.inc(ins)
            return f

        def v_Bmev(st):
            def f(e):
                e.wait_ge(sq.h, at[f"t_Bm_{st}_7"])
                if st >= 1:
                    e.wait_ge(sem_s3.h, s3bpev(st))
                ins = None
                for b in range(4):
                    ins = e.tensor_tensor(B_sb[:, b * 512:(b + 1) * 512],
                                          comb_sb[:, b * 512:(b + 1) * 512], pf[b][:, 0:512],
                                          OP.subtract)
                at[f"v_Bev{st}"] = sv.inc(ins)
            return f


        blk.tensor(t_A(0))
        blk.vector(v_Aev(0))
        blk.tensor(t_Bpass(0, SIGNP, "p"))
        blk.vector(v_Bpev(0))
        blk.tensor(t_Bpass(0, SIGNM, "m"))
        blk.vector(v_Bmev(0))

        def g_ccx(e):
            e.wait_ge(sv.h, at["v_xnsum"])
            ins = e.dma_start(out=bass.AP(ccx_in, 0, [[8, 128], [1, 8]]),
                              in_=xnsum_sb[:, :], allow_slow_non_contiguous=True)
            k1 = sem_dcc.inc(ins, 16)
            e.wait_ge(sem_dcc.h, k1)
            ins = e.collective_compute(
                "AllGather", OP.bypass,
                replica_groups=[[0, 1, 2, 3], [4, 5, 6, 7]],
                ins=[ccx_in[:, :]], outs=[ccx_out[:, :]])
            at["ccx"] = sem_ccx.inc(ins, 1)

        blk.gpsimd(g_ccx)

        def g_cc(e):
            e.wait_ge(sv.h, at["v_T0"])
            ins = e.dma_start(out=bass.AP(cc_in, 0, [[512, 128], [1, 512]]), in_=t_st[:, 0:512])
            k2 = sem_dcc.inc(ins, 16)
            e.wait_ge(sem_dcc.h, k2)
            ins = e.collective_compute(
                "AllGather", OP.bypass,
                replica_groups=[[0, 1, 2, 3], [4, 5, 6, 7]],
                ins=[cc_in[:, :]], outs=[cc_out[:, :]])
            at["cc"] = sem_cc.inc(ins, 1)

        blk.gpsimd(g_cc)

        def g_xg(e):
            e.wait_ge(sem_ccx.h, at["ccx"])
            ins = None
            for r in range(GRP):
                ins = e.dma_start(out=xg_sb[r][:, :],
                                  in_=bass.AP(ccx_out, r * D, [[8, 128], [1, 8]]),
                                  allow_slow_non_contiguous=True)
                sem_dxg.inc(ins, 16)
            at["d_xg"] = sem_dxg.n

        blk.gpsimd(g_xg)

        def v_rx(e):
            e.wait_ge(sem_dxg.h, at["d_xg"])
            ins = e.tensor_copy(xnt_sb[:, :], xg_sb[0][:, :])
            for r in range(1, GRP):
                k = sv.inc(ins)
                e.wait_ge(sv.h, k)
                ins = e.tensor_tensor(xnt_sb[:, :], xnt_sb[:, :], xg_sb[r][:, :], OP.add)
            k = sv.inc(ins)
            e.wait_ge(sv.h, k)
            ins = e.tensor_copy(xnt_bf[:, :], xnt_sb[:, :])
            at["v_xnt"] = sv.inc(ins)

        v_r_bodies = []
        for r in range(GRP):
            def g_r(e, r=r):
                if r == 0:
                    e.wait_ge(sem_cc.h, at["cc"])
                else:
                    e.wait_ge(sem_vr.h, r)
                ins = e.dma_start(out=gath_sb[:, :], in_=bass.AP(cc_out, r * CCN, [[512, 128], [1, 512]]))
                at[f"d_rnd{r}"] = sem_dcc.inc(ins, 16)

            def v_r(e, r=r):
                e.wait_ge(sem_dcc.h, at[f"d_rnd{r}"])
                if r == 0:
                    e.wait_ge(sem_dbc.h, dw_at["bcast"])
                if r == 0:
                    ins = e.tensor_scalar(sin_sb[:, :], gath_sb[:, :], wr_ap(0), None, OP.mult)
                else:
                    ins = e.scalar_tensor_tensor(sin_sb[:, :], gath_sb[:, :], wr_ap(r),
                                                 sin_sb[:, :], OP.mult, OP.add)
                ins.then_inc(sem_vr.h, 1)
                sem_vr.n += 1

            blk.gpsimd(g_r)
            v_r_bodies.append(v_r)

        def t_gf1(e):
            e.wait_ge(sv.h, at["v_xnt"])
            e.wait_ge(sem_dwb.h, sem_dwb.n)
            e.wait_ge(sem_s3.h, s3g3(0, 0, 2))
            last = None
            for mt in range(4):
                for kc in range(8):
                    last = e.matmul(pf[4][:, mt:mt + 1],
                                    g1_sb[:, kc * 512 + mt * 128: kc * 512 + mt * 128 + 128],
                                    xnt_bf[:, kc:kc + 1],
                                    start=(kc == 0), stop=(kc == 7))
            at["t_gf1"] = sq.inc(last)

        def t_gf2(e):
            e.wait_ge(sem_s3.h, s3pos["gf1"] + 1)
            e.wait_ge(sem_s3.h, s3th4(0, 0))
            last = None
            for mt in range(2):
                for kc in range(4):
                    last = e.matmul(pf[5][:, mt:mt + 1],
                                    g2_sb[:, kc * 256 + mt * 128: kc * 256 + mt * 128 + 128],
                                    gg1_sb[:, kc:kc + 1],
                                    start=(kc == 0), stop=(kc == 3))
            at["t_gf2"] = sq.inc(last)

        def t_gf3(e):
            e.wait_ge(sem_s3.h, s3pos["gf2"] + 1)
            last = None
            for kc in range(2):
                last = e.matmul(pf[4][0:1, 16:17], g3_sb[:, kc:kc + 1], gg2_sb[:, kc:kc + 1],
                                start=(kc == 0), stop=(kc == 1))
            at["t_gf3"] = sq.inc(last)

        def v_gf(e):
            e.wait_ge(sem_s3.h, s3pos["gf3"] + 1)
            e.wait_ge(sv.h, at["v_scl"])
            ins = e.tensor_scalar(sclr[0:1, 9:10], sclr[0:1, 8:9], sclr[0:1, 7:8], 0.1, OP.mult, OP.add)
            at["v_A3"] = sv.inc(ins)


        def t_h2(st, cyc):
            def f(e):
                waited = 0
                if cyc == 0 and st >= 1:
                    e.wait_ge(sv.h, at[f"v_retr{st * cpst + cpst - 1}"])
                for mt in range(4):
                    bank = pf[4 + mt // 2]
                    cols = (mt % 2) * 256
                    last = None
                    for kc in range(8):
                        tgt = s3g1(st, cyc, kc + 1 if CHASE else 8)
                        if tgt > waited:
                            waited = tgt
                            e.wait_ge(sem_s3.h, waited)
                        last = e.matmul(bank[:, cols:cols + 256],
                                        w2_sb[:, kc * 512 + mt * 128: kc * 512 + mt * 128 + 128],
                                        h1_bf[:, kc * FST:(kc + 1) * FST],
                                        start=(kc == 0), stop=(kc == 7))
                    at[f"t_h2_{st}_{cyc}_{mt}"] = sq.inc(last)
            return f

        def t_h3z4(st, cyc):
            def f(e):
                waited = 0
                for mt in range(2):
                    last = None
                    for kc in range(4):
                        tgt = max(s3g2(st, cyc, kc + 1), s3g2(st, cyc, 2)) if CHASE else s3g2(st, cyc, 4)
                        if tgt > waited:
                            waited = tgt
                            e.wait_ge(sem_s3.h, waited)
                        last = e.matmul(pf[4][:, mt * 256: mt * 256 + 256],
                                        w3_sb[:, kc * 256 + mt * 128: kc * 256 + mt * 128 + 128],
                                        h2_bf[:, kc * FST:(kc + 1) * FST],
                                        start=(kc == 0), stop=(kc == 3))
                    at[f"t_h3_{st}_{cyc}_{mt}"] = sq.inc(last)
                last = None
                for kc in range(2):
                    tgt = s3g3(st, cyc, kc + 1) if CHASE else s3g3(st, cyc, 2)
                    if tgt > waited:
                        waited = tgt
                        e.wait_ge(sem_s3.h, waited)
                    last = e.matmul(pf[5][0:1, 0:FST], w4_sb[:, kc:kc + 1],
                                    h3_bf[:, kc * FST:(kc + 1) * FST],
                                    start=(kc == 0), stop=(kc == 1))
                at[f"t_z4_{st}_{cyc}"] = sq.inc(last)
            return f

        def v_crow_f(st, cyc):
            def f(e):
                e.wait_ge(sem_s3.h, s3th4(st, cyc))
                if cyc == 1:
                    ins = e.tensor_scalar(c2_bf[0:1, :], th4_row[0:1, :],
                                          sclr[0:1, 6:7], sclr[0:1, 9:10], OP.mult, OP.add)
                    k = sv.inc(ins)
                    e.wait_ge(sv.h, k)
                    ins = e.tensor_tensor(abrow[0:1, 0:FST], brow_bf[0:1, :],
                                          c2_bf[0:1, :], OP.mult)
                    k = sv.inc(ins)
                    e.wait_ge(sv.h, k)
                    e.tensor_scalar(arow_bf[0:1, :], abrow[0:1, 0:FST], -1.0, 1.0,
                                    OP.mult, OP.add)
                    ins = e.tensor_tensor(brow_bf[0:1, :], brow_bf[0:1, :],
                                          c2_bf[0:1, :], OP.add)
                else:
                    ins = e.tensor_scalar(crows[0:1, 0:FST], th4_row[0:1, :],
                                          sclr[0:1, 6:7], sclr[0:1, 9:10], OP.mult, OP.add)
                    k = sv.inc(ins)
                    e.wait_ge(sv.h, k)
                    e.tensor_tensor(abrow[0:1, 0:FST], brow_bf[0:1, :],
                                    crows[0:1, 0:FST], OP.mult)
                    ins = e.tensor_tensor(abrow[0:1, FST:2 * FST], arow_bf[0:1, :],
                                          crows[0:1, 0:FST], OP.mult)
                    k = sv.inc(ins)
                    e.wait_ge(sv.h, k)
                    e.tensor_tensor(crows[0:1, 0:FST], arow_bf[0:1, :],
                                    abrow[0:1, 0:FST], OP.subtract)
                    ins = e.tensor_tensor(crows[0:1, FST:2 * FST], brow_bf[0:1, :],
                                          abrow[0:1, FST:2 * FST], OP.add)
                    k = sv.inc(ins)
                    e.wait_ge(sv.h, k)
                    ins = e.tensor_scalar(crows[0:1, 2 * FST:3 * FST],
                                          crows[0:1, FST:2 * FST], -1.0, None, OP.mult)
                at[f"v_crow{st}_{cyc}"] = sv.inc(ins)
            return f

        def t_cb_f(st, cyc):
            def f(e):
                if cyc == 0:
                    e.wait_ge(sem_s3.h, s3pos[f"crow_{st}_0"] + 1)
                    ins = e.matmul(pf[5][:, 256:256 + FST], onesr_bf[0:1, 0:128],
                                   brow_bf[0:1, :], start=True, stop=True)
                else:
                    e.wait_ge(sv.h, at[f"v_crow{st}_{cyc}"])
                    e.matmul(pf[5][:, 0:FST], onesr_bf[0:1, 0:128],
                             arow_bf[0:1, :], start=True, stop=True)
                    ins = e.matmul(pf[5][:, 256:256 + FST], onesr_bf[0:1, 0:128],
                                   brow_bf[0:1, :], start=True, stop=True)
                at[f"t_cb{st}_{cyc}"] = sq.inc(ins)
            return f

        def v_comb(st, cyc):
            def f(e):
                e.wait_ge(sq.h, at[f"t_cb{st}_{cyc - 1}"])
                e.wait_ge(sv.h, at[f"v_Bev{st}"])
                e.wait_ge(sq.h, at[f"t_h2_{st}_{cyc - 1}_3"])
                for ck in range(8):
                    c0, c1 = ck * FST, (ck + 1) * FST
                    if cyc == 1:
                        ins = e.tensor_tensor(comb_sb[:, c0:c1], B_sb[:, c0:c1],
                                              pf[5][:, 256:256 + FST], OP.mult)
                        k = sv.inc(ins)
                        e.wait_ge(sv.h, k)
                        ins = e.tensor_tensor(comb_sb[:, c0:c1], comb_sb[:, c0:c1],
                                              A_sb[:, c0:c1], OP.add)
                    else:
                        e.tensor_tensor(h1_bf[:, c0:c1], A_sb[:, c0:c1],
                                        pf[5][:, 0:FST], OP.mult)
                        ins = e.tensor_tensor(comb_sb[:, c0:c1], B_sb[:, c0:c1],
                                              pf[5][:, 256:256 + FST], OP.mult)
                        k = sv.inc(ins)
                        e.wait_ge(sv.h, k)
                        ins = e.tensor_tensor(comb_sb[:, c0:c1], comb_sb[:, c0:c1],
                                              h1_bf[:, c0:c1], OP.add)
                    ins.then_inc(sem_v3.h, 1)
                    sem_v3.n += 1
            return f

        def t_ffnT(st, tt):
            def f(e):
                pb = pbfa if tt == 0 else pbfb
                if st >= 1:
                    e.wait_ge(sem_fin.h, cpst * (st - 1) + tt + 1)
                last = None
                for ck in range(8):
                    last = e.transpose(pb[:, ck * 128:(ck + 1) * 128],
                                       xn_fm[:, ck * rpc + st * FST + tt * 128:
                                             ck * rpc + st * FST + tt * 128 + 128],
                                       id_sb[:, :])
                at[f"t_fT{st}_{tt}"] = sq.inc(last)
            return f

        def g_abT(st):
            def f(e):
                e.wait_ge(sv.h, at[f"v_crow{st}_2"])
                ins = e.dma_start(out=ab_dram[:, :], in_=crows[0:1, 0:3 * FST])
                sem_dab.inc(ins, 16)
                e.wait_ge(sem_dab.h, 112 * st + 16)
                for tt in range(2):
                    for c in range(3):
                        src = bass.AP(ab_dram, c * FST + tt * 128, [[1, 128], [1, 1]])
                        ins = e.dma_start(out=abcol[:, tt * 3 + c:tt * 3 + c + 1], in_=src)
                        sem_dab.inc(ins, 16)
            return f

        def v_fin(st):
            sign = [1, 1, -1, -1, -1, -1, 1, 1]
            def f(e):
                for tt in range(2):
                    gt = st * cpst + tt
                    pb = pbfa if tt == 0 else pbfb
                    e.wait_ge(sq.h, at[f"t_fT{st}_{tt}"])
                    if st >= 1:
                        e.wait_ge(sem_s3.h, s3atev(st, cpst - 1))
                    e.wait_ge(sem_dab.h, 112 * (st + 1))
                    e.wait_ge(sem_dx[gt % 4].h, 16 * (ntiles // 4 + gt // 4 + 1))
                    if gt >= 2:
                        e.wait_ge(sem_dy[gt % 2].h, 16 * (gt // 2))
                    ins = e.scalar_tensor_tensor(y_sb[tt][:, :], pb[:, 0:1024],
                                                 abcol[:, tt * 3:tt * 3 + 1],
                                                 attn2[st % 2][:, tt * D: (tt + 1) * D],
                                                 OP.mult, OP.add)
                    k = sv.inc(ins)
                    e.wait_ge(sv.h, k)
                    for (g0, g1, pos) in ((0, 2, True), (2, 4, False),
                                          (4, 6, False), (6, 8, True)):
                        col = tt * 3 + (1 if pos else 2)
                        ins = e.scalar_tensor_tensor(
                            y_sb[tt][:, g0 * 128:g1 * 128],
                            pb[:, PERM[g0] * 128:(PERM[g0] + g1 - g0) * 128],
                            abcol[:, col:col + 1],
                            y_sb[tt][:, g0 * 128:g1 * 128], OP.mult, OP.add)
                    k = sv.inc(ins)
                    e.wait_ge(sv.h, k)
                    ins = e.tensor_tensor(y_sb[tt][:, :], y_sb[tt][:, :], xin[gt % 4][:, :], OP.add)
                    ins.then_inc(sem_fin.h, 1)
                    sem_fin.n += 1
            return f

        blk.tensor(t_h2(0, 0))
        blk.vector(v_rx)
        blk.tensor(t_h3z4(0, 0))
        blk.tensor(t_gf1)
        blk.tensor(t_gf2)
        blk.tensor(t_gf3)
        blk.vector(v_gf)
        blk.tensor(t_cb_f(0, 0))
        blk.vector(v_comb(0, 1))
        blk.tensor(t_h2(0, 1))
        if nfst > 1:
            blk.tensor(t_A(1, half=0))
        blk.tensor(t_h3z4(0, 1))
        if nfst > 1:
            blk.tensor(t_A(1, half=1))
        for vb in v_r_bodies:
            blk.vector(vb)
        blk.vector(v_crow_f(0, 1))
        blk.tensor(t_cb_f(0, 1))
        blk.vector(v_comb(0, 2))
        blk.tensor(t_h2(0, 2))
        blk.tensor(t_ffnT(0, 0))
        blk.tensor(t_h3z4(0, 2))
        blk.tensor(t_ffnT(0, 1))
        blk.vector(v_crow_f(0, 2))
        blk.gpsimd(g_abT(0))
        emit_attn(0)
        if nfst > 1:
            blk.tensor(t_Bpass(1, SIGNP, "p", half=0))
            emit_attn(1)
        blk.vector(v_fin(0))

        for st in range(1, nfst):
            if st >= 1:
                blk.tensor(t_Bpass(st, SIGNP, "p", half=1))
            blk.tensor(t_h2(st, 0))
            if st >= 1:
                blk.tensor(t_Bpass(st, SIGNM, "m", half=0))
            blk.tensor(t_h3z4(st, 0))
            if st >= 1:
                blk.tensor(t_Bpass(st, SIGNM, "m", half=1))
                blk.vector(v_Bmev(st))
            blk.tensor(t_cb_f(st, 0))
            if st + 1 < nfst:
                blk.tensor(t_A(st + 1, half=0))
            blk.vector(v_comb(st, 1))
            blk.tensor(t_h2(st, 1))
            blk.tensor(t_h3z4(st, 1))
            blk.vector(v_crow_f(st, 1))
            blk.tensor(t_cb_f(st, 1))
            if st + 1 < nfst:
                blk.tensor(t_A(st + 1, half=1))
            blk.vector(v_comb(st, 2))
            blk.tensor(t_h2(st, 2))
            blk.tensor(t_ffnT(st, 0))
            blk.tensor(t_h3z4(st, 2))
            blk.tensor(t_ffnT(st, 1))
            if st + 1 < nfst:
                blk.tensor(t_Bpass(st + 1, SIGNP, "p", half=0))
                emit_attn(st + 1)
            blk.vector(v_crow_f(st, 2))
            blk.gpsimd(g_abT(st))
            blk.vector(v_fin(st))



        def s3_gelu1_c0(e, st):
            if st >= 1:
                e.wait_ge(sem_v3.h, ((st - 1) * 2 + 2) * 8)
                e.wait_ge(sq.h, at[f"t_h2_{st - 1}_2_3"])
            else:
                e.wait_ge(sv.h, at["v_Aev0"])
            for mt in range(8):
                ins = e.activation(h1_bf[:, mt * FST:(mt + 1) * FST],
                                   A_sb[:, mt * FST:(mt + 1) * FST],
                                   AF.Gelu, bias=b1_sb[:, mt:mt + 1])
                ins.then_inc(sem_s3.h, 1)
                sem_s3.n += 1

        def scalar_mega3(e):
            preload(e, SET_GELU)
            e.wait_ge(sem_dwb.h, sem_dwb.n)
            for name, cnt in s3sched:
                p = name.split("_")
                base = sem_s3.n
                if name == "gf1":
                    e.wait_ge(sq.h, at["t_gf1"])
                    last = None
                    for mt in range(4):
                        last = e.activation(gg1_sb[:, mt:mt + 1], pf[4][:, mt:mt + 1], AF.Gelu,
                                            bias=gb1_sb[:, mt:mt + 1], scale=1.0 / float(seq_t))
                    last.then_inc(sem_s3.h, 1)
                    sem_s3.n += 1
                elif name == "gf2":
                    e.wait_ge(sq.h, at["t_gf2"])
                    last = None
                    for mt in range(2):
                        last = e.activation(gg2_sb[:, mt:mt + 1], pf[5][:, mt:mt + 1], AF.Gelu,
                                            bias=gb2_sb[:, mt:mt + 1])
                    last.then_inc(sem_s3.h, 1)
                    sem_s3.n += 1
                elif name == "gf3":
                    e.wait_ge(sq.h, at["t_gf3"])
                    ins = e.activation(sclr[0:1, 8:9], pf[4][0:1, 16:17], AF.Tanh,
                                       bias=sclr[0:1, 5:6], scale=0.5)
                    ins.then_inc(sem_s3.h, 1)
                    sem_s3.n += 1
                elif p[0] == "aev":
                    stp = int(p[1])
                    e.wait_ge(sq.h, at[f"t_h1_{stp}_0_7"])
                    for b in range(4):
                        ins = e.activation(A_sb[:, b * 512:(b + 1) * 512],
                                           pf[b][:, 0:512], AF.Copy)
                        ins.then_inc(sem_s3.h, 1)
                        sem_s3.n += 1
                elif p[0] == "atev":
                    stp, tt = int(p[1]), int(p[2])
                    jj = stp * cpst + tt
                    e.wait_ge(sq.h, at[f"t_o{jj}"])
                    ab = attn2[stp % 2]
                    for half, bank in ((0, pf[2]), (1, pf[3])):
                        ins = e.activation(ab[:, tt * D + half * 512: tt * D + half * 512 + 512],
                                           bank[:, 0:512], AF.Copy)
                        ins.then_inc(sem_s3.h, 1)
                        sem_s3.n += 1
                elif p[0] == "bpev":
                    stp = int(p[1])
                    e.wait_ge(sq.h, at[f"t_Bp_{stp}_7"])
                    for b in range(4):
                        ins = e.activation(comb_sb[:, b * 512:(b + 1) * 512],
                                           pf[b][:, 0:512], AF.Copy)
                        ins.then_inc(sem_s3.h, 1)
                        sem_s3.n += 1
                elif p[0] == "g1":
                    st, cyc = int(p[1]), int(p[2])
                    if cyc == 0:
                        s3_gelu1_c0(e, st)
                    else:
                        e.wait_ge(sq.h, at[f"t_h2_{st}_{cyc - 1}_3"])
                        v3base = (st * 2 + cyc - 1) * 8
                        for mt in range(8):
                            e.wait_ge(sem_v3.h, v3base + mt + 1)
                            ins = e.activation(h1_bf[:, mt * FST:(mt + 1) * FST],
                                               comb_sb[:, mt * FST:(mt + 1) * FST],
                                               AF.Gelu, bias=b1_sb[:, mt:mt + 1])
                            ins.then_inc(sem_s3.h, 1)
                            sem_s3.n += 1
                elif p[0] == "g2":
                    st, cyc = int(p[1]), int(p[2])
                    for mt in range(4):
                        e.wait_ge(sq.h, at[f"t_h2_{st}_{cyc}_{(mt // 2) * 2 + 1}"])
                        ins = e.activation(h2_bf[:, mt * FST:(mt + 1) * FST],
                                           pf[4 + mt // 2][:, (mt % 2) * 256:(mt % 2) * 256 + 256],
                                           AF.Gelu, bias=b2_sb[:, mt:mt + 1])
                        ins.then_inc(sem_s3.h, 1)
                        sem_s3.n += 1
                elif p[0] == "g3":
                    st, cyc = int(p[1]), int(p[2])
                    for mt in range(2):
                        e.wait_ge(sq.h, at[f"t_h3_{st}_{cyc}_1"])
                        ins = e.activation(h3_bf[:, mt * FST:(mt + 1) * FST],
                                           pf[4][:, mt * 256: mt * 256 + 256],
                                           AF.Gelu, bias=b3_sb[:, mt:mt + 1])
                        ins.then_inc(sem_s3.h, 1)
                        sem_s3.n += 1
                elif p[0] == "crow":
                    st = int(p[1])
                    if st == 0:
                        e.wait_ge(sv.h, at["v_A3"])
                    e.wait_ge(sem_s3.h, s3th4(st, 0))
                    ins = e.activation(brow_bf[0:1, 0:FST], th4_row[0:1, :], AF.Identity,
                                       bias=sclr[0:1, 9:10], scale=sclr[0:1, 6:7])
                    ins.then_inc(sem_s3.h, 1)
                    sem_s3.n += 1
                elif p[0] == "th4":
                    st, cyc = int(p[1]), int(p[2])
                    if cyc == 2:
                        e.wait_ge(sv.h, at[f"v_crow{st}_1"])
                    elif cyc == 0 and st >= 1:
                        e.wait_ge(sv.h, at[f"v_crow{st - 1}_2"])
                    e.wait_ge(sq.h, at[f"t_z4_{st}_{cyc}"])
                    ins = e.activation(th4_row[0:1, :], pf[5][0:1, 0:FST], AF.Tanh,
                                       bias=sclr[0:1, 4:5], scale=0.5)
                    ins.then_inc(sem_s3.h, 1)
                    sem_s3.n += 1
                else:
                    raise AssertionError(name)
                assert sem_s3.n == base + cnt == s3pos[name] + cnt, name

        blk.scalar(scalar_mega3)

    return nc


def _consts(rpc):
    nch = rpc // L
    p = np.arange(128, dtype=np.float32)
    f = np.arange(128, dtype=np.float32)
    E1 = np.maximum(p[:, None] - f[None, :] - 1.0, 0.0).astype(np.float32)
    M1 = (p[:, None] > f[None, :]).astype(np.float32)
    EQ = np.ascontiguousarray(np.broadcast_to((127.0 - f)[None, :], (128, 128))).astype(np.float32)
    CK = np.stack([p, np.full(128, 128.0, np.float32)], axis=1).astype(np.float32)
    return dict(
        cpack=np.concatenate([E1, M1, EQ, CK], axis=1).astype(np.float32),
        ident=np.eye(128, dtype=np.float32),
        onesr=np.ones((1, 128), dtype=np.float32),
    )


def _core_consts(rpc, r):
    nch = rpc // L
    rr = np.arange(GRP, dtype=np.float32)
    EJ = (np.float32(L) * (nch - 1 - np.arange(nch, dtype=np.float32))).astype(np.float32)
    E4 = np.maximum(np.float32(rpc) * (rr - r - 1.0), 0.0).astype(np.float32)
    M4 = (rr > r).astype(np.float32)
    rpack = np.zeros(nch + 2 * GRP + 128, np.float32)
    rpack[0:nch] = EJ
    rpack[nch:nch + GRP] = E4
    rpack[nch + GRP:nch + 2 * GRP] = M4
    return dict(rpack=rpack[None, :])


def make_in_maps(inputs, rpc=RPC):
    x = np.ascontiguousarray(inputs["L0_attn_x"], dtype=np.float32)
    f32 = lambda k: np.ascontiguousarray(inputs[k], np.float32)
    s11 = lambda k: np.asarray(inputs[k], np.float32).reshape(1, 1)
    shared = {
        "basis": f32("L0_attn_basis"),
        "w1": f32("L0_ffn_cn_w1"), "b1": f32("L0_ffn_cn_b1"),
        "w2": f32("L0_ffn_cn_w2"), "b2": f32("L0_ffn_cn_b2"),
        "w3": f32("L0_ffn_cn_w3"), "b3": f32("L0_ffn_cn_b3"),
        "w4": f32("L0_ffn_cn_w4"),
        "gw1": f32("L0_ffn_gc_w1"), "gb1": f32("L0_ffn_gc_b1"),
        "gw2": f32("L0_ffn_gc_w2"), "gb2": f32("L0_ffn_gc_b2"),
        "gw3": f32("L0_ffn_gc_w3"),
        "spack": np.concatenate([
            s11("L0_ffn_cn_b4"), s11("L0_ffn_gc_b3"), s11("L0_ffn_adaptive_range"),
            s11("L0_attn_decay_logit"), s11("L0_attn_out_scale")], axis=1),
        "cfs": np.ascontiguousarray(np.stack([
            f32("L0_attn_q_coeffs"), f32("L0_attn_k_coeffs"),
            f32("L0_attn_v_coeffs"), f32("L0_attn_o_coeffs")], axis=0)),
    }
    shared.update(_consts(rpc))
    in_maps = []
    for core in range(NCORES):
        b, r = core // GRP, core % GRP
        m = dict(shared)
        m["x"] = np.ascontiguousarray(x[b, r * rpc:(r + 1) * rpc, :])
        m.update(_core_consts(rpc, r))
        in_maps.append(m)
    return in_maps


def kernel(**inputs):
    in_maps = make_in_maps(inputs)
    nc = build_nc()
    res = run_bass_kernel_spmd(nc, in_maps, core_ids=list(range(NCORES)))
    y = np.empty((B, T, D), dtype=np.float32)
    for core in range(NCORES):
        b, r = core // GRP, core % GRP
        y[b, r * RPC:(r + 1) * RPC, :] = res.results[core]["y"]
    return y


# version 35
# speedup vs baseline: 1.0078x; 1.0078x over previous
import os
import numpy as np
import concourse.bass as bass
import concourse.mybir as mybir
from concourse.bass_utils import run_bass_kernel_spmd
from contextlib import ExitStack

F32 = mybir.dt.float32
BF16 = mybir.dt.bfloat16
AF = mybir.ActivationFunctionType
OP = mybir.AluOpType

B, T, D = 2, 8192, 1024
NCORES = 8
GRP = 4
RPC = (B * T) // NCORES
L = 128
PST = 512
FST = 256
C = 256
EARLY_A = int(os.environ.get("EARLY_A", "1"))
CHASE = int(os.environ.get("CHASE", "1"))


class Sem:
    def __init__(self, ctx, nc, name):
        self.h = ctx.enter_context(nc.semaphore(name))
        self.n = 0

    def inc(self, instr, k=1):
        instr.then_inc(self.h, k)
        self.n += k
        return self.n


def build_nc(rpc=RPC, debug=False):
    nch = rpc // L
    ntiles = rpc // 128
    npst = rpc // PST
    nfst = rpc // FST
    cpst = FST // L
    seq_t = GRP * rpc

    nc = bass.Bass("TRN2", target_bir_lowering=False, debug=debug)

    dx = nc.dram_tensor("x", [rpc, D], F32, kind="ExternalInput")
    dy = nc.dram_tensor("y", [rpc, D], F32, kind="ExternalOutput")
    din = {}
    for nm, shape in (("basis", [D, C]), ("w1", [D, D]), ("b1", [D]), ("w2", [D, 512]),
                      ("b2", [512]), ("w3", [512, 256]), ("b3", [256]), ("w4", [256, 1]),
                      ("spack", [1, 5]), ("gw1", [D, 512]), ("gb1", [512]), ("gw2", [512, 256]),
                      ("gb2", [256]), ("gw3", [256, 1]), ("ident", [128, 128]),
                      ("cpack", [128, 386]), ("rpack", [1, nch + 2 * GRP + 128]),
                      ("cfs", [4, 256, 256]), ("onesr", [1, 128])):
        din[nm] = nc.dram_tensor(nm, shape, F32, kind="ExternalInput")

    CCN = C * C
    cc_in = nc.dram_tensor("cc_in", [1, CCN], BF16)
    cc_out = nc.dram_tensor("cc_out", [GRP, CCN], BF16)
    ccx_in = nc.dram_tensor("ccx_in", [1, D], BF16)
    ccx_out = nc.dram_tensor("ccx_out", [GRP, D], BF16)
    bstage = nc.dram_tensor("bstage", [1, 32], F32)
    ab_dram = nc.dram_tensor("ab_dram", [1, 768], F32)

    ctx = ExitStack()
    with ctx:
        sb = lambda name, shape, dt: ctx.enter_context(nc.sbuf_tensor(name, shape, dt))
        ps = lambda name, shape, dt: ctx.enter_context(nc.psum_tensor(name, shape, dt))

        class View:
            def __init__(self, t, base, width):
                self.t, self.base, self.w = t, base, width

            def __getitem__(self, idx):
                p, c = idx
                c0 = 0 if c.start is None else c.start
                c1 = self.w if c.stop is None else c.stop
                return self.t[p, self.base + c0: self.base + c1]


        xn_fm = sb("xn_fm", [128, 8 * rpc], BF16)
        q_fm = sb("q_fm", [128, 2 * rpc], BF16)
        k_fm = sb("k_fm", [128, 2 * rpc], BF16)
        kd_tm = sb("kd_tm", [128, nch * C], BF16)
        v_tm = sb("v_tm", [128, nch * C], BF16)
        t_st = sb("t_st", [128, nch * 512], BF16)
        w1_sb = sb("w1_sb", [128, 8 * D], BF16)
        w2_sb = sb("w2_sb", [128, 8 * 512], BF16)
        w3_sb = sb("w3_sb", [128, 4 * 256], BF16)
        w4_sb = sb("w4_sb", [128, 2], BF16)
        g1_sb = sb("g1_sb", [128, 8 * 512], BF16)
        g2_sb = sb("g2_sb", [128, 4 * 256], BF16)
        g3_sb = sb("g3_sb", [128, 2], BF16)
        owt_sb = sb("owt_sb", [128, 2 * D], BF16)
        b1_sb = sb("b1_sb", [128, 8], F32)
        b2_sb = sb("b2_sb", [128, 4], F32)
        b3_sb = sb("b3_sb", [128, 2], F32)
        gb1_sb = sb("gb1_sb", [128, 4], F32)
        gb2_sb = sb("gb2_sb", [128, 2], F32)
        sc_in = sb("sc_in", [1, 8], F32)
        id_sb = sb("id_sb", [128, 128], BF16)
        rpack_sb = sb("rpack_sb", [1, nch + 2 * GRP + 128], F32)
        EJ_sb = View(rpack_sb, 0, nch)
        E4_sb = View(rpack_sb, nch, GRP)
        M4_sb = View(rpack_sb, nch + GRP, GRP)
        onesr_bf = sb("onesr_bf", [1, 128], BF16)
        onesc_bf = sb("onesc_bf", [128, 1], BF16)
        DT_sb = sb("DT_sb", [128, 128], BF16)
        DQ_sb = sb("DQ_sb", [128, 128], BF16)
        dk_col = sb("dk_col", [128, 1], F32)
        dL_col = sb("dL_col", [128, 1], F32)
        bcast = sb("bcast", [128, 32], F32)
        row_sb = sb("row_sb", [1, 32], F32)
        sclr = sb("sclr", [1, 16], F32)
        c1_col = sb("c1_col", [128, 1], F32)
        ceps_col = sb("ceps_col", [128, 1], F32)
        xnsum_sb = sb("xnsum_sb", [128, 8], BF16)
        xnt_sb = sb("xnt_sb", [128, 8], F32)
        xnt_bf = sb("xnt_bf", [128, 8], BF16)
        sin_sb = sb("sin_sb", [128, 512], F32)
        gath_sb = sb("gath_sb", [128, 512], BF16)
        xg_sb = [sb(f"xg_sb{r}", [128, 8], BF16) for r in range(GRP)]
        gg1_sb = sb("gg1_sb", [128, 4], BF16)
        gg2_sb = sb("gg2_sb", [128, 2], BF16)
        dsrow_sb = sb("dsrow_sb", [1, nch], F32)
        wrrow_sb = sb("wrrow_sb", [1, GRP], F32)
        xin = [sb(f"xin{i}", [128, D], F32) for i in range(4)]
        stats = [sb(f"stats{i}", [128, 12], F32) for i in range(4)]
        mv = [sb(f"mv{i}", [128, 2], F32) for i in range(4)]
        rcol = [sb(f"rcol{i}", [128, 1], F32) for i in range(4)]
        mrcol = [sb(f"mrcol{i}", [128, 1], F32) for i in range(2)]
        xn_tm = [sb(f"xn_tm{i}", [128, D], BF16) for i in range(2)]
        sj_bf = [sb(f"sj_bf{i}", [128, 512], BF16) for i in range(2)]
        qd_tmp = [sb(f"qd_tmp{i}", [128, 256], BF16) for i in range(2)]
        sco_sb = [sb(f"sco_sb{i}", [128, 128], BF16) for i in range(2)]
        retr_sb = [sb(f"retr_sb{i}", [128, C], BF16) for i in range(2)]
        attn_sb = sb("attn_sb", [128, 2 * D], BF16)
        attn_sb_b = sb("attn_sb_b", [128, 2 * D], BF16)
        attn2 = [attn_sb, attn_sb_b]
        A_sb = sb("A_sb", [128, 8 * FST], BF16)
        B_sb = sb("B_sb", [128, 8 * FST], BF16)
        comb_sb = sb("comb_sb", [128, 8 * FST], BF16)
        h1_bf = sb("h1_bf", [128, 8 * FST], BF16)
        h2_bf = sb("h2_bf", [128, 4 * FST], BF16)
        h3_bf = sb("h3_bf", [128, 2 * FST], BF16)
        th4_row = sb("th4_row", [1, FST], F32)
        crows = sb("crows", [1, 3 * FST], F32)
        abrow = sb("abrow", [1, 2 * FST], F32)
        arow_bf = sb("arow_bf", [1, FST], BF16)
        brow_bf = sb("brow_bf", [1, FST], BF16)
        c2_bf = sb("c2_bf", [1, FST], BF16)
        abcol = sb("abcol", [128, 6], F32)
        y_sb = [sb(f"y_sb{i}", [128, D], F32) for i in range(2)]
        E1_sb = View(y_sb[0], 0, 128)
        M1_sb = View(y_sb[0], 128, 128)
        EQ_sb = View(y_sb[0], 256, 128)
        CK_sb = View(y_sb[0], 384, 2)
        cpack_sb = View(y_sb[0], 0, 386)
        basis_bf = View(A_sb, 0, 8 * C)
        qw_sb = View(B_sb, 0, 8 * C)
        kw_sb = View(h1_bf, 0, 8 * C)
        vw_sb = View(attn_sb_b, 0, 8 * C)
        basT = View(comb_sb, 0, 2 * D)
        qc_bf = View(h1_bf, 0, 2 * C)
        kc_bf = View(h1_bf, 512, 2 * C)
        vc_bf = View(h1_bf, 1024, 2 * C)
        oc_bf = View(h1_bf, 1536, 2 * C)
        qcT = View(attn_sb, 0, 2 * C)
        kcT = View(attn_sb, 512, 2 * C)
        vcT = View(attn_sb, 1024, 2 * C)
        ocT = View(attn_sb, 1536, 2 * C)

        pbfa = ps("pbfa", [128, 1024], BF16)
        pbfb = ps("pbfb", [128, 1024], BF16)
        pf = [ps(f"pf{i}", [128, 512], F32) for i in range(6)]

        sem_dx = [Sem(ctx, nc, f"dx{i}") for i in range(4)]
        sem_dwa = Sem(ctx, nc, "dwa")
        sem_dw0 = Sem(ctx, nc, "dw0")
        sem_dw00 = Sem(ctx, nc, "dw00")
        sem_dwb = Sem(ctx, nc, "dwb")
        sem_dbc = Sem(ctx, nc, "dbc")
        sem_dcc = Sem(ctx, nc, "dcc")
        sem_dy = [Sem(ctx, nc, "dy0"), Sem(ctx, nc, "dy1")]
        sem_dab = Sem(ctx, nc, "dab")
        sem_vr = Sem(ctx, nc, "vr")
        sem_dxg = Sem(ctx, nc, "dxg")
        sem_ttr = Sem(ctx, nc, "ttr")
        sem_tpj = Sem(ctx, nc, "tpj")
        sem_tv = Sem(ctx, nc, "tv")
        sem_tkt = Sem(ctx, nc, "tkt")
        sem_cc = Sem(ctx, nc, "scc")
        sem_ccx = Sem(ctx, nc, "sccx")
        sem_nrm = Sem(ctx, nc, "snrm")
        sem_fin = Sem(ctx, nc, "sfin")
        sv = Sem(ctx, nc, "sv")
        ss = Sem(ctx, nc, "ss")
        sq = Sem(ctx, nc, "sq")
        sem_vsu = Sem(ctx, nc, "svsu")
        sem_vbn = Sem(ctx, nc, "svbn")
        sem_vmr = Sem(ctx, nc, "svmr")
        sem_txs = Sem(ctx, nc, "stxs")
        sem_s3 = Sem(ctx, nc, "ss3")
        sem_v3 = Sem(ctx, nc, "sv3")

        from concourse.hw_specs import get_activation_tables
        _tabs = list(get_activation_tables(nc.m.arch))
        SET_LNEXP = _tabs.index("natural_log_exp_and_others")
        SET_GELU = _tabs.index("gelu_and_others")
        SET_RSQRT = _tabs.index("reciprocal_sqrt_and_small")

        def preload(e, set_idx):
            e.add_instruction(mybir.InstLoadActFuncSet(
                name=nc.get_next_instruction_name(), act_func_set_id=set_idx,
                ins=[], outs=[]))

        at = {}
        nsp_ap = lambda: bcast[:, 0:1]
        osc_ap = lambda: bcast[:, 1:2]
        dS_ap = lambda j: bcast[:, 2 + j:3 + j]
        wr_ap = lambda r: bcast[:, 2 + nch + r:3 + nch + r]
        blk = ctx.enter_context(nc.Block())

        def sync_all(e):
            for i in range(ntiles):
                if i >= 4:
                    e.wait_ge(sem_nrm.h, i - 3)
                e.dma_start(out=xin[i % 4][:, :], in_=dx[i * 128:(i + 1) * 128, :]).then_inc(sem_dx[i % 4].h, 16)
                sem_dx[i % 4].n += 16
            for gt in range(ntiles):
                if gt < 4:
                    e.wait_ge(sem_nrm.h, ntiles)
                else:
                    e.wait_ge(sem_fin.h, gt - 3)
                e.dma_start(out=xin[gt % 4][:, :], in_=dx[gt * 128:(gt + 1) * 128, :]).then_inc(sem_dx[gt % 4].h, 16)
                sem_dx[gt % 4].n += 16
                k = gt - 4
                if k >= 0:
                    e.wait_ge(sem_fin.h, k + 1)
                    ins = e.dma_start(out=dy[k * 128:(k + 1) * 128, :], in_=y_sb[k % 2][:, :])
                    sem_dy[k % 2].inc(ins, 16)
            for k in range(ntiles - 4, ntiles):
                e.wait_ge(sem_fin.h, k + 1)
                ins = e.dma_start(out=dy[k * 128:(k + 1) * 128, :], in_=y_sb[k % 2][:, :])
                sem_dy[k % 2].inc(ins, 16)
            e.wait_ge(sem_dy[0].h, sem_dy[0].n)
            e.wait_ge(sem_dy[1].h, sem_dy[1].n)

        blk.sync(sync_all)

        dw_at = {}

        def g_loads(e):
            items = [
                (id_sb[:, :], din["ident"][:, :], "id"),
                (onesr_bf[:, :], din["onesr"][:, :], "onesr"),
                (onesc_bf[:, :], din["onesr"].ap().rearrange("a (p b) -> (a p) b", b=1), "onesc"),
                (sc_in[0:1, 0:5], din["spack"][:, :], "spack"),
                (cpack_sb[:, :], din["cpack"][:, :], "cpack"),
                (rpack_sb[0:1, :], din["rpack"][:, :], "rpack"),
                (basis_bf[:, :], din["basis"].ap().rearrange("(c p) n -> p c n", p=128), "basis"),
                (h1_bf[:, 0:2048], din["cfs"].ap().rearrange("w (c p) n -> p w c n", p=128), "cfs"),
                (b1_sb[:, :], din["b1"].ap().rearrange("(c p) -> p c", p=128), "b1"),
                (b2_sb[:, :], din["b2"].ap().rearrange("(c p) -> p c", p=128), "b2"),
                (b3_sb[:, :], din["b3"].ap().rearrange("(c p) -> p c", p=128), "b3"),
                (gb1_sb[:, :], din["gb1"].ap().rearrange("(c p) -> p c", p=128), "gb1"),
                (gb2_sb[:, :], din["gb2"].ap().rearrange("(c p) -> p c", p=128), "gb2"),
                (w1_sb[:, :], din["w1"].ap().rearrange("(c p) n -> p c n", p=128), "w1"),
                (w2_sb[:, :], din["w2"].ap().rearrange("(c p) n -> p c n", p=128), "w2"),
                (w3_sb[:, :], din["w3"].ap().rearrange("(c p) n -> p c n", p=128), "w3"),
                (w4_sb[:, :], din["w4"].ap().rearrange("(c p) n -> p (c n)", p=128), "w4"),
                (g1_sb[:, :], din["gw1"].ap().rearrange("(c p) n -> p c n", p=128), "g1"),
                (g2_sb[:, :], din["gw2"].ap().rearrange("(c p) n -> p c n", p=128), "g2"),
                (g3_sb[:, :], din["gw3"].ap().rearrange("(c p) n -> p (c n)", p=128), "g3"),
            ]
            slow_ok = {"b1", "b2", "b3", "gb1", "gb2", "onesc", "w4", "g3"}
            grp_b = {"w1", "w2", "w3", "w4", "g1", "g2", "g3",
                     "b1", "b2", "b3", "gb1", "gb2"}
            grp_00 = {"id", "basis", "cfs"}
            grp_0 = {"onesr", "onesc", "spack", "cpack", "rpack"}
            items.sort(key=lambda it: 0 if it[2] in grp_00 else
                       (1 if it[2] in grp_0 else (3 if it[2] in grp_b else 2)))
            for out_ap, in_ap, key in items:
                ins = e.dma_start(out=out_ap, in_=in_ap,
                                  allow_slow_non_contiguous=(key in slow_ok))
                sem = (sem_dw00 if key in grp_00 else
                       (sem_dw0 if key in grp_0 else (sem_dwb if key in grp_b else sem_dwa)))
                dw_at[key] = sem.inc(ins, 16)

        blk.gpsimd(g_loads)

        def s_proj_ev(e, st):
            e.wait_ge(sem_tpj.h, st + 1)
            for nm, banks, dst in (("q", (pf[1], pf[2]), q_fm), ("k", (pf[3], pf[4]), k_fm)):
                for mt in range(2):
                    ins = e.activation(dst[:, mt * rpc + st * PST: mt * rpc + st * PST + PST],
                                       banks[mt][:, 0:512], AF.Copy)
                    at[f"s_{nm}ev{st}_{mt}"] = ss.inc(ins)
            for tt in range(4):
                gt = 4 * st + tt
                e.wait_ge(sem_tv.h, gt + 1)
                ins = e.activation(v_tm[:, gt * C: gt * C + C], pf[5][:, 0:C], AF.Copy)
                at[f"s_vev{gt}"] = ss.inc(ins)
                e.wait_ge(sem_tkt.h, gt + 1)
                ins = e.activation(kd_tm[:, gt * C: gt * C + C], pbfb[:, 0:C],
                                   AF.Identity, scale=dk_col[:, 0:1])
                at[f"s_kdev{gt}"] = ss.inc(ins)

        def scalar_mega1(e):
            preload(e, SET_LNEXP)
            e.wait_ge(sem_vsu.h, 2)
            ins = e.activation(sclr[0:1, 1:2], sclr[0:1, 0:1], AF.Exp)
            k = ss.inc(ins)
            e.wait_ge(ss.h, k)
            ins = e.activation(sclr[0:1, 2:3], sclr[0:1, 1:2], AF.Ln, bias=c1_col[0:1, 0:1])
            at["s_sp"] = ss.inc(ins)
            e.wait_ge(sem_vsu.h, 3)
            ins = e.activation(dsrow_sb[0:1, :], EJ_sb[0:1, :], AF.Exp, scale=sclr[0:1, 3:4])
            k = ss.inc(ins)
            ins = e.activation(wrrow_sb[0:1, :], E4_sb[0:1, :], AF.Exp, scale=sclr[0:1, 3:4])
            at["s_rows"] = ss.inc(ins)
            e.wait_ge(sem_dbc.h, 32)
            ins = e.activation(gath_sb[:, 0:128], E1_sb[:, :], AF.Exp, scale=nsp_ap())
            at["s_dtraw"] = ss.inc(ins)
            ins = e.activation(DQ_sb[:, :], EQ_sb[:, :], AF.Exp, scale=nsp_ap())
            at["s_dq"] = ss.inc(ins)
            e.activation(dk_col[:, 0:1], CK_sb[:, 0:1], AF.Exp, scale=nsp_ap())
            ins = e.activation(dL_col[:, 0:1], CK_sb[:, 1:2], AF.Exp, scale=nsp_ap())
            at["s_dk"] = ss.inc(ins)
            for i in range(ntiles):
                e.wait_ge(sem_vbn.h, i + 1)
                ins = e.activation(rcol[i % 4][:, 0:1], mv[i % 4][:, 1:2], AF.Ln, bias=ceps_col[:, 0:1])
                k = ss.inc(ins)
                e.wait_ge(ss.h, k)
                ins = e.activation(rcol[i % 4][:, 0:1], rcol[i % 4][:, 0:1], AF.Exp, scale=-0.5)
                at[f"s_r{i}"] = ss.inc(ins)
                if i >= 1:
                    e.wait_ge(sem_ttr.h, i)
                    src = bass.AP(pbfa, 0, [[1024, 128], [128, 8], [1, 128]])
                    dst = bass.AP(xn_fm, (i - 1) * 128, [[8 * rpc, 128], [rpc, 8], [1, 128]])
                    ins = e.activation(dst, src, AF.Copy)
                    at[f"s_xnev{i - 1}"] = ss.inc(ins)
                    if (i - 1) % 4 == 3:
                        s_proj_ev(e, (i - 1) // 4)
            e.wait_ge(sem_ttr.h, ntiles)
            src = bass.AP(pbfa, 0, [[1024, 128], [128, 8], [1, 128]])
            dst = bass.AP(xn_fm, (ntiles - 1) * 128, [[8 * rpc, 128], [rpc, 8], [1, 128]])
            ins = e.activation(dst, src, AF.Copy)
            at[f"s_xnev{ntiles - 1}"] = ss.inc(ins)
            s_proj_ev(e, (ntiles - 1) // 4)

        blk.scalar(scalar_mega1)

        def v_setup1(e):
            e.memset(row_sb[0:1, :], 0.0)
            e.memset(c1_col[:, :], 1.0)
            e.memset(ceps_col[:, :], 1e-5)
            e.wait_ge(sem_dw0.h, sem_dw0.n)
            ins = e.tensor_scalar(sclr[0:1, 0:1], sc_in[0:1, 3:4], -1.0, 80.0, OP.mult, OP.min)
            ins.then_inc(sem_vsu.h, 2)
            sem_vsu.n += 2

        blk.vector(v_setup1)

        def v_setup2(e):
            e.wait_ge(ss.h, at["s_sp"])
            ins = e.tensor_scalar(sclr[0:1, 3:4], sclr[0:1, 2:3], -1.0, None, OP.mult)
            ins.then_inc(sem_vsu.h, 1)
            sem_vsu.n += 1
            e.tensor_scalar(sclr[0:1, 4:5], sc_in[0:1, 0:1], 0.5, None, OP.mult)
            e.tensor_scalar(sclr[0:1, 5:6], sc_in[0:1, 1:2], 0.5, None, OP.mult)
            e.tensor_scalar(sclr[0:1, 6:7], sc_in[0:1, 2:3], 0.3, None, OP.mult)
            ins = e.tensor_scalar(sclr[0:1, 7:8], sc_in[0:1, 2:3], 0.7, None, OP.mult)
            at["v_scl"] = sv.inc(ins)

        blk.vector(v_setup2)

        def v_setup3(e):
            e.wait_ge(ss.h, at["s_rows"])
            ins = e.tensor_tensor(wrrow_sb[0:1, :], wrrow_sb[0:1, :], M4_sb[0:1, :], OP.mult)
            k = sv.inc(ins)
            e.wait_ge(sv.h, k)
            e.tensor_copy(row_sb[0:1, 0:1], sclr[0:1, 3:4])
            e.tensor_copy(row_sb[0:1, 1:2], sc_in[0:1, 4:5])
            e.tensor_copy(row_sb[0:1, 2:2 + nch], dsrow_sb[0:1, :])
            ins = e.tensor_copy(row_sb[0:1, 2 + nch:2 + nch + GRP], wrrow_sb[0:1, :])
            at["v_row"] = sv.inc(ins)

        blk.vector(v_setup3)

        def g_bcast(e):
            e.wait_ge(sv.h, at["v_row"])
            ins = e.dma_start(out=bstage[:, :], in_=row_sb[0:1, :])
            k = sem_dbc.inc(ins, 16)
            e.wait_ge(sem_dbc.h, k)
            ins = e.dma_start(out=bcast[:, :], in_=bass.AP(bstage, 0, [[0, 128], [1, 32]]))
            dw_at["bcast"] = sem_dbc.inc(ins, 16)

        blk.gpsimd(g_bcast)

        def v_decay(e):
            e.wait_ge(ss.h, at["s_dtraw"])
            ins = e.tensor_tensor(DT_sb[:, :], gath_sb[:, 0:128], M1_sb[:, :], OP.mult)
            at["v_dt"] = sv.inc(ins)

        blk.vector(v_decay)

        trn = {"n": 0}

        def emit_tr_batch(specs):
            i = trn["n"]
            trn["n"] += 1
            dst_sb, dst_col = specs[0][0], specs[0][1]
            nb = len(specs)

            def t_f(e):
                if i == 0:
                    e.wait_ge(sem_dw00.h, sem_dw00.n)
                else:
                    e.wait_ge(sv.h, at[f"v_tr{i - 1}"])
                last = None
                for bi, (_, _, src_sb, src_col) in enumerate(specs):
                    last = e.transpose(pbfb[:, bi * 128:(bi + 1) * 128],
                                       src_sb[:, src_col:src_col + 128], id_sb[:, :])
                at[f"t_tr{i}"] = sq.inc(last)

            def v_f(e):
                e.wait_ge(sq.h, at[f"t_tr{i}"])
                ins = e.tensor_copy(dst_sb[:, dst_col:dst_col + nb * 128], pbfb[:, 0:nb * 128])
                at[f"v_tr{i}"] = sv.inc(ins)

            blk.tensor(t_f)
            blk.vector(v_f)

        for jck in range(2):
            for vh in range(2):
                emit_tr_batch([(basT, jck * D + (vh * 4) * 128, basis_bf, vck * C + jck * 128)
                               for vck in range(vh * 4, vh * 4 + 4)])
        for w_bf, w_T in ((qc_bf, qcT), (kc_bf, kcT), (vc_bf, vcT), (oc_bf, ocT)):
            emit_tr_batch([(w_T, 0, w_bf, cck * C + jck * 128)
                           for jck in range(2) for cck in range(2)])
        n_tr_total = trn["n"]

        prev_w = {"q": None, "k": "q", "v": "k"}

        def emit_w_pair(wT, dst, nm):
            for mv_ in range(8):
                bank = pf[4] if mv_ % 2 == 0 else pf[5]

                def t_f(e, mv_=mv_, wT=wT, bank=bank, nm=nm):
                    if mv_ == 0 and nm == "q":
                        e.wait_ge(sv.h, at[f"v_tr{n_tr_total - 1}"])
                    if mv_ < 2 and prev_w[nm] is not None:
                        e.wait_ge(sv.h, at[f"v_w{prev_w[nm]}{mv_ + 6}"])
                    prev = at.get(f"v_w{nm}{mv_ - 2}")
                    if prev is not None:
                        e.wait_ge(sv.h, prev)
                    last = None
                    for kj in range(2):
                        last = e.matmul(bank[:, 0:C],
                                        basT[:, kj * D + mv_ * 128: kj * D + mv_ * 128 + 128],
                                        wT[:, kj * C: kj * C + C],
                                        start=(kj == 0), stop=(kj == 1))
                    at[f"t_w{nm}{mv_}"] = sq.inc(last)

                def v_f(e, mv_=mv_, dst=dst, bank=bank, nm=nm):
                    e.wait_ge(sq.h, at[f"t_w{nm}{mv_}"])
                    ins = e.tensor_copy(dst[:, mv_ * C: mv_ * C + C], bank[:, 0:C])
                    at[f"v_w{nm}{mv_}"] = sv.inc(ins)

                blk.tensor(t_f)
                blk.vector(v_f)

        emit_w_pair(qcT, qw_sb, "q")
        emit_w_pair(kcT, kw_sb, "k")
        emit_w_pair(vcT, vw_sb, "v")

        for mc in range(2):
            for nh in range(2):
                bank = pf[4] if nh == 0 else pf[5]
                prev_key = (f"v_ow{mc}{nh - 1}" if nh == 1 else (f"v_ow{mc - 1}{1}" if mc > 0 else "v_wv7"))

                def t_f(e, mc=mc, nh=nh, bank=bank, prev_key=prev_key):
                    e.wait_ge(sv.h, at[prev_key])
                    last = None
                    for kj in range(2):
                        last = e.matmul(bank[:, 0:512],
                                        ocT[:, kj * C + mc * 128: kj * C + mc * 128 + 128],
                                        basT[:, kj * D + nh * 512: kj * D + nh * 512 + 512],
                                        start=(kj == 0), stop=(kj == 1))
                    at[f"t_ow{mc}{nh}"] = sq.inc(last)

                def v_f(e, mc=mc, nh=nh, bank=bank):
                    e.wait_ge(sq.h, at[f"t_ow{mc}{nh}"])
                    e.wait_ge(sem_dbc.h, dw_at["bcast"])
                    ins = e.tensor_scalar(owt_sb[:, mc * D + nh * 512: mc * D + nh * 512 + 512],
                                          bank[:, 0:512], osc_ap(), None, OP.mult)
                    at[f"v_ow{mc}{nh}"] = sv.inc(ins)

                blk.tensor(t_f)
                blk.vector(v_f)

        def v_tile(i):
            def f(e):
                e.wait_ge(sem_dx[i % 4].h, 16 * (i // 4 + 1))
                if i >= 4:
                    e.wait_ge(ss.h, at[f"s_r{i - 4}"])
                e.bn_stats(stats[i % 4][:, 0:6], xin[i % 4][:, 0:512])
                ins = e.bn_stats(stats[i % 4][:, 6:12], xin[i % 4][:, 512:1024])
                k = sv.inc(ins)
                e.wait_ge(sv.h, k)
                ins = e.bn_aggr(mv[i % 4][:, 0:2], stats[i % 4][:, 0:12])
                ins.then_inc(sem_vbn.h, 1)
                sem_vbn.n += 1
            return f

        def v_norm(i):
            def f(e):
                e.wait_ge(ss.h, at[f"s_r{i}"])
                if i >= 2:
                    e.wait_ge(sem_txs.h, i - 1)
                ins = e.tensor_scalar(xn_tm[i % 2][:, :], xin[i % 4][:, :],
                                      mv[i % 4][:, 0:1], rcol[i % 4][:, 0:1],
                                      OP.subtract, OP.mult)
                ins.then_inc(sem_nrm.h, 1)
                sem_nrm.n += 1
            return f

        def t_tile(i):
            def f(e):
                e.wait_ge(sem_nrm.h, i + 1)
                if i == 0:
                    e.wait_ge(sem_dw00.h, sem_dw00.n)
                    e.wait_ge(sem_dw0.h, sem_dw0.n)
                if i >= 1:
                    e.wait_ge(ss.h, at[f"s_xnev{i - 1}"])
                last = None
                for ck in range(8):
                    last = e.transpose(pbfa[:, ck * 128:(ck + 1) * 128],
                                       xn_tm[i % 2][:, ck * 128:(ck + 1) * 128], id_sb[:, :])
                last.then_inc(sem_ttr.h, 1)
                sem_ttr.n += 1
                last = None
                for ck in range(8):
                    last = e.matmul(pf[0][:, ck:ck + 1],
                                    xn_tm[i % 2][:, ck * 128:(ck + 1) * 128],
                                    onesc_bf[:, 0:1],
                                    start=(i == 0 and ck == 0), stop=(i == ntiles - 1 and ck == 7))
                last.then_inc(sem_txs.h, 1)
                sem_txs.n += 1
            return f

        def t_proj(st):
            def f(e):
                e.wait_ge(ss.h, at[f"s_xnev{4 * st + 3}"])
                e.wait_ge(sv.h, at["v_wv7"])
                if st >= 1:
                    e.wait_ge(ss.h, at[f"s_kev{st - 1}_1"])
                else:
                    e.wait_ge(sv.h, at["v_ow11"])
                last = None
                for (wsb, banks, nm) in ((qw_sb, (pf[1], pf[2]), "q"), (kw_sb, (pf[3], pf[4]), "k")):
                    for mt in range(2):
                        for kc in range(8):
                            last = e.matmul(banks[mt][:, 0:512],
                                            wsb[:, kc * C + mt * 128: kc * C + mt * 128 + 128],
                                            xn_fm[:, kc * rpc + st * PST: kc * rpc + st * PST + PST],
                                            start=(kc == 0), stop=(kc == 7))
                last.then_inc(sem_tpj.h, 1)
                sem_tpj.n += 1
            return f

        def t_v(gt, st):
            def f(e):
                prev = at.get(f"s_vev{gt - 1}")
                if prev is not None:
                    e.wait_ge(ss.h, prev)
                elif st == 0:
                    e.wait_ge(sv.h, at["v_ow11"])
                e.wait_ge(ss.h, at[f"s_xnev{gt}"])
                last = None
                for kc in range(8):
                    last = e.matmul(pf[5][:, 0:C],
                                    xn_fm[:, kc * rpc + gt * 128: kc * rpc + gt * 128 + 128],
                                    vw_sb[:, kc * C: kc * C + C],
                                    start=(kc == 0), stop=(kc == 7))
                last.then_inc(sem_tv.h, 1)
                sem_tv.n += 1
            return f

        def t_ktr(gt, st):
            def f(e):
                e.wait_ge(ss.h, at[f"s_kev{st}_1"])
                prev = at.get(f"s_kdev{gt - 1}")
                if prev is not None:
                    e.wait_ge(ss.h, prev)
                last = None
                for ct in range(2):
                    last = e.transpose(pbfb[:, ct * 128: ct * 128 + 128],
                                       k_fm[:, ct * rpc + gt * 128: ct * rpc + gt * 128 + 128],
                                       id_sb[:, :])
                last.then_inc(sem_tkt.h, 1)
                sem_tkt.n += 1
            return f

        for i in range(ntiles + 1):
            if i < ntiles:
                blk.vector(v_tile(i))
            if i >= 1:
                j0 = i - 1
                blk.vector(v_norm(j0))
                blk.tensor(t_tile(j0))
                if j0 % 4 == 3:
                    st = j0 // 4
                    blk.tensor(t_proj(st))
                    for tt in range(4):
                        gt = 4 * st + tt
                        blk.tensor(t_v(gt, st))
                        blk.tensor(t_ktr(gt, st))

        def v_xnsum(e):
            e.wait_ge(sem_txs.h, ntiles)
            ins = e.tensor_copy(xnsum_sb[:, :], pf[0][:, 0:8])
            at["v_xnsum"] = sv.inc(ins)

        blk.vector(v_xnsum)

        for jj in range(nch - 1, -1, -1):
            bank = pf[1] if jj % 2 else pf[2]

            def t_cj(e, jj=jj, bank=bank):
                if jj == nch - 1:
                    e.wait_ge(ss.h, at[f"s_kdev{nch - 1}"])
                    e.wait_ge(ss.h, at[f"s_vev{nch - 1}"])
                    e.wait_ge(ss.h, at[f"s_kev{npst - 1}_1"])
                prev = at.get(f"v_T{jj + 2}")
                if prev is not None:
                    e.wait_ge(sv.h, prev)
                last = None
                for mt in range(2):
                    last = e.matmul(bank[:, mt * 256: mt * 256 + 256],
                                    kd_tm[:, jj * C + mt * 128: jj * C + mt * 128 + 128],
                                    v_tm[:, jj * C: jj * C + C],
                                    start=True, stop=True)
                at[f"t_C{jj}"] = sq.inc(last)

            def v_tj(e, jj=jj, bank=bank):
                e.wait_ge(sq.h, at[f"t_C{jj}"])
                if jj == nch - 1:
                    ins = e.tensor_copy(t_st[:, jj * 512:(jj + 1) * 512], bank[:, 0:512])
                else:
                    e.wait_ge(sv.h, at[f"v_T{jj + 1}"])
                    ins = e.scalar_tensor_tensor(t_st[:, jj * 512:(jj + 1) * 512],
                                                 t_st[:, (jj + 1) * 512:(jj + 2) * 512],
                                                 dL_col[:, 0:1], bank[:, 0:512],
                                                 OP.mult, OP.add)
                at[f"v_T{jj}"] = sv.inc(ins)

            blk.tensor(t_cj)
            blk.vector(v_tj)

        PERM = [4, 5, 6, 7, 0, 1, 2, 3]
        SIGNP = [u for u in range(8) if [1, 1, -1, -1, -1, -1, 1, 1][u] > 0]
        SIGNM = [u for u in range(8) if [1, 1, -1, -1, -1, -1, 1, 1][u] < 0]

        s3sched = []
        for st_ in range(nfst):
            if st_ == 0:
                s3sched += [("g1_0_0", 8)]
            else:
                s3sched += [(f"g1_{st_}_0", 8), (f"atev_{st_}_0", 2),
                            (f"atev_{st_}_1", 2), (f"bpev_{st_}", 4)]
            s3sched += [(f"g2_{st_}_0", 4), (f"g3_{st_}_0", 2), (f"th4_{st_}_0", 1)]
            if st_ == 0:
                s3sched += [("gf1", 1), ("gf2", 1), ("gf3", 1)]
            s3sched += [(f"crow_{st_}_0", 1)]
            s3sched += [(f"g1_{st_}_1", 8), (f"g2_{st_}_1", 4), (f"g3_{st_}_1", 2),
                        (f"th4_{st_}_1", 1), (f"g1_{st_}_2", 8)]
            if st_ + 1 < nfst:
                s3sched += [(f"aev_{st_ + 1}", 4)]
            s3sched += [(f"g2_{st_}_2", 4), (f"g3_{st_}_2", 2), (f"th4_{st_}_2", 1)]
        s3pos = {}
        _c = 0
        for _n, _k in s3sched:
            s3pos[_n] = _c
            _c += _k

        def s3g1(st, cyc, k):
            return s3pos[f"g1_{st}_{cyc}"] + k
        def s3g2(st, cyc, k):
            return s3pos[f"g2_{st}_{cyc}"] + k
        def s3g3(st, cyc, k):
            return s3pos[f"g3_{st}_{cyc}"] + k
        def s3th4(st, cyc):
            return s3pos[f"th4_{st}_{cyc}"] + 1
        def s3aev(st):
            return s3pos[f"aev_{st}"] + 4
        def s3bpev(st):
            return s3pos[f"bpev_{st}"] + 4
        def s3atev(st, tt):
            return s3pos[f"atev_{st}_{tt}"] + 2

        def s3b(st, cyc):
            return s3g1(st, cyc, 0)

        def v_sj(j):
            def f(e):
                e.wait_ge(sv.h, at["v_A3"])
                e.wait_ge(sem_vr.h, GRP)
                prev = at.get(f"v_retr{j - 2}")
                if prev is not None:
                    e.wait_ge(sv.h, prev)
                if j == nch - 1:
                    ins = e.tensor_scalar(sj_bf[j % 2][:, :], sin_sb[:, :], dS_ap(j), None, OP.mult)
                else:
                    ins = e.scalar_tensor_tensor(sj_bf[j % 2][:, :], sin_sb[:, :], dS_ap(j),
                                                 t_st[:, (j + 1) * 512:(j + 2) * 512], OP.mult, OP.add)
                k = sv.inc(ins)
                for ck in range(2):
                    ins = e.tensor_tensor(qd_tmp[j % 2][:, ck * 128:(ck + 1) * 128],
                                          q_fm[:, ck * rpc + j * 128: ck * rpc + j * 128 + 128],
                                          DQ_sb[:, :], OP.mult)
                at[f"v_sj{j}"] = sv.inc(ins)
            return f

        def t_scores(j, st):
            def f(e):
                e.wait_ge(sv.h, at[f"v_sj{j}"])
                prev = at.get(f"v_sco{j - 2}")
                if prev is not None:
                    e.wait_ge(sv.h, prev)
                e.wait_ge(sem_s3.h, s3g3(max(st - 1, 0), 2, 2))
                rg = (j % 2) * 128
                last = None
                for kc in range(2):
                    last = e.matmul(pf[4][:, rg:rg + 128],
                                    k_fm[:, kc * rpc + j * 128: kc * rpc + j * 128 + 128],
                                    q_fm[:, kc * rpc + j * 128: kc * rpc + j * 128 + 128],
                                    start=(kc == 0), stop=(kc == 1))
                at[f"t_sco{j}"] = sq.inc(last)
            return f

        def v_scores(j):
            def f(e):
                e.wait_ge(sq.h, at[f"t_sco{j}"])
                prev = at.get(f"t_retr{j - 2}")
                if prev is not None:
                    e.wait_ge(sq.h, prev)
                rg = (j % 2) * 128
                ins = e.tensor_tensor(sco_sb[j % 2][:, :], pf[4][:, rg:rg + 128],
                                      DT_sb[:, :], OP.mult)
                at[f"v_sco{j}"] = sv.inc(ins)
            return f

        def t_retr(j, st):
            def f(e):
                e.wait_ge(sv.h, at[f"v_sco{j}"])
                pst = max(st - 1, 0)
                e.wait_ge(sem_v3.h, (pst * 2 + 2) * 8)
                e.wait_ge(sem_s3.h, s3th4(pst, 2))
                rg = (j % 2) * 256
                last = None
                for mc in range(2):
                    e.matmul(pf[5][:, rg + mc * 128: rg + mc * 128 + 128],
                             v_tm[:, j * C + mc * 128: j * C + mc * 128 + 128],
                             sco_sb[j % 2][:, :], start=True, stop=False)
                    e.matmul(pf[5][:, rg + mc * 128: rg + mc * 128 + 128],
                             sj_bf[j % 2][:, mc * 128: mc * 128 + 128],
                             qd_tmp[j % 2][:, 0:128], start=False, stop=False)
                    last = e.matmul(pf[5][:, rg + mc * 128: rg + mc * 128 + 128],
                                    sj_bf[j % 2][:, C + mc * 128: C + mc * 128 + 128],
                                    qd_tmp[j % 2][:, 128:256], start=False, stop=True)
                at[f"t_retr{j}"] = sq.inc(last)
            return f

        def v_retr(j):
            def f(e):
                e.wait_ge(sq.h, at[f"t_retr{j}"])
                rg = (j % 2) * 256
                ins = e.tensor_copy(retr_sb[j % 2][:, :], pf[5][:, rg:rg + C])
                at[f"v_retr{j}"] = sv.inc(ins)
            return f

        def t_oproj(j, st):
            def f(e):
                e.wait_ge(sv.h, at[f"v_retr{j}"])
                if st == 0:
                    prev = at.get(f"v_attnev{j - 1}")
                    if prev is not None:
                        e.wait_ge(sv.h, prev)
                    e.wait_ge(sv.h, at["v_Bev0"])
                    if nfst > 1:
                        e.wait_ge(sem_s3.h, s3aev(1))
                elif j % cpst == 0:
                    e.wait_ge(sem_s3.h, s3aev(st))
                else:
                    e.wait_ge(sem_s3.h, s3atev(st, j % cpst - 1))
                last = None
                for nh in range(2):
                    for kc in range(2):
                        last = e.matmul(pf[2 + nh][:, 0:512],
                                        retr_sb[j % 2][:, kc * 128: kc * 128 + 128],
                                        owt_sb[:, kc * D + nh * 512: kc * D + nh * 512 + 512],
                                        start=(kc == 0), stop=(kc == 1))
                at[f"t_o{j}"] = sq.inc(last)
            return f

        def v_oproj(j, st):
            tt = j - st * cpst

            def f(e):
                e.wait_ge(sq.h, at[f"t_o{j}"])
                ab = attn2[st % 2]
                e.tensor_copy(ab[:, tt * D: tt * D + 512], pf[2][:, 0:512])
                ins = e.tensor_copy(ab[:, tt * D + 512: tt * D + 1024], pf[3][:, 0:512])
                at[f"v_attnev{j}"] = sv.inc(ins)
            return f

        def emit_attn(st):
            for j in range(st * cpst, (st + 1) * cpst):
                blk.vector(v_sj(j))
                blk.tensor(t_scores(j, st))
                blk.vector(v_scores(j))
                blk.tensor(t_retr(j, st))
                blk.vector(v_retr(j))
                blk.tensor(t_oproj(j, st))
                if st == 0:
                    blk.vector(v_oproj(j, st))

        def xn_slice(ck, st):
            return xn_fm[:, ck * rpc + st * FST: ck * rpc + st * FST + FST]

        def t_A(st, half=None):
            mts = range(8) if half is None else range(half * 4, half * 4 + 4)
            def f(e):
                e.wait_ge(sem_dwb.h, sem_dwb.n)
                if half in (None, 0):
                    if st == 0:
                        e.wait_ge(sv.h, at["v_T0"])
                    else:
                        e.wait_ge(sv.h, at[f"v_Bev{st - 1}"])
                for mt in mts:
                    bank = pf[mt // 2]
                    cols = (mt % 2) * 256
                    last = None
                    for kc in range(8):
                        last = e.matmul(bank[:, cols:cols + 256],
                                        w1_sb[:, kc * D + mt * 128: kc * D + mt * 128 + 128],
                                        xn_slice(kc, st), start=(kc == 0), stop=(kc == 7))
                    at[f"t_h1_{st}_0_{mt}"] = sq.inc(last)
            return f

        def v_Aev(st):
            def f(e):
                if st >= 1:
                    e.wait_ge(sem_v3.h, ((st - 1) * 2 + 2) * 8)
                ins = None
                for b in range(4):
                    e.wait_ge(sq.h, at[f"t_h1_{st}_0_{2 * b + 1}"])
                    ins = e.tensor_copy(A_sb[:, b * 512:(b + 1) * 512], pf[b][:, 0:512])
                at[f"v_Aev{st}"] = sv.inc(ins)
            return f

        def t_Bpass(st, sign_set, tag, half=None):
            mts = range(8) if half is None else range(half * 4, half * 4 + 4)
            def f(e):
                if tag == "p":
                    if st == 0:
                        e.wait_ge(sv.h, at["v_Aev0"])
                    else:
                        e.wait_ge(sem_s3.h, s3atev(st, cpst - 1))
                elif half in (None, 0):
                    if st == 0:
                        e.wait_ge(sv.h, at["v_Bpev0"])
                    else:
                        e.wait_ge(sem_s3.h, s3bpev(st))
                for mt in mts:
                    bank = pf[mt // 2]
                    cols = (mt % 2) * 256
                    last = None
                    for ki, u in enumerate(sign_set):
                        last = e.matmul(bank[:, cols:cols + 256],
                                        w1_sb[:, u * D + mt * 128: u * D + mt * 128 + 128],
                                        xn_slice(PERM[u], st),
                                        start=(ki == 0), stop=(ki == len(sign_set) - 1))
                    at[f"t_B{tag}_{st}_{mt}"] = sq.inc(last)
            return f

        def v_Bpev(st):
            def f(e):
                e.wait_ge(sq.h, at[f"t_Bp_{st}_7"])
                if st >= 1:
                    e.wait_ge(sq.h, at[f"t_fT{st - 1}_1"])
                ins = None
                for b in range(4):
                    ins = e.tensor_copy(comb_sb[:, b * 512:(b + 1) * 512], pf[b][:, 0:512])
                at[f"v_Bpev{st}"] = sv.inc(ins)
            return f

        def v_Bmev(st):
            def f(e):
                e.wait_ge(sq.h, at[f"t_Bm_{st}_7"])
                if st >= 1:
                    e.wait_ge(sem_s3.h, s3bpev(st))
                ins = None
                for b in range(4):
                    ins = e.tensor_tensor(B_sb[:, b * 512:(b + 1) * 512],
                                          comb_sb[:, b * 512:(b + 1) * 512], pf[b][:, 0:512],
                                          OP.subtract)
                at[f"v_Bev{st}"] = sv.inc(ins)
            return f


        blk.tensor(t_A(0))
        blk.vector(v_Aev(0))
        blk.tensor(t_Bpass(0, SIGNP, "p"))
        blk.vector(v_Bpev(0))
        blk.tensor(t_Bpass(0, SIGNM, "m"))
        blk.vector(v_Bmev(0))

        def g_ccx(e):
            e.wait_ge(sv.h, at["v_xnsum"])
            ins = e.dma_start(out=bass.AP(ccx_in, 0, [[8, 128], [1, 8]]),
                              in_=xnsum_sb[:, :], allow_slow_non_contiguous=True)
            k1 = sem_dcc.inc(ins, 16)
            e.wait_ge(sem_dcc.h, k1)
            ins = e.collective_compute(
                "AllGather", OP.bypass,
                replica_groups=[[0, 1, 2, 3], [4, 5, 6, 7]],
                ins=[ccx_in[:, :]], outs=[ccx_out[:, :]])
            at["ccx"] = sem_ccx.inc(ins, 1)

        blk.gpsimd(g_ccx)

        def g_cc(e):
            e.wait_ge(sv.h, at["v_T0"])
            ins = e.dma_start(out=bass.AP(cc_in, 0, [[512, 128], [1, 512]]), in_=t_st[:, 0:512])
            k2 = sem_dcc.inc(ins, 16)
            e.wait_ge(sem_dcc.h, k2)
            ins = e.collective_compute(
                "AllGather", OP.bypass,
                replica_groups=[[0, 1, 2, 3], [4, 5, 6, 7]],
                ins=[cc_in[:, :]], outs=[cc_out[:, :]])
            at["cc"] = sem_cc.inc(ins, 1)

        blk.gpsimd(g_cc)

        def g_xg(e):
            e.wait_ge(sem_ccx.h, at["ccx"])
            ins = None
            for r in range(GRP):
                ins = e.dma_start(out=xg_sb[r][:, :],
                                  in_=bass.AP(ccx_out, r * D, [[8, 128], [1, 8]]),
                                  allow_slow_non_contiguous=True)
                sem_dxg.inc(ins, 16)
            at["d_xg"] = sem_dxg.n

        blk.gpsimd(g_xg)

        def v_rx(e):
            e.wait_ge(sem_dxg.h, at["d_xg"])
            ins = e.tensor_copy(xnt_sb[:, :], xg_sb[0][:, :])
            for r in range(1, GRP):
                k = sv.inc(ins)
                e.wait_ge(sv.h, k)
                ins = e.tensor_tensor(xnt_sb[:, :], xnt_sb[:, :], xg_sb[r][:, :], OP.add)
            k = sv.inc(ins)
            e.wait_ge(sv.h, k)
            ins = e.tensor_copy(xnt_bf[:, :], xnt_sb[:, :])
            at["v_xnt"] = sv.inc(ins)

        v_r_bodies = []
        for r in range(GRP):
            def g_r(e, r=r):
                if r == 0:
                    e.wait_ge(sem_cc.h, at["cc"])
                else:
                    e.wait_ge(sem_vr.h, r)
                ins = e.dma_start(out=gath_sb[:, :], in_=bass.AP(cc_out, r * CCN, [[512, 128], [1, 512]]))
                at[f"d_rnd{r}"] = sem_dcc.inc(ins, 16)

            def v_r(e, r=r):
                e.wait_ge(sem_dcc.h, at[f"d_rnd{r}"])
                if r == 0:
                    e.wait_ge(sem_dbc.h, dw_at["bcast"])
                if r == 0:
                    ins = e.tensor_scalar(sin_sb[:, :], gath_sb[:, :], wr_ap(0), None, OP.mult)
                else:
                    ins = e.scalar_tensor_tensor(sin_sb[:, :], gath_sb[:, :], wr_ap(r),
                                                 sin_sb[:, :], OP.mult, OP.add)
                ins.then_inc(sem_vr.h, 1)
                sem_vr.n += 1

            blk.gpsimd(g_r)
            v_r_bodies.append(v_r)

        def t_gf1(e):
            e.wait_ge(sv.h, at["v_xnt"])
            e.wait_ge(sem_dwb.h, sem_dwb.n)
            e.wait_ge(sem_s3.h, s3g3(0, 0, 2))
            last = None
            for mt in range(4):
                for kc in range(8):
                    last = e.matmul(pf[4][:, mt:mt + 1],
                                    g1_sb[:, kc * 512 + mt * 128: kc * 512 + mt * 128 + 128],
                                    xnt_bf[:, kc:kc + 1],
                                    start=(kc == 0), stop=(kc == 7))
            at["t_gf1"] = sq.inc(last)

        def t_gf2(e):
            e.wait_ge(sem_s3.h, s3pos["gf1"] + 1)
            e.wait_ge(sem_s3.h, s3th4(0, 0))
            last = None
            for mt in range(2):
                for kc in range(4):
                    last = e.matmul(pf[5][:, mt:mt + 1],
                                    g2_sb[:, kc * 256 + mt * 128: kc * 256 + mt * 128 + 128],
                                    gg1_sb[:, kc:kc + 1],
                                    start=(kc == 0), stop=(kc == 3))
            at["t_gf2"] = sq.inc(last)

        def t_gf3(e):
            e.wait_ge(sem_s3.h, s3pos["gf2"] + 1)
            last = None
            for kc in range(2):
                last = e.matmul(pf[4][0:1, 16:17], g3_sb[:, kc:kc + 1], gg2_sb[:, kc:kc + 1],
                                start=(kc == 0), stop=(kc == 1))
            at["t_gf3"] = sq.inc(last)

        def v_gf(e):
            e.wait_ge(sem_s3.h, s3pos["gf3"] + 1)
            e.wait_ge(sv.h, at["v_scl"])
            ins = e.tensor_scalar(sclr[0:1, 9:10], sclr[0:1, 8:9], sclr[0:1, 7:8], 0.1, OP.mult, OP.add)
            at["v_A3"] = sv.inc(ins)


        def t_h2(st, cyc):
            def f(e):
                waited = 0
                if cyc == 0 and st >= 1:
                    e.wait_ge(sv.h, at[f"v_retr{st * cpst + cpst - 1}"])
                for mt in range(4):
                    bank = pf[4 + mt // 2]
                    cols = (mt % 2) * 256
                    last = None
                    for kc in range(8):
                        tgt = s3g1(st, cyc, kc + 1 if CHASE else 8)
                        if tgt > waited:
                            waited = tgt
                            e.wait_ge(sem_s3.h, waited)
                        last = e.matmul(bank[:, cols:cols + 256],
                                        w2_sb[:, kc * 512 + mt * 128: kc * 512 + mt * 128 + 128],
                                        h1_bf[:, kc * FST:(kc + 1) * FST],
                                        start=(kc == 0), stop=(kc == 7))
                    at[f"t_h2_{st}_{cyc}_{mt}"] = sq.inc(last)
            return f

        def t_h3z4(st, cyc):
            def f(e):
                waited = 0
                for mt in range(2):
                    last = None
                    for kc in range(4):
                        tgt = max(s3g2(st, cyc, kc + 1), s3g2(st, cyc, 2)) if CHASE else s3g2(st, cyc, 4)
                        if tgt > waited:
                            waited = tgt
                            e.wait_ge(sem_s3.h, waited)
                        last = e.matmul(pf[4][:, mt * 256: mt * 256 + 256],
                                        w3_sb[:, kc * 256 + mt * 128: kc * 256 + mt * 128 + 128],
                                        h2_bf[:, kc * FST:(kc + 1) * FST],
                                        start=(kc == 0), stop=(kc == 3))
                    at[f"t_h3_{st}_{cyc}_{mt}"] = sq.inc(last)
                last = None
                for kc in range(2):
                    tgt = s3g3(st, cyc, kc + 1) if CHASE else s3g3(st, cyc, 2)
                    if tgt > waited:
                        waited = tgt
                        e.wait_ge(sem_s3.h, waited)
                    last = e.matmul(pf[5][0:1, 0:FST], w4_sb[:, kc:kc + 1],
                                    h3_bf[:, kc * FST:(kc + 1) * FST],
                                    start=(kc == 0), stop=(kc == 1))
                at[f"t_z4_{st}_{cyc}"] = sq.inc(last)
            return f

        def v_crow_f(st, cyc):
            def f(e):
                e.wait_ge(sem_s3.h, s3th4(st, cyc))
                if cyc == 1:
                    ins = e.tensor_scalar(c2_bf[0:1, :], th4_row[0:1, :],
                                          sclr[0:1, 6:7], sclr[0:1, 9:10], OP.mult, OP.add)
                    k = sv.inc(ins)
                    e.wait_ge(sv.h, k)
                    ins = e.tensor_tensor(abrow[0:1, 0:FST], brow_bf[0:1, :],
                                          c2_bf[0:1, :], OP.mult)
                    k = sv.inc(ins)
                    e.wait_ge(sv.h, k)
                    e.tensor_scalar(arow_bf[0:1, :], abrow[0:1, 0:FST], -1.0, 1.0,
                                    OP.mult, OP.add)
                    ins = e.tensor_tensor(brow_bf[0:1, :], brow_bf[0:1, :],
                                          c2_bf[0:1, :], OP.add)
                else:
                    ins = e.tensor_scalar(crows[0:1, 0:FST], th4_row[0:1, :],
                                          sclr[0:1, 6:7], sclr[0:1, 9:10], OP.mult, OP.add)
                    k = sv.inc(ins)
                    e.wait_ge(sv.h, k)
                    e.tensor_tensor(abrow[0:1, 0:FST], brow_bf[0:1, :],
                                    crows[0:1, 0:FST], OP.mult)
                    ins = e.tensor_tensor(abrow[0:1, FST:2 * FST], arow_bf[0:1, :],
                                          crows[0:1, 0:FST], OP.mult)
                    k = sv.inc(ins)
                    e.wait_ge(sv.h, k)
                    e.tensor_tensor(crows[0:1, 0:FST], arow_bf[0:1, :],
                                    abrow[0:1, 0:FST], OP.subtract)
                    ins = e.tensor_tensor(crows[0:1, FST:2 * FST], brow_bf[0:1, :],
                                          abrow[0:1, FST:2 * FST], OP.add)
                    k = sv.inc(ins)
                    e.wait_ge(sv.h, k)
                    ins = e.tensor_scalar(crows[0:1, 2 * FST:3 * FST],
                                          crows[0:1, FST:2 * FST], -1.0, None, OP.mult)
                at[f"v_crow{st}_{cyc}"] = sv.inc(ins)
            return f

        def t_cb_f(st, cyc):
            def f(e):
                if cyc == 0:
                    e.wait_ge(sem_s3.h, s3pos[f"crow_{st}_0"] + 1)
                    ins = e.matmul(pf[5][:, 256:256 + FST], onesr_bf[0:1, 0:128],
                                   brow_bf[0:1, :], start=True, stop=True)
                else:
                    e.wait_ge(sv.h, at[f"v_crow{st}_{cyc}"])
                    e.matmul(pf[5][:, 0:FST], onesr_bf[0:1, 0:128],
                             arow_bf[0:1, :], start=True, stop=True)
                    ins = e.matmul(pf[5][:, 256:256 + FST], onesr_bf[0:1, 0:128],
                                   brow_bf[0:1, :], start=True, stop=True)
                at[f"t_cb{st}_{cyc}"] = sq.inc(ins)
            return f

        def v_comb(st, cyc):
            def f(e):
                e.wait_ge(sq.h, at[f"t_cb{st}_{cyc - 1}"])
                e.wait_ge(sv.h, at[f"v_Bev{st}"])
                e.wait_ge(sq.h, at[f"t_h2_{st}_{cyc - 1}_3"])
                ins = None
                for ck in range(8):
                    c0, c1 = ck * FST, (ck + 1) * FST
                    if cyc == 2:
                        e.tensor_tensor(h1_bf[:, c0:c1], A_sb[:, c0:c1],
                                        pf[5][:, 0:FST], OP.mult)
                    ins = e.tensor_tensor(comb_sb[:, c0:c1], B_sb[:, c0:c1],
                                          pf[5][:, 256:256 + FST], OP.mult)
                k = sv.inc(ins)
                e.wait_ge(sv.h, k)
                for ck in range(8):
                    c0, c1 = ck * FST, (ck + 1) * FST
                    other = A_sb if cyc == 1 else h1_bf
                    ins = e.tensor_tensor(comb_sb[:, c0:c1], comb_sb[:, c0:c1],
                                          other[:, c0:c1], OP.add)
                    ins.then_inc(sem_v3.h, 1)
                    sem_v3.n += 1
            return f

        def t_ffnT(st, tt):
            def f(e):
                pb = pbfa if tt == 0 else pbfb
                if st >= 1:
                    e.wait_ge(sem_fin.h, cpst * (st - 1) + tt + 1)
                last = None
                for ck in range(8):
                    last = e.transpose(pb[:, ck * 128:(ck + 1) * 128],
                                       xn_fm[:, ck * rpc + st * FST + tt * 128:
                                             ck * rpc + st * FST + tt * 128 + 128],
                                       id_sb[:, :])
                at[f"t_fT{st}_{tt}"] = sq.inc(last)
            return f

        def g_abT(st):
            def f(e):
                e.wait_ge(sv.h, at[f"v_crow{st}_2"])
                ins = e.dma_start(out=ab_dram[:, :], in_=crows[0:1, 0:3 * FST])
                sem_dab.inc(ins, 16)
                e.wait_ge(sem_dab.h, 112 * st + 16)
                for tt in range(2):
                    for c in range(3):
                        src = bass.AP(ab_dram, c * FST + tt * 128, [[1, 128], [1, 1]])
                        ins = e.dma_start(out=abcol[:, tt * 3 + c:tt * 3 + c + 1], in_=src)
                        sem_dab.inc(ins, 16)
            return f

        def v_fin(st):
            sign = [1, 1, -1, -1, -1, -1, 1, 1]
            def f(e):
                for tt in range(2):
                    gt = st * cpst + tt
                    pb = pbfa if tt == 0 else pbfb
                    e.wait_ge(sq.h, at[f"t_fT{st}_{tt}"])
                    if st >= 1:
                        e.wait_ge(sem_s3.h, s3atev(st, cpst - 1))
                    e.wait_ge(sem_dab.h, 112 * (st + 1))
                    e.wait_ge(sem_dx[gt % 4].h, 16 * (ntiles // 4 + gt // 4 + 1))
                    if gt >= 2:
                        e.wait_ge(sem_dy[gt % 2].h, 16 * (gt // 2))
                    ins = e.scalar_tensor_tensor(y_sb[tt][:, :], pb[:, 0:1024],
                                                 abcol[:, tt * 3:tt * 3 + 1],
                                                 attn2[st % 2][:, tt * D: (tt + 1) * D],
                                                 OP.mult, OP.add)
                    k = sv.inc(ins)
                    e.wait_ge(sv.h, k)
                    for (g0, g1, pos) in ((0, 2, True), (2, 4, False),
                                          (4, 6, False), (6, 8, True)):
                        col = tt * 3 + (1 if pos else 2)
                        ins = e.scalar_tensor_tensor(
                            y_sb[tt][:, g0 * 128:g1 * 128],
                            pb[:, PERM[g0] * 128:(PERM[g0] + g1 - g0) * 128],
                            abcol[:, col:col + 1],
                            y_sb[tt][:, g0 * 128:g1 * 128], OP.mult, OP.add)
                    k = sv.inc(ins)
                    e.wait_ge(sv.h, k)
                    ins = e.tensor_tensor(y_sb[tt][:, :], y_sb[tt][:, :], xin[gt % 4][:, :], OP.add)
                    ins.then_inc(sem_fin.h, 1)
                    sem_fin.n += 1
            return f

        blk.tensor(t_h2(0, 0))
        blk.vector(v_rx)
        blk.tensor(t_h3z4(0, 0))
        blk.tensor(t_gf1)
        blk.tensor(t_gf2)
        blk.tensor(t_gf3)
        blk.vector(v_gf)
        blk.tensor(t_cb_f(0, 0))
        blk.vector(v_comb(0, 1))
        blk.tensor(t_h2(0, 1))
        if nfst > 1:
            blk.tensor(t_A(1, half=0))
        blk.tensor(t_h3z4(0, 1))
        if nfst > 1:
            blk.tensor(t_A(1, half=1))
        for vb in v_r_bodies:
            blk.vector(vb)
        blk.vector(v_crow_f(0, 1))
        blk.tensor(t_cb_f(0, 1))
        blk.vector(v_comb(0, 2))
        blk.tensor(t_h2(0, 2))
        blk.tensor(t_ffnT(0, 0))
        blk.tensor(t_h3z4(0, 2))
        blk.tensor(t_ffnT(0, 1))
        blk.vector(v_crow_f(0, 2))
        blk.gpsimd(g_abT(0))
        emit_attn(0)
        if nfst > 1:
            emit_attn(1)
        blk.vector(v_fin(0))

        for st in range(1, nfst):
            if st >= 1:
                blk.tensor(t_Bpass(st, SIGNP, "p"))
            blk.tensor(t_h2(st, 0))
            if st >= 1:
                blk.tensor(t_Bpass(st, SIGNM, "m", half=0))
            blk.tensor(t_h3z4(st, 0))
            if st >= 1:
                blk.tensor(t_Bpass(st, SIGNM, "m", half=1))
                blk.vector(v_Bmev(st))
            blk.tensor(t_cb_f(st, 0))
            if st + 1 < nfst:
                blk.tensor(t_A(st + 1, half=0))
            blk.vector(v_comb(st, 1))
            blk.tensor(t_h2(st, 1))
            blk.tensor(t_h3z4(st, 1))
            blk.vector(v_crow_f(st, 1))
            blk.tensor(t_cb_f(st, 1))
            if st + 1 < nfst:
                blk.tensor(t_A(st + 1, half=1))
            blk.vector(v_comb(st, 2))
            blk.tensor(t_h2(st, 2))
            blk.tensor(t_ffnT(st, 0))
            blk.tensor(t_h3z4(st, 2))
            blk.tensor(t_ffnT(st, 1))
            if st + 1 < nfst:
                emit_attn(st + 1)
            blk.vector(v_crow_f(st, 2))
            blk.gpsimd(g_abT(st))
            blk.vector(v_fin(st))



        def s3_gelu1_c0(e, st):
            if st >= 1:
                e.wait_ge(sem_v3.h, ((st - 1) * 2 + 2) * 8)
                e.wait_ge(sq.h, at[f"t_h2_{st - 1}_2_3"])
            else:
                e.wait_ge(sv.h, at["v_Aev0"])
            for mt in range(8):
                ins = e.activation(h1_bf[:, mt * FST:(mt + 1) * FST],
                                   A_sb[:, mt * FST:(mt + 1) * FST],
                                   AF.Gelu, bias=b1_sb[:, mt:mt + 1])
                ins.then_inc(sem_s3.h, 1)
                sem_s3.n += 1

        def scalar_mega3(e):
            preload(e, SET_GELU)
            e.wait_ge(sem_dwb.h, sem_dwb.n)
            for name, cnt in s3sched:
                p = name.split("_")
                base = sem_s3.n
                if name == "gf1":
                    e.wait_ge(sq.h, at["t_gf1"])
                    last = None
                    for mt in range(4):
                        last = e.activation(gg1_sb[:, mt:mt + 1], pf[4][:, mt:mt + 1], AF.Gelu,
                                            bias=gb1_sb[:, mt:mt + 1], scale=1.0 / float(seq_t))
                    last.then_inc(sem_s3.h, 1)
                    sem_s3.n += 1
                elif name == "gf2":
                    e.wait_ge(sq.h, at["t_gf2"])
                    last = None
                    for mt in range(2):
                        last = e.activation(gg2_sb[:, mt:mt + 1], pf[5][:, mt:mt + 1], AF.Gelu,
                                            bias=gb2_sb[:, mt:mt + 1])
                    last.then_inc(sem_s3.h, 1)
                    sem_s3.n += 1
                elif name == "gf3":
                    e.wait_ge(sq.h, at["t_gf3"])
                    ins = e.activation(sclr[0:1, 8:9], pf[4][0:1, 16:17], AF.Tanh,
                                       bias=sclr[0:1, 5:6], scale=0.5)
                    ins.then_inc(sem_s3.h, 1)
                    sem_s3.n += 1
                elif p[0] == "aev":
                    stp = int(p[1])
                    e.wait_ge(sq.h, at[f"t_h1_{stp}_0_7"])
                    for b in range(4):
                        ins = e.activation(A_sb[:, b * 512:(b + 1) * 512],
                                           pf[b][:, 0:512], AF.Copy)
                        ins.then_inc(sem_s3.h, 1)
                        sem_s3.n += 1
                elif p[0] == "atev":
                    stp, tt = int(p[1]), int(p[2])
                    jj = stp * cpst + tt
                    e.wait_ge(sq.h, at[f"t_o{jj}"])
                    ab = attn2[stp % 2]
                    for half, bank in ((0, pf[2]), (1, pf[3])):
                        ins = e.activation(ab[:, tt * D + half * 512: tt * D + half * 512 + 512],
                                           bank[:, 0:512], AF.Copy)
                        ins.then_inc(sem_s3.h, 1)
                        sem_s3.n += 1
                elif p[0] == "bpev":
                    stp = int(p[1])
                    e.wait_ge(sq.h, at[f"t_Bp_{stp}_7"])
                    for b in range(4):
                        ins = e.activation(comb_sb[:, b * 512:(b + 1) * 512],
                                           pf[b][:, 0:512], AF.Copy)
                        ins.then_inc(sem_s3.h, 1)
                        sem_s3.n += 1
                elif p[0] == "g1":
                    st, cyc = int(p[1]), int(p[2])
                    if cyc == 0:
                        s3_gelu1_c0(e, st)
                    else:
                        e.wait_ge(sq.h, at[f"t_h2_{st}_{cyc - 1}_3"])
                        v3base = (st * 2 + cyc - 1) * 8
                        for mt in range(8):
                            e.wait_ge(sem_v3.h, v3base + mt + 1)
                            ins = e.activation(h1_bf[:, mt * FST:(mt + 1) * FST],
                                               comb_sb[:, mt * FST:(mt + 1) * FST],
                                               AF.Gelu, bias=b1_sb[:, mt:mt + 1])
                            ins.then_inc(sem_s3.h, 1)
                            sem_s3.n += 1
                elif p[0] == "g2":
                    st, cyc = int(p[1]), int(p[2])
                    for mt in range(4):
                        e.wait_ge(sq.h, at[f"t_h2_{st}_{cyc}_{(mt // 2) * 2 + 1}"])
                        ins = e.activation(h2_bf[:, mt * FST:(mt + 1) * FST],
                                           pf[4 + mt // 2][:, (mt % 2) * 256:(mt % 2) * 256 + 256],
                                           AF.Gelu, bias=b2_sb[:, mt:mt + 1])
                        ins.then_inc(sem_s3.h, 1)
                        sem_s3.n += 1
                elif p[0] == "g3":
                    st, cyc = int(p[1]), int(p[2])
                    for mt in range(2):
                        e.wait_ge(sq.h, at[f"t_h3_{st}_{cyc}_1"])
                        ins = e.activation(h3_bf[:, mt * FST:(mt + 1) * FST],
                                           pf[4][:, mt * 256: mt * 256 + 256],
                                           AF.Gelu, bias=b3_sb[:, mt:mt + 1])
                        ins.then_inc(sem_s3.h, 1)
                        sem_s3.n += 1
                elif p[0] == "crow":
                    st = int(p[1])
                    if st == 0:
                        e.wait_ge(sv.h, at["v_A3"])
                    e.wait_ge(sem_s3.h, s3th4(st, 0))
                    ins = e.activation(brow_bf[0:1, 0:FST], th4_row[0:1, :], AF.Identity,
                                       bias=sclr[0:1, 9:10], scale=sclr[0:1, 6:7])
                    ins.then_inc(sem_s3.h, 1)
                    sem_s3.n += 1
                elif p[0] == "th4":
                    st, cyc = int(p[1]), int(p[2])
                    if cyc == 2:
                        e.wait_ge(sv.h, at[f"v_crow{st}_1"])
                    elif cyc == 0 and st >= 1:
                        e.wait_ge(sv.h, at[f"v_crow{st - 1}_2"])
                    e.wait_ge(sq.h, at[f"t_z4_{st}_{cyc}"])
                    ins = e.activation(th4_row[0:1, :], pf[5][0:1, 0:FST], AF.Tanh,
                                       bias=sclr[0:1, 4:5], scale=0.5)
                    ins.then_inc(sem_s3.h, 1)
                    sem_s3.n += 1
                else:
                    raise AssertionError(name)
                assert sem_s3.n == base + cnt == s3pos[name] + cnt, name

        blk.scalar(scalar_mega3)

    return nc


def _consts(rpc):
    nch = rpc // L
    p = np.arange(128, dtype=np.float32)
    f = np.arange(128, dtype=np.float32)
    E1 = np.maximum(p[:, None] - f[None, :] - 1.0, 0.0).astype(np.float32)
    M1 = (p[:, None] > f[None, :]).astype(np.float32)
    EQ = np.ascontiguousarray(np.broadcast_to((127.0 - f)[None, :], (128, 128))).astype(np.float32)
    CK = np.stack([p, np.full(128, 128.0, np.float32)], axis=1).astype(np.float32)
    return dict(
        cpack=np.concatenate([E1, M1, EQ, CK], axis=1).astype(np.float32),
        ident=np.eye(128, dtype=np.float32),
        onesr=np.ones((1, 128), dtype=np.float32),
    )


def _core_consts(rpc, r):
    nch = rpc // L
    rr = np.arange(GRP, dtype=np.float32)
    EJ = (np.float32(L) * (nch - 1 - np.arange(nch, dtype=np.float32))).astype(np.float32)
    E4 = np.maximum(np.float32(rpc) * (rr - r - 1.0), 0.0).astype(np.float32)
    M4 = (rr > r).astype(np.float32)
    rpack = np.zeros(nch + 2 * GRP + 128, np.float32)
    rpack[0:nch] = EJ
    rpack[nch:nch + GRP] = E4
    rpack[nch + GRP:nch + 2 * GRP] = M4
    return dict(rpack=rpack[None, :])


def make_in_maps(inputs, rpc=RPC):
    x = np.ascontiguousarray(inputs["L0_attn_x"], dtype=np.float32)
    f32 = lambda k: np.ascontiguousarray(inputs[k], np.float32)
    s11 = lambda k: np.asarray(inputs[k], np.float32).reshape(1, 1)
    shared = {
        "basis": f32("L0_attn_basis"),
        "w1": f32("L0_ffn_cn_w1"), "b1": f32("L0_ffn_cn_b1"),
        "w2": f32("L0_ffn_cn_w2"), "b2": f32("L0_ffn_cn_b2"),
        "w3": f32("L0_ffn_cn_w3"), "b3": f32("L0_ffn_cn_b3"),
        "w4": f32("L0_ffn_cn_w4"),
        "gw1": f32("L0_ffn_gc_w1"), "gb1": f32("L0_ffn_gc_b1"),
        "gw2": f32("L0_ffn_gc_w2"), "gb2": f32("L0_ffn_gc_b2"),
        "gw3": f32("L0_ffn_gc_w3"),
        "spack": np.concatenate([
            s11("L0_ffn_cn_b4"), s11("L0_ffn_gc_b3"), s11("L0_ffn_adaptive_range"),
            s11("L0_attn_decay_logit"), s11("L0_attn_out_scale")], axis=1),
        "cfs": np.ascontiguousarray(np.stack([
            f32("L0_attn_q_coeffs"), f32("L0_attn_k_coeffs"),
            f32("L0_attn_v_coeffs"), f32("L0_attn_o_coeffs")], axis=0)),
    }
    shared.update(_consts(rpc))
    in_maps = []
    for core in range(NCORES):
        b, r = core // GRP, core % GRP
        m = dict(shared)
        m["x"] = np.ascontiguousarray(x[b, r * rpc:(r + 1) * rpc, :])
        m.update(_core_consts(rpc, r))
        in_maps.append(m)
    return in_maps


def kernel(**inputs):
    in_maps = make_in_maps(inputs)
    nc = build_nc()
    res = run_bass_kernel_spmd(nc, in_maps, core_ids=list(range(NCORES)))
    y = np.empty((B, T, D), dtype=np.float32)
    for core in range(NCORES):
        b, r = core // GRP, core % GRP
        y[b, r * RPC:(r + 1) * RPC, :] = res.results[core]["y"]
    return y


# version 37
# speedup vs baseline: 1.0263x; 1.0183x over previous
import os
import numpy as np
import concourse.bass as bass
import concourse.mybir as mybir
from concourse.bass_utils import run_bass_kernel_spmd
from contextlib import ExitStack

F32 = mybir.dt.float32
BF16 = mybir.dt.bfloat16
AF = mybir.ActivationFunctionType
OP = mybir.AluOpType

B, T, D = 2, 8192, 1024
NCORES = 8
GRP = 4
RPC = (B * T) // NCORES
L = 128
PST = 512
FST = 256
C = 256
EARLY_A = int(os.environ.get("EARLY_A", "1"))
CHASE = int(os.environ.get("CHASE", "1"))


class Sem:
    def __init__(self, ctx, nc, name):
        self.h = ctx.enter_context(nc.semaphore(name))
        self.n = 0

    def inc(self, instr, k=1):
        instr.then_inc(self.h, k)
        self.n += k
        return self.n


def build_nc(rpc=RPC, debug=False):
    nch = rpc // L
    ntiles = rpc // 128
    npst = rpc // PST
    nfst = rpc // FST
    cpst = FST // L
    seq_t = GRP * rpc

    nc = bass.Bass("TRN2", target_bir_lowering=False, debug=debug)

    dx = nc.dram_tensor("x", [rpc, D], F32, kind="ExternalInput")
    dy = nc.dram_tensor("y", [rpc, D], F32, kind="ExternalOutput")
    din = {}
    for nm, shape in (("basis", [D, C]), ("w1", [D, D]), ("b1", [D]), ("w2", [D, 512]),
                      ("b2", [512]), ("w3", [512, 256]), ("b3", [256]), ("w4", [256, 1]),
                      ("spack", [1, 5]), ("gw1", [D, 512]), ("gb1", [512]), ("gw2", [512, 256]),
                      ("gb2", [256]), ("gw3", [256, 1]), ("ident", [128, 128]),
                      ("cpack", [128, 386]), ("rpack", [1, nch + 2 * GRP + 128]),
                      ("cfs", [4, 256, 256]), ("onesr", [1, 128])):
        din[nm] = nc.dram_tensor(nm, shape, F32, kind="ExternalInput")

    CCN = C * C
    cc_in = nc.dram_tensor("cc_in", [1, CCN], BF16)
    cc_out = nc.dram_tensor("cc_out", [GRP, CCN], BF16)
    ccx_in = nc.dram_tensor("ccx_in", [1, D], BF16)
    ccx_out = nc.dram_tensor("ccx_out", [GRP, D], BF16)
    bstage = nc.dram_tensor("bstage", [1, 32], F32)
    ab_dram = nc.dram_tensor("ab_dram", [1, 768], F32)

    ctx = ExitStack()
    with ctx:
        sb = lambda name, shape, dt: ctx.enter_context(nc.sbuf_tensor(name, shape, dt))
        ps = lambda name, shape, dt: ctx.enter_context(nc.psum_tensor(name, shape, dt))

        class View:
            def __init__(self, t, base, width):
                self.t, self.base, self.w = t, base, width

            def __getitem__(self, idx):
                p, c = idx
                c0 = 0 if c.start is None else c.start
                c1 = self.w if c.stop is None else c.stop
                return self.t[p, self.base + c0: self.base + c1]


        xn_fm = sb("xn_fm", [128, 8 * rpc], BF16)
        q_fm = sb("q_fm", [128, 2 * rpc], BF16)
        k_fm = sb("k_fm", [128, 2 * rpc], BF16)
        kd_tm = sb("kd_tm", [128, nch * C], BF16)
        v_tm = sb("v_tm", [128, nch * C], BF16)
        t_st = sb("t_st", [128, nch * 512], BF16)
        w1_sb = sb("w1_sb", [128, 8 * D], BF16)
        w2_sb = sb("w2_sb", [128, 8 * 512], BF16)
        w3_sb = sb("w3_sb", [128, 4 * 256], BF16)
        w4_sb = sb("w4_sb", [128, 2], BF16)
        g1_sb = sb("g1_sb", [128, 8 * 512], BF16)
        g2_sb = sb("g2_sb", [128, 4 * 256], BF16)
        g3_sb = sb("g3_sb", [128, 2], BF16)
        owt_sb = sb("owt_sb", [128, 2 * D], BF16)
        b1_sb = sb("b1_sb", [128, 8], F32)
        b2_sb = sb("b2_sb", [128, 4], F32)
        b3_sb = sb("b3_sb", [128, 2], F32)
        gb1_sb = sb("gb1_sb", [128, 4], F32)
        gb2_sb = sb("gb2_sb", [128, 2], F32)
        sc_in = sb("sc_in", [1, 8], F32)
        id_sb = sb("id_sb", [128, 128], BF16)
        rpack_sb = sb("rpack_sb", [1, nch + 2 * GRP + 128], F32)
        EJ_sb = View(rpack_sb, 0, nch)
        E4_sb = View(rpack_sb, nch, GRP)
        M4_sb = View(rpack_sb, nch + GRP, GRP)
        onesr_bf = sb("onesr_bf", [1, 128], BF16)
        onesc_bf = sb("onesc_bf", [128, 1], BF16)
        DT_sb = sb("DT_sb", [128, 128], BF16)
        DQ_sb = sb("DQ_sb", [128, 128], BF16)
        dk_col = sb("dk_col", [128, 1], F32)
        dL_col = sb("dL_col", [128, 1], F32)
        bcast = sb("bcast", [128, 32], F32)
        row_sb = sb("row_sb", [1, 32], F32)
        sclr = sb("sclr", [1, 16], F32)
        c1_col = sb("c1_col", [128, 1], F32)
        ceps_col = sb("ceps_col", [128, 1], F32)
        xnsum_sb = sb("xnsum_sb", [128, 8], BF16)
        xnt_sb = sb("xnt_sb", [128, 8], F32)
        xnt_bf = sb("xnt_bf", [128, 8], BF16)
        sin_sb = sb("sin_sb", [128, 512], F32)
        gath_sb = sb("gath_sb", [128, 512], BF16)
        xg_sb = [sb(f"xg_sb{r}", [128, 8], BF16) for r in range(GRP)]
        gg1_sb = sb("gg1_sb", [128, 4], BF16)
        gg2_sb = sb("gg2_sb", [128, 2], BF16)
        dsrow_sb = sb("dsrow_sb", [1, nch], F32)
        wrrow_sb = sb("wrrow_sb", [1, GRP], F32)
        xin = [sb(f"xin{i}", [128, D], F32) for i in range(4)]
        stats = [sb(f"stats{i}", [128, 12], F32) for i in range(4)]
        mv = [sb(f"mv{i}", [128, 2], F32) for i in range(4)]
        rcol = [sb(f"rcol{i}", [128, 1], F32) for i in range(4)]
        mrcol = [sb(f"mrcol{i}", [128, 1], F32) for i in range(2)]
        xn_tm = [sb(f"xn_tm{i}", [128, D], BF16) for i in range(2)]
        sj_bf = [sb(f"sj_bf{i}", [128, 512], BF16) for i in range(2)]
        qd_tmp = [sb(f"qd_tmp{i}", [128, 256], BF16) for i in range(2)]
        sco_sb = [sb(f"sco_sb{i}", [128, 128], BF16) for i in range(2)]
        retr_sb = [sb(f"retr_sb{i}", [128, C], BF16) for i in range(2)]
        attn_sb = sb("attn_sb", [128, 2 * D], BF16)
        attn_sb_b = sb("attn_sb_b", [128, 2 * D], BF16)
        attn2 = [attn_sb, attn_sb_b]
        A_sb = sb("A_sb", [128, 8 * FST], BF16)
        B_sb = sb("B_sb", [128, 8 * FST], BF16)
        comb_sb = sb("comb_sb", [128, 8 * FST], BF16)
        h1_bf = sb("h1_bf", [128, 8 * FST], BF16)
        h2_bf = sb("h2_bf", [128, 4 * FST], BF16)
        h3_bf = sb("h3_bf", [128, 2 * FST], BF16)
        th4_row = sb("th4_row", [1, FST], F32)
        crows = sb("crows", [1, 3 * FST], F32)
        crows3 = sb("crows3", [1, 3 * FST], F32)
        abrow = sb("abrow", [1, 2 * FST], F32)
        arow_bf = sb("arow_bf", [1, FST], BF16)
        brow_bf = sb("brow_bf", [1, FST], BF16)
        c2_bf = sb("c2_bf", [1, FST], BF16)
        abcol = sb("abcol", [128, 6], F32)
        y_sb = [sb(f"y_sb{i}", [128, D], F32) for i in range(2)]
        E1_sb = View(y_sb[0], 0, 128)
        M1_sb = View(y_sb[0], 128, 128)
        EQ_sb = View(y_sb[0], 256, 128)
        CK_sb = View(y_sb[0], 384, 2)
        cpack_sb = View(y_sb[0], 0, 386)
        basis_bf = View(A_sb, 0, 8 * C)
        qw_sb = View(B_sb, 0, 8 * C)
        kw_sb = View(h1_bf, 0, 8 * C)
        vw_sb = View(attn_sb_b, 0, 8 * C)
        basT = View(comb_sb, 0, 2 * D)
        qc_bf = View(h1_bf, 0, 2 * C)
        kc_bf = View(h1_bf, 512, 2 * C)
        vc_bf = View(h1_bf, 1024, 2 * C)
        oc_bf = View(h1_bf, 1536, 2 * C)
        qcT = View(attn_sb, 0, 2 * C)
        kcT = View(attn_sb, 512, 2 * C)
        vcT = View(attn_sb, 1024, 2 * C)
        ocT = View(attn_sb, 1536, 2 * C)

        pbfa = ps("pbfa", [128, 1024], BF16)
        pbfb = ps("pbfb", [128, 1024], BF16)
        pf = [ps(f"pf{i}", [128, 512], F32) for i in range(6)]

        sem_dx = [Sem(ctx, nc, f"dx{i}") for i in range(4)]
        sem_dwa = Sem(ctx, nc, "dwa")
        sem_dw0 = Sem(ctx, nc, "dw0")
        sem_dw00 = Sem(ctx, nc, "dw00")
        sem_dwb = Sem(ctx, nc, "dwb")
        sem_dbc = Sem(ctx, nc, "dbc")
        sem_dcc = Sem(ctx, nc, "dcc")
        sem_dy = [Sem(ctx, nc, "dy0"), Sem(ctx, nc, "dy1")]
        sem_dab = Sem(ctx, nc, "dab")
        sem_vr = Sem(ctx, nc, "vr")
        sem_dxg = Sem(ctx, nc, "dxg")
        sem_ttr = Sem(ctx, nc, "ttr")
        sem_tpj = Sem(ctx, nc, "tpj")
        sem_tv = Sem(ctx, nc, "tv")
        sem_tkt = Sem(ctx, nc, "tkt")
        sem_cc = Sem(ctx, nc, "scc")
        sem_ccx = Sem(ctx, nc, "sccx")
        sem_nrm = Sem(ctx, nc, "snrm")
        sem_fin = Sem(ctx, nc, "sfin")
        sv = Sem(ctx, nc, "sv")
        ss = Sem(ctx, nc, "ss")
        sq = Sem(ctx, nc, "sq")
        sem_vsu = Sem(ctx, nc, "svsu")
        sem_vbn = Sem(ctx, nc, "svbn")
        sem_vmr = Sem(ctx, nc, "svmr")
        sem_txs = Sem(ctx, nc, "stxs")
        sem_s3 = Sem(ctx, nc, "ss3")
        sem_v3 = Sem(ctx, nc, "sv3")

        from concourse.hw_specs import get_activation_tables
        _tabs = list(get_activation_tables(nc.m.arch))
        SET_LNEXP = _tabs.index("natural_log_exp_and_others")
        SET_GELU = _tabs.index("gelu_and_others")
        SET_RSQRT = _tabs.index("reciprocal_sqrt_and_small")

        def preload(e, set_idx):
            e.add_instruction(mybir.InstLoadActFuncSet(
                name=nc.get_next_instruction_name(), act_func_set_id=set_idx,
                ins=[], outs=[]))

        at = {}
        nsp_ap = lambda: bcast[:, 0:1]
        osc_ap = lambda: bcast[:, 1:2]
        dS_ap = lambda j: bcast[:, 2 + j:3 + j]
        wr_ap = lambda r: bcast[:, 2 + nch + r:3 + nch + r]
        blk = ctx.enter_context(nc.Block())

        def sync_all(e):
            for i in range(ntiles):
                if i >= 4:
                    e.wait_ge(sem_nrm.h, i - 3)
                e.dma_start(out=xin[i % 4][:, :], in_=dx[i * 128:(i + 1) * 128, :]).then_inc(sem_dx[i % 4].h, 16)
                sem_dx[i % 4].n += 16
            for gt in range(ntiles):
                if gt < 4:
                    e.wait_ge(sem_nrm.h, ntiles)
                else:
                    e.wait_ge(sem_fin.h, gt - 3)
                e.dma_start(out=xin[gt % 4][:, :], in_=dx[gt * 128:(gt + 1) * 128, :]).then_inc(sem_dx[gt % 4].h, 16)
                sem_dx[gt % 4].n += 16
                k = gt - 4
                if k >= 0:
                    e.wait_ge(sem_fin.h, k + 1)
                    ins = e.dma_start(out=dy[k * 128:(k + 1) * 128, :], in_=y_sb[k % 2][:, :])
                    sem_dy[k % 2].inc(ins, 16)
            for k in range(ntiles - 4, ntiles):
                e.wait_ge(sem_fin.h, k + 1)
                ins = e.dma_start(out=dy[k * 128:(k + 1) * 128, :], in_=y_sb[k % 2][:, :])
                sem_dy[k % 2].inc(ins, 16)
            e.wait_ge(sem_dy[0].h, sem_dy[0].n)
            e.wait_ge(sem_dy[1].h, sem_dy[1].n)

        blk.sync(sync_all)

        dw_at = {}

        def g_loads(e):
            items = [
                (id_sb[:, :], din["ident"][:, :], "id"),
                (onesr_bf[:, :], din["onesr"][:, :], "onesr"),
                (onesc_bf[:, :], din["onesr"].ap().rearrange("a (p b) -> (a p) b", b=1), "onesc"),
                (sc_in[0:1, 0:5], din["spack"][:, :], "spack"),
                (cpack_sb[:, :], din["cpack"][:, :], "cpack"),
                (rpack_sb[0:1, :], din["rpack"][:, :], "rpack"),
                (basis_bf[:, :], din["basis"].ap().rearrange("(c p) n -> p c n", p=128), "basis"),
                (h1_bf[:, 0:2048], din["cfs"].ap().rearrange("w (c p) n -> p w c n", p=128), "cfs"),
                (b1_sb[:, :], din["b1"].ap().rearrange("(c p) -> p c", p=128), "b1"),
                (b2_sb[:, :], din["b2"].ap().rearrange("(c p) -> p c", p=128), "b2"),
                (b3_sb[:, :], din["b3"].ap().rearrange("(c p) -> p c", p=128), "b3"),
                (gb1_sb[:, :], din["gb1"].ap().rearrange("(c p) -> p c", p=128), "gb1"),
                (gb2_sb[:, :], din["gb2"].ap().rearrange("(c p) -> p c", p=128), "gb2"),
                (w1_sb[:, :], din["w1"].ap().rearrange("(c p) n -> p c n", p=128), "w1"),
                (w2_sb[:, :], din["w2"].ap().rearrange("(c p) n -> p c n", p=128), "w2"),
                (w3_sb[:, :], din["w3"].ap().rearrange("(c p) n -> p c n", p=128), "w3"),
                (w4_sb[:, :], din["w4"].ap().rearrange("(c p) n -> p (c n)", p=128), "w4"),
                (g1_sb[:, :], din["gw1"].ap().rearrange("(c p) n -> p c n", p=128), "g1"),
                (g2_sb[:, :], din["gw2"].ap().rearrange("(c p) n -> p c n", p=128), "g2"),
                (g3_sb[:, :], din["gw3"].ap().rearrange("(c p) n -> p (c n)", p=128), "g3"),
            ]
            slow_ok = {"b1", "b2", "b3", "gb1", "gb2", "onesc", "w4", "g3"}
            grp_b = {"w1", "w2", "w3", "w4", "g1", "g2", "g3",
                     "b1", "b2", "b3", "gb1", "gb2"}
            grp_00 = {"id", "basis", "cfs"}
            grp_0 = {"onesr", "onesc", "spack", "cpack", "rpack"}
            items.sort(key=lambda it: 0 if it[2] in grp_00 else
                       (1 if it[2] in grp_0 else (3 if it[2] in grp_b else 2)))
            for out_ap, in_ap, key in items:
                ins = e.dma_start(out=out_ap, in_=in_ap,
                                  allow_slow_non_contiguous=(key in slow_ok))
                sem = (sem_dw00 if key in grp_00 else
                       (sem_dw0 if key in grp_0 else (sem_dwb if key in grp_b else sem_dwa)))
                dw_at[key] = sem.inc(ins, 16)

        blk.gpsimd(g_loads)

        def s_proj_ev(e, st):
            e.wait_ge(sem_tpj.h, st + 1)
            for nm, banks, dst in (("q", (pf[1], pf[2]), q_fm), ("k", (pf[3], pf[4]), k_fm)):
                for mt in range(2):
                    ins = e.activation(dst[:, mt * rpc + st * PST: mt * rpc + st * PST + PST],
                                       banks[mt][:, 0:512], AF.Copy)
                    at[f"s_{nm}ev{st}_{mt}"] = ss.inc(ins)
            for tt in range(4):
                gt = 4 * st + tt
                e.wait_ge(sem_tv.h, gt + 1)
                ins = e.activation(v_tm[:, gt * C: gt * C + C], pf[5][:, 0:C], AF.Copy)
                at[f"s_vev{gt}"] = ss.inc(ins)
                e.wait_ge(sem_tkt.h, gt + 1)
                ins = e.activation(kd_tm[:, gt * C: gt * C + C], pbfb[:, 0:C],
                                   AF.Identity, scale=dk_col[:, 0:1])
                at[f"s_kdev{gt}"] = ss.inc(ins)

        def scalar_mega1(e):
            preload(e, SET_LNEXP)
            e.wait_ge(sem_vsu.h, 2)
            ins = e.activation(sclr[0:1, 1:2], sclr[0:1, 0:1], AF.Exp)
            k = ss.inc(ins)
            e.wait_ge(ss.h, k)
            ins = e.activation(sclr[0:1, 2:3], sclr[0:1, 1:2], AF.Ln, bias=c1_col[0:1, 0:1])
            at["s_sp"] = ss.inc(ins)
            e.wait_ge(sem_vsu.h, 3)
            ins = e.activation(dsrow_sb[0:1, :], EJ_sb[0:1, :], AF.Exp, scale=sclr[0:1, 3:4])
            k = ss.inc(ins)
            ins = e.activation(wrrow_sb[0:1, :], E4_sb[0:1, :], AF.Exp, scale=sclr[0:1, 3:4])
            at["s_rows"] = ss.inc(ins)
            e.wait_ge(sem_dbc.h, 32)
            ins = e.activation(gath_sb[:, 0:128], E1_sb[:, :], AF.Exp, scale=nsp_ap())
            at["s_dtraw"] = ss.inc(ins)
            ins = e.activation(DQ_sb[:, :], EQ_sb[:, :], AF.Exp, scale=nsp_ap())
            at["s_dq"] = ss.inc(ins)
            e.activation(dk_col[:, 0:1], CK_sb[:, 0:1], AF.Exp, scale=nsp_ap())
            ins = e.activation(dL_col[:, 0:1], CK_sb[:, 1:2], AF.Exp, scale=nsp_ap())
            at["s_dk"] = ss.inc(ins)
            for i in range(ntiles):
                e.wait_ge(sem_vbn.h, i + 1)
                ins = e.activation(rcol[i % 4][:, 0:1], mv[i % 4][:, 1:2], AF.Ln, bias=ceps_col[:, 0:1])
                k = ss.inc(ins)
                e.wait_ge(ss.h, k)
                ins = e.activation(rcol[i % 4][:, 0:1], rcol[i % 4][:, 0:1], AF.Exp, scale=-0.5)
                at[f"s_r{i}"] = ss.inc(ins)
                if i >= 1:
                    e.wait_ge(sem_ttr.h, i)
                    src = bass.AP(pbfa, 0, [[1024, 128], [128, 8], [1, 128]])
                    dst = bass.AP(xn_fm, (i - 1) * 128, [[8 * rpc, 128], [rpc, 8], [1, 128]])
                    ins = e.activation(dst, src, AF.Copy)
                    at[f"s_xnev{i - 1}"] = ss.inc(ins)
                    if (i - 1) % 4 == 3:
                        s_proj_ev(e, (i - 1) // 4)
            e.wait_ge(sem_ttr.h, ntiles)
            src = bass.AP(pbfa, 0, [[1024, 128], [128, 8], [1, 128]])
            dst = bass.AP(xn_fm, (ntiles - 1) * 128, [[8 * rpc, 128], [rpc, 8], [1, 128]])
            ins = e.activation(dst, src, AF.Copy)
            at[f"s_xnev{ntiles - 1}"] = ss.inc(ins)
            s_proj_ev(e, (ntiles - 1) // 4)

        blk.scalar(scalar_mega1)

        def v_setup1(e):
            e.memset(row_sb[0:1, :], 0.0)
            e.memset(c1_col[:, :], 1.0)
            e.memset(ceps_col[:, :], 1e-5)
            e.wait_ge(sem_dw0.h, sem_dw0.n)
            ins = e.tensor_scalar(sclr[0:1, 0:1], sc_in[0:1, 3:4], -1.0, 80.0, OP.mult, OP.min)
            ins.then_inc(sem_vsu.h, 2)
            sem_vsu.n += 2

        blk.vector(v_setup1)

        def v_setup2(e):
            e.wait_ge(ss.h, at["s_sp"])
            ins = e.tensor_scalar(sclr[0:1, 3:4], sclr[0:1, 2:3], -1.0, None, OP.mult)
            ins.then_inc(sem_vsu.h, 1)
            sem_vsu.n += 1
            e.tensor_scalar(sclr[0:1, 4:5], sc_in[0:1, 0:1], 0.5, None, OP.mult)
            e.tensor_scalar(sclr[0:1, 5:6], sc_in[0:1, 1:2], 0.5, None, OP.mult)
            e.tensor_scalar(sclr[0:1, 6:7], sc_in[0:1, 2:3], 0.3, None, OP.mult)
            ins = e.tensor_scalar(sclr[0:1, 7:8], sc_in[0:1, 2:3], 0.7, None, OP.mult)
            at["v_scl"] = sv.inc(ins)

        blk.vector(v_setup2)

        def v_setup3(e):
            e.wait_ge(ss.h, at["s_rows"])
            ins = e.tensor_tensor(wrrow_sb[0:1, :], wrrow_sb[0:1, :], M4_sb[0:1, :], OP.mult)
            k = sv.inc(ins)
            e.wait_ge(sv.h, k)
            e.tensor_copy(row_sb[0:1, 0:1], sclr[0:1, 3:4])
            e.tensor_copy(row_sb[0:1, 1:2], sc_in[0:1, 4:5])
            e.tensor_copy(row_sb[0:1, 2:2 + nch], dsrow_sb[0:1, :])
            ins = e.tensor_copy(row_sb[0:1, 2 + nch:2 + nch + GRP], wrrow_sb[0:1, :])
            at["v_row"] = sv.inc(ins)

        blk.vector(v_setup3)

        def g_bcast(e):
            e.wait_ge(sv.h, at["v_row"])
            ins = e.dma_start(out=bstage[:, :], in_=row_sb[0:1, :])
            k = sem_dbc.inc(ins, 16)
            e.wait_ge(sem_dbc.h, k)
            ins = e.dma_start(out=bcast[:, :], in_=bass.AP(bstage, 0, [[0, 128], [1, 32]]))
            dw_at["bcast"] = sem_dbc.inc(ins, 16)

        blk.gpsimd(g_bcast)

        def v_decay(e):
            e.wait_ge(ss.h, at["s_dtraw"])
            ins = e.tensor_tensor(DT_sb[:, :], gath_sb[:, 0:128], M1_sb[:, :], OP.mult)
            at["v_dt"] = sv.inc(ins)

        blk.vector(v_decay)

        trn = {"n": 0}

        def emit_tr_batch(specs):
            i = trn["n"]
            trn["n"] += 1
            dst_sb, dst_col = specs[0][0], specs[0][1]
            nb = len(specs)

            def t_f(e):
                if i == 0:
                    e.wait_ge(sem_dw00.h, sem_dw00.n)
                else:
                    e.wait_ge(sv.h, at[f"v_tr{i - 1}"])
                last = None
                for bi, (_, _, src_sb, src_col) in enumerate(specs):
                    last = e.transpose(pbfb[:, bi * 128:(bi + 1) * 128],
                                       src_sb[:, src_col:src_col + 128], id_sb[:, :])
                at[f"t_tr{i}"] = sq.inc(last)

            def v_f(e):
                e.wait_ge(sq.h, at[f"t_tr{i}"])
                ins = e.tensor_copy(dst_sb[:, dst_col:dst_col + nb * 128], pbfb[:, 0:nb * 128])
                at[f"v_tr{i}"] = sv.inc(ins)

            blk.tensor(t_f)
            blk.vector(v_f)

        for jck in range(2):
            for vh in range(2):
                emit_tr_batch([(basT, jck * D + (vh * 4) * 128, basis_bf, vck * C + jck * 128)
                               for vck in range(vh * 4, vh * 4 + 4)])
        for w_bf, w_T in ((qc_bf, qcT), (kc_bf, kcT), (vc_bf, vcT), (oc_bf, ocT)):
            emit_tr_batch([(w_T, 0, w_bf, cck * C + jck * 128)
                           for jck in range(2) for cck in range(2)])
        n_tr_total = trn["n"]

        prev_w = {"q": None, "k": "q", "v": "k"}

        def emit_w_pair(wT, dst, nm):
            for mv_ in range(8):
                bank = pf[4] if mv_ % 2 == 0 else pf[5]

                def t_f(e, mv_=mv_, wT=wT, bank=bank, nm=nm):
                    if mv_ == 0 and nm == "q":
                        e.wait_ge(sv.h, at[f"v_tr{n_tr_total - 1}"])
                    if mv_ < 2 and prev_w[nm] is not None:
                        e.wait_ge(sv.h, at[f"v_w{prev_w[nm]}{mv_ + 6}"])
                    prev = at.get(f"v_w{nm}{mv_ - 2}")
                    if prev is not None:
                        e.wait_ge(sv.h, prev)
                    last = None
                    for kj in range(2):
                        last = e.matmul(bank[:, 0:C],
                                        basT[:, kj * D + mv_ * 128: kj * D + mv_ * 128 + 128],
                                        wT[:, kj * C: kj * C + C],
                                        start=(kj == 0), stop=(kj == 1))
                    at[f"t_w{nm}{mv_}"] = sq.inc(last)

                def v_f(e, mv_=mv_, dst=dst, bank=bank, nm=nm):
                    e.wait_ge(sq.h, at[f"t_w{nm}{mv_}"])
                    ins = e.tensor_copy(dst[:, mv_ * C: mv_ * C + C], bank[:, 0:C])
                    at[f"v_w{nm}{mv_}"] = sv.inc(ins)

                blk.tensor(t_f)
                blk.vector(v_f)

        emit_w_pair(qcT, qw_sb, "q")
        emit_w_pair(kcT, kw_sb, "k")
        emit_w_pair(vcT, vw_sb, "v")

        for mc in range(2):
            for nh in range(2):
                bank = pf[4] if nh == 0 else pf[5]
                prev_key = (f"v_ow{mc}{nh - 1}" if nh == 1 else (f"v_ow{mc - 1}{1}" if mc > 0 else "v_wv7"))

                def t_f(e, mc=mc, nh=nh, bank=bank, prev_key=prev_key):
                    e.wait_ge(sv.h, at[prev_key])
                    last = None
                    for kj in range(2):
                        last = e.matmul(bank[:, 0:512],
                                        ocT[:, kj * C + mc * 128: kj * C + mc * 128 + 128],
                                        basT[:, kj * D + nh * 512: kj * D + nh * 512 + 512],
                                        start=(kj == 0), stop=(kj == 1))
                    at[f"t_ow{mc}{nh}"] = sq.inc(last)

                def v_f(e, mc=mc, nh=nh, bank=bank):
                    e.wait_ge(sq.h, at[f"t_ow{mc}{nh}"])
                    e.wait_ge(sem_dbc.h, dw_at["bcast"])
                    ins = e.tensor_scalar(owt_sb[:, mc * D + nh * 512: mc * D + nh * 512 + 512],
                                          bank[:, 0:512], osc_ap(), None, OP.mult)
                    at[f"v_ow{mc}{nh}"] = sv.inc(ins)

                blk.tensor(t_f)
                blk.vector(v_f)

        def v_tile(i):
            def f(e):
                e.wait_ge(sem_dx[i % 4].h, 16 * (i // 4 + 1))
                if i >= 4:
                    e.wait_ge(ss.h, at[f"s_r{i - 4}"])
                e.bn_stats(stats[i % 4][:, 0:6], xin[i % 4][:, 0:512])
                ins = e.bn_stats(stats[i % 4][:, 6:12], xin[i % 4][:, 512:1024])
                k = sv.inc(ins)
                e.wait_ge(sv.h, k)
                ins = e.bn_aggr(mv[i % 4][:, 0:2], stats[i % 4][:, 0:12])
                ins.then_inc(sem_vbn.h, 1)
                sem_vbn.n += 1
            return f

        def v_norm(i):
            def f(e):
                e.wait_ge(ss.h, at[f"s_r{i}"])
                if i >= 2:
                    e.wait_ge(sem_txs.h, i - 1)
                ins = e.tensor_scalar(xn_tm[i % 2][:, :], xin[i % 4][:, :],
                                      mv[i % 4][:, 0:1], rcol[i % 4][:, 0:1],
                                      OP.subtract, OP.mult)
                ins.then_inc(sem_nrm.h, 1)
                sem_nrm.n += 1
            return f

        def t_tile(i):
            def f(e):
                e.wait_ge(sem_nrm.h, i + 1)
                if i == 0:
                    e.wait_ge(sem_dw00.h, sem_dw00.n)
                    e.wait_ge(sem_dw0.h, sem_dw0.n)
                if i >= 1:
                    e.wait_ge(ss.h, at[f"s_xnev{i - 1}"])
                last = None
                for ck in range(8):
                    last = e.transpose(pbfa[:, ck * 128:(ck + 1) * 128],
                                       xn_tm[i % 2][:, ck * 128:(ck + 1) * 128], id_sb[:, :])
                last.then_inc(sem_ttr.h, 1)
                sem_ttr.n += 1
                last = None
                for ck in range(8):
                    last = e.matmul(pf[0][:, ck:ck + 1],
                                    xn_tm[i % 2][:, ck * 128:(ck + 1) * 128],
                                    onesc_bf[:, 0:1],
                                    start=(i == 0 and ck == 0), stop=(i == ntiles - 1 and ck == 7))
                last.then_inc(sem_txs.h, 1)
                sem_txs.n += 1
            return f

        def t_proj(st):
            def f(e):
                e.wait_ge(ss.h, at[f"s_xnev{4 * st + 3}"])
                e.wait_ge(sv.h, at["v_wv7"])
                if st >= 1:
                    e.wait_ge(ss.h, at[f"s_kev{st - 1}_1"])
                else:
                    e.wait_ge(sv.h, at["v_ow11"])
                last = None
                for (wsb, banks, nm) in ((qw_sb, (pf[1], pf[2]), "q"), (kw_sb, (pf[3], pf[4]), "k")):
                    for mt in range(2):
                        for kc in range(8):
                            last = e.matmul(banks[mt][:, 0:512],
                                            wsb[:, kc * C + mt * 128: kc * C + mt * 128 + 128],
                                            xn_fm[:, kc * rpc + st * PST: kc * rpc + st * PST + PST],
                                            start=(kc == 0), stop=(kc == 7))
                last.then_inc(sem_tpj.h, 1)
                sem_tpj.n += 1
            return f

        def t_v(gt, st):
            def f(e):
                prev = at.get(f"s_vev{gt - 1}")
                if prev is not None:
                    e.wait_ge(ss.h, prev)
                elif st == 0:
                    e.wait_ge(sv.h, at["v_ow11"])
                e.wait_ge(ss.h, at[f"s_xnev{gt}"])
                last = None
                for kc in range(8):
                    last = e.matmul(pf[5][:, 0:C],
                                    xn_fm[:, kc * rpc + gt * 128: kc * rpc + gt * 128 + 128],
                                    vw_sb[:, kc * C: kc * C + C],
                                    start=(kc == 0), stop=(kc == 7))
                last.then_inc(sem_tv.h, 1)
                sem_tv.n += 1
            return f

        def t_ktr(gt, st):
            def f(e):
                e.wait_ge(ss.h, at[f"s_kev{st}_1"])
                prev = at.get(f"s_kdev{gt - 1}")
                if prev is not None:
                    e.wait_ge(ss.h, prev)
                last = None
                for ct in range(2):
                    last = e.transpose(pbfb[:, ct * 128: ct * 128 + 128],
                                       k_fm[:, ct * rpc + gt * 128: ct * rpc + gt * 128 + 128],
                                       id_sb[:, :])
                last.then_inc(sem_tkt.h, 1)
                sem_tkt.n += 1
            return f

        for i in range(ntiles + 1):
            if i < ntiles:
                blk.vector(v_tile(i))
            if i >= 1:
                j0 = i - 1
                blk.vector(v_norm(j0))
                blk.tensor(t_tile(j0))
                if j0 % 4 == 3:
                    st = j0 // 4
                    blk.tensor(t_proj(st))
                    for tt in range(4):
                        gt = 4 * st + tt
                        blk.tensor(t_v(gt, st))
                        blk.tensor(t_ktr(gt, st))

        def v_xnsum(e):
            e.wait_ge(sem_txs.h, ntiles)
            ins = e.tensor_copy(xnsum_sb[:, :], pf[0][:, 0:8])
            at["v_xnsum"] = sv.inc(ins)

        blk.vector(v_xnsum)

        for jj in range(nch - 1, -1, -1):
            bank = pf[1] if jj % 2 else pf[2]

            def t_cj(e, jj=jj, bank=bank):
                if jj == nch - 1:
                    e.wait_ge(ss.h, at[f"s_kdev{nch - 1}"])
                    e.wait_ge(ss.h, at[f"s_vev{nch - 1}"])
                    e.wait_ge(ss.h, at[f"s_kev{npst - 1}_1"])
                prev = at.get(f"v_T{jj + 2}")
                if prev is not None:
                    e.wait_ge(sv.h, prev)
                last = None
                for mt in range(2):
                    last = e.matmul(bank[:, mt * 256: mt * 256 + 256],
                                    kd_tm[:, jj * C + mt * 128: jj * C + mt * 128 + 128],
                                    v_tm[:, jj * C: jj * C + C],
                                    start=True, stop=True)
                at[f"t_C{jj}"] = sq.inc(last)

            def v_tj(e, jj=jj, bank=bank):
                e.wait_ge(sq.h, at[f"t_C{jj}"])
                if jj == nch - 1:
                    ins = e.tensor_copy(t_st[:, jj * 512:(jj + 1) * 512], bank[:, 0:512])
                else:
                    e.wait_ge(sv.h, at[f"v_T{jj + 1}"])
                    ins = e.scalar_tensor_tensor(t_st[:, jj * 512:(jj + 1) * 512],
                                                 t_st[:, (jj + 1) * 512:(jj + 2) * 512],
                                                 dL_col[:, 0:1], bank[:, 0:512],
                                                 OP.mult, OP.add)
                at[f"v_T{jj}"] = sv.inc(ins)

            blk.tensor(t_cj)
            blk.vector(v_tj)

        PERM = [4, 5, 6, 7, 0, 1, 2, 3]
        SIGNP = [u for u in range(8) if [1, 1, -1, -1, -1, -1, 1, 1][u] > 0]
        SIGNM = [u for u in range(8) if [1, 1, -1, -1, -1, -1, 1, 1][u] < 0]

        s3sched = []
        for st_ in range(nfst):
            if st_ == 0:
                s3sched += [("g1_0_0", 8)]
            else:
                s3sched += [(f"g1_{st_}_0", 8), (f"atev_{st_}_0", 2),
                            (f"atev_{st_}_1", 2), (f"bpev_{st_}", 4)]
            s3sched += [(f"g2_{st_}_0", 4), (f"g3_{st_}_0", 2), (f"th4_{st_}_0", 1)]
            if st_ == 0:
                s3sched += [("gf1", 1), ("gf2", 1), ("gf3", 1)]
            s3sched += [(f"crow_{st_}_0", 1)]
            s3sched += [(f"g1_{st_}_1", 8), (f"g2_{st_}_1", 4), (f"g3_{st_}_1", 2),
                        (f"th4_{st_}_1", 1), (f"g1_{st_}_2", 8)]
            if st_ + 1 < nfst:
                s3sched += [(f"aev_{st_ + 1}", 4)]
            s3sched += [(f"g2_{st_}_2", 4), (f"g3_{st_}_2", 2), (f"th4_{st_}_2", 1)]
        s3pos = {}
        _c = 0
        for _n, _k in s3sched:
            s3pos[_n] = _c
            _c += _k

        def s3g1(st, cyc, k):
            return s3pos[f"g1_{st}_{cyc}"] + k
        def s3g2(st, cyc, k):
            return s3pos[f"g2_{st}_{cyc}"] + k
        def s3g3(st, cyc, k):
            return s3pos[f"g3_{st}_{cyc}"] + k
        def s3th4(st, cyc):
            return s3pos[f"th4_{st}_{cyc}"] + 1
        def s3aev(st):
            return s3pos[f"aev_{st}"] + 4
        def s3bpev(st):
            return s3pos[f"bpev_{st}"] + 4
        def s3atev(st, tt):
            return s3pos[f"atev_{st}_{tt}"] + 2

        def s3b(st, cyc):
            return s3g1(st, cyc, 0)

        def v_sj(j):
            def f(e):
                e.wait_ge(sv.h, at["v_A3"])
                e.wait_ge(sem_vr.h, GRP)
                prev = at.get(f"v_retr{j - 2}")
                if prev is not None:
                    e.wait_ge(sv.h, prev)
                if j == nch - 1:
                    ins = e.tensor_scalar(sj_bf[j % 2][:, :], sin_sb[:, :], dS_ap(j), None, OP.mult)
                else:
                    ins = e.scalar_tensor_tensor(sj_bf[j % 2][:, :], sin_sb[:, :], dS_ap(j),
                                                 t_st[:, (j + 1) * 512:(j + 2) * 512], OP.mult, OP.add)
                k = sv.inc(ins)
                for ck in range(2):
                    ins = e.tensor_tensor(qd_tmp[j % 2][:, ck * 128:(ck + 1) * 128],
                                          q_fm[:, ck * rpc + j * 128: ck * rpc + j * 128 + 128],
                                          DQ_sb[:, :], OP.mult)
                at[f"v_sj{j}"] = sv.inc(ins)
            return f

        def t_scores(j, st):
            def f(e):
                e.wait_ge(sv.h, at[f"v_sj{j}"])
                prev = at.get(f"v_sco{j - 2}")
                if prev is not None:
                    e.wait_ge(sv.h, prev)
                e.wait_ge(sem_s3.h, s3g3(max(st - 1, 0), 2, 2))
                rg = (j % 2) * 128
                last = None
                for kc in range(2):
                    last = e.matmul(pf[4][:, rg:rg + 128],
                                    k_fm[:, kc * rpc + j * 128: kc * rpc + j * 128 + 128],
                                    q_fm[:, kc * rpc + j * 128: kc * rpc + j * 128 + 128],
                                    start=(kc == 0), stop=(kc == 1))
                at[f"t_sco{j}"] = sq.inc(last)
            return f

        def v_scores(j):
            def f(e):
                e.wait_ge(sq.h, at[f"t_sco{j}"])
                prev = at.get(f"t_retr{j - 2}")
                if prev is not None:
                    e.wait_ge(sq.h, prev)
                rg = (j % 2) * 128
                ins = e.tensor_tensor(sco_sb[j % 2][:, :], pf[4][:, rg:rg + 128],
                                      DT_sb[:, :], OP.mult)
                at[f"v_sco{j}"] = sv.inc(ins)
            return f

        def t_retr(j, st):
            def f(e):
                e.wait_ge(sv.h, at[f"v_sco{j}"])
                pst = max(st - 1, 0)
                e.wait_ge(sem_v3.h, (pst * 2 + 2) * 8)
                e.wait_ge(sem_s3.h, s3th4(pst, 2))
                rg = (j % 2) * 256
                last = None
                for mc in range(2):
                    e.matmul(pf[5][:, rg + mc * 128: rg + mc * 128 + 128],
                             v_tm[:, j * C + mc * 128: j * C + mc * 128 + 128],
                             sco_sb[j % 2][:, :], start=True, stop=False)
                    e.matmul(pf[5][:, rg + mc * 128: rg + mc * 128 + 128],
                             sj_bf[j % 2][:, mc * 128: mc * 128 + 128],
                             qd_tmp[j % 2][:, 0:128], start=False, stop=False)
                    last = e.matmul(pf[5][:, rg + mc * 128: rg + mc * 128 + 128],
                                    sj_bf[j % 2][:, C + mc * 128: C + mc * 128 + 128],
                                    qd_tmp[j % 2][:, 128:256], start=False, stop=True)
                at[f"t_retr{j}"] = sq.inc(last)
            return f

        def v_retr(j):
            def f(e):
                e.wait_ge(sq.h, at[f"t_retr{j}"])
                rg = (j % 2) * 256
                ins = e.tensor_copy(retr_sb[j % 2][:, :], pf[5][:, rg:rg + C])
                at[f"v_retr{j}"] = sv.inc(ins)
            return f

        def t_oproj(j, st):
            def f(e):
                e.wait_ge(sv.h, at[f"v_retr{j}"])
                if st == 0:
                    prev = at.get(f"v_attnev{j - 1}")
                    if prev is not None:
                        e.wait_ge(sv.h, prev)
                    e.wait_ge(sv.h, at["v_Bev0"])
                    if nfst > 1:
                        e.wait_ge(sem_s3.h, s3aev(1))
                elif j % cpst == 0:
                    e.wait_ge(sem_s3.h, s3aev(st))
                else:
                    e.wait_ge(sem_s3.h, s3atev(st, j % cpst - 1))
                last = None
                for nh in range(2):
                    for kc in range(2):
                        last = e.matmul(pf[2 + nh][:, 0:512],
                                        retr_sb[j % 2][:, kc * 128: kc * 128 + 128],
                                        owt_sb[:, kc * D + nh * 512: kc * D + nh * 512 + 512],
                                        start=(kc == 0), stop=(kc == 1))
                at[f"t_o{j}"] = sq.inc(last)
            return f

        def v_oproj(j, st):
            tt = j - st * cpst

            def f(e):
                e.wait_ge(sq.h, at[f"t_o{j}"])
                ab = attn2[st % 2]
                e.tensor_copy(ab[:, tt * D: tt * D + 512], pf[2][:, 0:512])
                ins = e.tensor_copy(ab[:, tt * D + 512: tt * D + 1024], pf[3][:, 0:512])
                at[f"v_attnev{j}"] = sv.inc(ins)
            return f

        def emit_attn(st):
            for j in range(st * cpst, (st + 1) * cpst):
                blk.vector(v_sj(j))
                blk.tensor(t_scores(j, st))
                blk.vector(v_scores(j))
                blk.tensor(t_retr(j, st))
                blk.vector(v_retr(j))
                blk.tensor(t_oproj(j, st))
                if st == 0:
                    blk.vector(v_oproj(j, st))

        def xn_slice(ck, st):
            return xn_fm[:, ck * rpc + st * FST: ck * rpc + st * FST + FST]

        def t_A(st, half=None):
            mts = range(8) if half is None else range(half * 4, half * 4 + 4)
            def f(e):
                e.wait_ge(sem_dwb.h, sem_dwb.n)
                if half in (None, 0):
                    if st == 0:
                        e.wait_ge(sv.h, at["v_T0"])
                    else:
                        e.wait_ge(sv.h, at[f"v_Bev{st - 1}"])
                for mt in mts:
                    bank = pf[mt // 2]
                    cols = (mt % 2) * 256
                    last = None
                    for kc in range(8):
                        last = e.matmul(bank[:, cols:cols + 256],
                                        w1_sb[:, kc * D + mt * 128: kc * D + mt * 128 + 128],
                                        xn_slice(kc, st), start=(kc == 0), stop=(kc == 7))
                    at[f"t_h1_{st}_0_{mt}"] = sq.inc(last)
            return f

        def v_Aev(st):
            def f(e):
                if st >= 1:
                    e.wait_ge(sem_v3.h, ((st - 1) * 2 + 2) * 8)
                ins = None
                for b in range(4):
                    e.wait_ge(sq.h, at[f"t_h1_{st}_0_{2 * b + 1}"])
                    ins = e.tensor_copy(A_sb[:, b * 512:(b + 1) * 512], pf[b][:, 0:512])
                at[f"v_Aev{st}"] = sv.inc(ins)
            return f

        def t_Bpass(st, sign_set, tag, half=None):
            mts = range(8) if half is None else range(half * 4, half * 4 + 4)
            def f(e):
                if tag == "p":
                    if st == 0:
                        e.wait_ge(sv.h, at["v_Aev0"])
                    else:
                        e.wait_ge(sem_s3.h, s3atev(st, cpst - 1))
                elif half in (None, 0):
                    if st == 0:
                        e.wait_ge(sv.h, at["v_Bpev0"])
                    else:
                        e.wait_ge(sem_s3.h, s3bpev(st))
                for mt in mts:
                    bank = pf[mt // 2]
                    cols = (mt % 2) * 256
                    last = None
                    for ki, u in enumerate(sign_set):
                        last = e.matmul(bank[:, cols:cols + 256],
                                        w1_sb[:, u * D + mt * 128: u * D + mt * 128 + 128],
                                        xn_slice(PERM[u], st),
                                        start=(ki == 0), stop=(ki == len(sign_set) - 1))
                    at[f"t_B{tag}_{st}_{mt}"] = sq.inc(last)
            return f

        def v_Bpev(st):
            def f(e):
                e.wait_ge(sq.h, at[f"t_Bp_{st}_7"])
                if st >= 1:
                    e.wait_ge(sq.h, at[f"t_fT{st - 1}_1"])
                ins = None
                for b in range(4):
                    ins = e.tensor_copy(comb_sb[:, b * 512:(b + 1) * 512], pf[b][:, 0:512])
                at[f"v_Bpev{st}"] = sv.inc(ins)
            return f

        def v_Bmev(st):
            def f(e):
                e.wait_ge(sq.h, at[f"t_Bm_{st}_7"])
                if st >= 1:
                    e.wait_ge(sem_s3.h, s3bpev(st))
                ins = None
                for b in range(4):
                    ins = e.tensor_tensor(B_sb[:, b * 512:(b + 1) * 512],
                                          comb_sb[:, b * 512:(b + 1) * 512], pf[b][:, 0:512],
                                          OP.subtract)
                at[f"v_Bev{st}"] = sv.inc(ins)
            return f


        blk.tensor(t_A(0))
        blk.vector(v_Aev(0))
        blk.tensor(t_Bpass(0, SIGNP, "p"))
        blk.vector(v_Bpev(0))
        blk.tensor(t_Bpass(0, SIGNM, "m"))
        blk.vector(v_Bmev(0))

        def g_ccx(e):
            e.wait_ge(sv.h, at["v_xnsum"])
            ins = e.dma_start(out=bass.AP(ccx_in, 0, [[8, 128], [1, 8]]),
                              in_=xnsum_sb[:, :], allow_slow_non_contiguous=True)
            k1 = sem_dcc.inc(ins, 16)
            e.wait_ge(sem_dcc.h, k1)
            ins = e.collective_compute(
                "AllGather", OP.bypass,
                replica_groups=[[0, 1, 2, 3], [4, 5, 6, 7]],
                ins=[ccx_in[:, :]], outs=[ccx_out[:, :]])
            at["ccx"] = sem_ccx.inc(ins, 1)

        blk.gpsimd(g_ccx)

        def g_cc(e):
            e.wait_ge(sv.h, at["v_T0"])
            ins = e.dma_start(out=bass.AP(cc_in, 0, [[512, 128], [1, 512]]), in_=t_st[:, 0:512])
            k2 = sem_dcc.inc(ins, 16)
            e.wait_ge(sem_dcc.h, k2)
            ins = e.collective_compute(
                "AllGather", OP.bypass,
                replica_groups=[[0, 1, 2, 3], [4, 5, 6, 7]],
                ins=[cc_in[:, :]], outs=[cc_out[:, :]])
            at["cc"] = sem_cc.inc(ins, 1)

        blk.gpsimd(g_cc)

        def g_xg(e):
            e.wait_ge(sem_ccx.h, at["ccx"])
            ins = None
            for r in range(GRP):
                ins = e.dma_start(out=xg_sb[r][:, :],
                                  in_=bass.AP(ccx_out, r * D, [[8, 128], [1, 8]]),
                                  allow_slow_non_contiguous=True)
                sem_dxg.inc(ins, 16)
            at["d_xg"] = sem_dxg.n

        blk.gpsimd(g_xg)

        def v_rx(e):
            e.wait_ge(sem_dxg.h, at["d_xg"])
            ins = e.tensor_copy(xnt_sb[:, :], xg_sb[0][:, :])
            for r in range(1, GRP):
                k = sv.inc(ins)
                e.wait_ge(sv.h, k)
                ins = e.tensor_tensor(xnt_sb[:, :], xnt_sb[:, :], xg_sb[r][:, :], OP.add)
            k = sv.inc(ins)
            e.wait_ge(sv.h, k)
            ins = e.tensor_copy(xnt_bf[:, :], xnt_sb[:, :])
            at["v_xnt"] = sv.inc(ins)

        v_r_bodies = []
        for r in range(GRP):
            def g_r(e, r=r):
                if r == 0:
                    e.wait_ge(sem_cc.h, at["cc"])
                else:
                    e.wait_ge(sem_vr.h, r)
                ins = e.dma_start(out=gath_sb[:, :], in_=bass.AP(cc_out, r * CCN, [[512, 128], [1, 512]]))
                at[f"d_rnd{r}"] = sem_dcc.inc(ins, 16)

            def v_r(e, r=r):
                e.wait_ge(sem_dcc.h, at[f"d_rnd{r}"])
                if r == 0:
                    e.wait_ge(sem_dbc.h, dw_at["bcast"])
                if r == 0:
                    ins = e.tensor_scalar(sin_sb[:, :], gath_sb[:, :], wr_ap(0), None, OP.mult)
                else:
                    ins = e.scalar_tensor_tensor(sin_sb[:, :], gath_sb[:, :], wr_ap(r),
                                                 sin_sb[:, :], OP.mult, OP.add)
                ins.then_inc(sem_vr.h, 1)
                sem_vr.n += 1

            blk.gpsimd(g_r)
            v_r_bodies.append(v_r)

        def t_gf1(e):
            e.wait_ge(sv.h, at["v_xnt"])
            e.wait_ge(sem_dwb.h, sem_dwb.n)
            e.wait_ge(sem_s3.h, s3g3(0, 0, 2))
            last = None
            for mt in range(4):
                for kc in range(8):
                    last = e.matmul(pf[4][:, mt:mt + 1],
                                    g1_sb[:, kc * 512 + mt * 128: kc * 512 + mt * 128 + 128],
                                    xnt_bf[:, kc:kc + 1],
                                    start=(kc == 0), stop=(kc == 7))
            at["t_gf1"] = sq.inc(last)

        def t_gf2(e):
            e.wait_ge(sem_s3.h, s3pos["gf1"] + 1)
            e.wait_ge(sem_s3.h, s3th4(0, 0))
            last = None
            for mt in range(2):
                for kc in range(4):
                    last = e.matmul(pf[5][:, mt:mt + 1],
                                    g2_sb[:, kc * 256 + mt * 128: kc * 256 + mt * 128 + 128],
                                    gg1_sb[:, kc:kc + 1],
                                    start=(kc == 0), stop=(kc == 3))
            at["t_gf2"] = sq.inc(last)

        def t_gf3(e):
            e.wait_ge(sem_s3.h, s3pos["gf2"] + 1)
            last = None
            for kc in range(2):
                last = e.matmul(pf[4][0:1, 16:17], g3_sb[:, kc:kc + 1], gg2_sb[:, kc:kc + 1],
                                start=(kc == 0), stop=(kc == 1))
            at["t_gf3"] = sq.inc(last)

        def v_gf(e):
            e.wait_ge(sem_s3.h, s3pos["gf3"] + 1)
            e.wait_ge(sv.h, at["v_scl"])
            ins = e.tensor_scalar(sclr[0:1, 9:10], sclr[0:1, 8:9], sclr[0:1, 7:8], 0.1, OP.mult, OP.add)
            at["v_A3"] = sv.inc(ins)


        def t_h2(st, cyc):
            def f(e):
                waited = 0
                if cyc == 0 and st >= 1:
                    e.wait_ge(sv.h, at[f"v_retr{st * cpst + cpst - 1}"])
                for mt in range(4):
                    bank = pf[4 + mt // 2]
                    cols = (mt % 2) * 256
                    last = None
                    for kc in range(8):
                        tgt = s3g1(st, cyc, kc + 1 if CHASE else 8)
                        if tgt > waited:
                            waited = tgt
                            e.wait_ge(sem_s3.h, waited)
                        last = e.matmul(bank[:, cols:cols + 256],
                                        w2_sb[:, kc * 512 + mt * 128: kc * 512 + mt * 128 + 128],
                                        h1_bf[:, kc * FST:(kc + 1) * FST],
                                        start=(kc == 0), stop=(kc == 7))
                    at[f"t_h2_{st}_{cyc}_{mt}"] = sq.inc(last)
            return f

        def t_h3z4(st, cyc):
            def f(e):
                waited = 0
                for mt in range(2):
                    last = None
                    for kc in range(4):
                        tgt = max(s3g2(st, cyc, kc + 1), s3g2(st, cyc, 2)) if CHASE else s3g2(st, cyc, 4)
                        if tgt > waited:
                            waited = tgt
                            e.wait_ge(sem_s3.h, waited)
                        last = e.matmul(pf[4][:, mt * 256: mt * 256 + 256],
                                        w3_sb[:, kc * 256 + mt * 128: kc * 256 + mt * 128 + 128],
                                        h2_bf[:, kc * FST:(kc + 1) * FST],
                                        start=(kc == 0), stop=(kc == 3))
                    at[f"t_h3_{st}_{cyc}_{mt}"] = sq.inc(last)
                last = None
                for kc in range(2):
                    tgt = s3g3(st, cyc, kc + 1) if CHASE else s3g3(st, cyc, 2)
                    if tgt > waited:
                        waited = tgt
                        e.wait_ge(sem_s3.h, waited)
                    last = e.matmul(pf[5][0:1, 0:FST], w4_sb[:, kc:kc + 1],
                                    h3_bf[:, kc * FST:(kc + 1) * FST],
                                    start=(kc == 0), stop=(kc == 1))
                at[f"t_z4_{st}_{cyc}"] = sq.inc(last)
            return f

        def v_crow_f(st, cyc):
            def f(e):
                e.wait_ge(sem_s3.h, s3th4(st, cyc))
                if cyc == 1:
                    ins = e.tensor_scalar(c2_bf[0:1, :], th4_row[0:1, :],
                                          sclr[0:1, 6:7], sclr[0:1, 9:10], OP.mult, OP.add)
                    k = sv.inc(ins)
                    e.wait_ge(sv.h, k)
                    ins = e.tensor_tensor(abrow[0:1, 0:FST], brow_bf[0:1, :],
                                          c2_bf[0:1, :], OP.mult)
                    k = sv.inc(ins)
                    e.wait_ge(sv.h, k)
                    e.tensor_scalar(arow_bf[0:1, :], abrow[0:1, 0:FST], -1.0, 1.0,
                                    OP.mult, OP.add)
                    ins = e.tensor_tensor(brow_bf[0:1, :], brow_bf[0:1, :],
                                          c2_bf[0:1, :], OP.add)
                else:
                    ins = e.tensor_scalar(crows[0:1, 0:FST], th4_row[0:1, :],
                                          sclr[0:1, 6:7], sclr[0:1, 9:10], OP.mult, OP.add)
                    k = sv.inc(ins)
                    e.wait_ge(sv.h, k)
                    e.tensor_tensor(abrow[0:1, 0:FST], brow_bf[0:1, :],
                                    crows[0:1, 0:FST], OP.mult)
                    ins = e.tensor_tensor(abrow[0:1, FST:2 * FST], arow_bf[0:1, :],
                                          crows[0:1, 0:FST], OP.mult)
                    k = sv.inc(ins)
                    e.wait_ge(sv.h, k)
                    e.tensor_tensor(crows[0:1, 0:FST], arow_bf[0:1, :],
                                    abrow[0:1, 0:FST], OP.subtract)
                    ins = e.tensor_tensor(crows[0:1, FST:2 * FST], brow_bf[0:1, :],
                                          abrow[0:1, FST:2 * FST], OP.add)
                    k = sv.inc(ins)
                    e.wait_ge(sv.h, k)
                    ins = e.tensor_scalar(crows[0:1, 2 * FST:3 * FST],
                                          crows[0:1, FST:2 * FST], -1.0, None, OP.mult)
                    k = sv.inc(ins)
                    e.wait_ge(sv.h, k)
                    for c in range(3):
                        ins = e.tensor_copy(
                            bass.AP(crows3, c, [[3 * FST, 1], [3, FST]]),
                            crows[0:1, c * FST:(c + 1) * FST])
                at[f"v_crow{st}_{cyc}"] = sv.inc(ins)
            return f

        def t_cb_f(st, cyc):
            def f(e):
                if cyc == 0:
                    e.wait_ge(sem_s3.h, s3pos[f"crow_{st}_0"] + 1)
                    ins = e.matmul(pf[5][:, 256:256 + FST], onesr_bf[0:1, 0:128],
                                   brow_bf[0:1, :], start=True, stop=True)
                else:
                    e.wait_ge(sv.h, at[f"v_crow{st}_{cyc}"])
                    e.matmul(pf[5][:, 0:FST], onesr_bf[0:1, 0:128],
                             arow_bf[0:1, :], start=True, stop=True)
                    ins = e.matmul(pf[5][:, 256:256 + FST], onesr_bf[0:1, 0:128],
                                   brow_bf[0:1, :], start=True, stop=True)
                at[f"t_cb{st}_{cyc}"] = sq.inc(ins)
            return f

        def v_comb(st, cyc):
            def f(e):
                e.wait_ge(sq.h, at[f"t_cb{st}_{cyc - 1}"])
                e.wait_ge(sv.h, at[f"v_Bev{st}"])
                e.wait_ge(sq.h, at[f"t_h2_{st}_{cyc - 1}_3"])
                for ck in range(8):
                    c0, c1 = ck * FST, (ck + 1) * FST
                    if cyc == 1:
                        ins = e.tensor_tensor(comb_sb[:, c0:c1], B_sb[:, c0:c1],
                                              pf[5][:, 256:256 + FST], OP.mult)
                        k = sv.inc(ins)
                        e.wait_ge(sv.h, k)
                        ins = e.tensor_tensor(comb_sb[:, c0:c1], comb_sb[:, c0:c1],
                                              A_sb[:, c0:c1], OP.add)
                    else:
                        e.tensor_tensor(h1_bf[:, c0:c1], A_sb[:, c0:c1],
                                        pf[5][:, 0:FST], OP.mult)
                        ins = e.tensor_tensor(comb_sb[:, c0:c1], B_sb[:, c0:c1],
                                              pf[5][:, 256:256 + FST], OP.mult)
                        k = sv.inc(ins)
                        e.wait_ge(sv.h, k)
                        ins = e.tensor_tensor(comb_sb[:, c0:c1], comb_sb[:, c0:c1],
                                              h1_bf[:, c0:c1], OP.add)
                    ins.then_inc(sem_v3.h, 1)
                    sem_v3.n += 1
            return f

        def t_ffnT(st, tt):
            def f(e):
                pb = pbfa if tt == 0 else pbfb
                if st >= 1:
                    e.wait_ge(sem_fin.h, cpst * (st - 1) + tt + 1)
                last = None
                for ck in range(8):
                    last = e.transpose(pb[:, ck * 128:(ck + 1) * 128],
                                       xn_fm[:, ck * rpc + st * FST + tt * 128:
                                             ck * rpc + st * FST + tt * 128 + 128],
                                       id_sb[:, :])
                at[f"t_fT{st}_{tt}"] = sq.inc(last)
            return f

        def g_abT(st):
            def f(e):
                e.wait_ge(sv.h, at[f"v_crow{st}_2"])
                ins = e.dma_start(out=ab_dram[:, :], in_=crows3[0:1, 0:3 * FST])
                sem_dab.inc(ins, 16)
                e.wait_ge(sem_dab.h, 48 * st + 16)
                for tt in range(2):
                    src = bass.AP(ab_dram, tt * 384, [[3, 128], [1, 3]])
                    ins = e.dma_start(out=abcol[:, tt * 3:tt * 3 + 3], in_=src)
                    sem_dab.inc(ins, 16)
            return f

        def v_fin(st):
            sign = [1, 1, -1, -1, -1, -1, 1, 1]
            def f(e):
                for tt in range(2):
                    gt = st * cpst + tt
                    pb = pbfa if tt == 0 else pbfb
                    e.wait_ge(sq.h, at[f"t_fT{st}_{tt}"])
                    if st >= 1:
                        e.wait_ge(sem_s3.h, s3atev(st, cpst - 1))
                    e.wait_ge(sem_dab.h, 48 * (st + 1))
                    e.wait_ge(sem_dx[gt % 4].h, 16 * (ntiles // 4 + gt // 4 + 1))
                    if gt >= 2:
                        e.wait_ge(sem_dy[gt % 2].h, 16 * (gt // 2))
                    ins = e.scalar_tensor_tensor(y_sb[tt][:, :], pb[:, 0:1024],
                                                 abcol[:, tt * 3:tt * 3 + 1],
                                                 attn2[st % 2][:, tt * D: (tt + 1) * D],
                                                 OP.mult, OP.add)
                    k = sv.inc(ins)
                    e.wait_ge(sv.h, k)
                    for (g0, g1, pos) in ((0, 2, True), (2, 4, False),
                                          (4, 6, False), (6, 8, True)):
                        col = tt * 3 + (1 if pos else 2)
                        ins = e.scalar_tensor_tensor(
                            y_sb[tt][:, g0 * 128:g1 * 128],
                            pb[:, PERM[g0] * 128:(PERM[g0] + g1 - g0) * 128],
                            abcol[:, col:col + 1],
                            y_sb[tt][:, g0 * 128:g1 * 128], OP.mult, OP.add)
                    k = sv.inc(ins)
                    e.wait_ge(sv.h, k)
                    ins = e.tensor_tensor(y_sb[tt][:, :], y_sb[tt][:, :], xin[gt % 4][:, :], OP.add)
                    ins.then_inc(sem_fin.h, 1)
                    sem_fin.n += 1
            return f

        blk.tensor(t_h2(0, 0))
        blk.vector(v_rx)
        blk.tensor(t_h3z4(0, 0))
        blk.tensor(t_gf1)
        blk.tensor(t_gf2)
        blk.tensor(t_gf3)
        blk.vector(v_gf)
        blk.tensor(t_cb_f(0, 0))
        blk.vector(v_comb(0, 1))
        blk.tensor(t_h2(0, 1))
        if nfst > 1:
            blk.tensor(t_A(1, half=0))
        blk.tensor(t_h3z4(0, 1))
        if nfst > 1:
            blk.tensor(t_A(1, half=1))
        for vb in v_r_bodies:
            blk.vector(vb)
        blk.vector(v_crow_f(0, 1))
        blk.tensor(t_cb_f(0, 1))
        blk.vector(v_comb(0, 2))
        blk.tensor(t_h2(0, 2))
        blk.tensor(t_ffnT(0, 0))
        blk.tensor(t_h3z4(0, 2))
        blk.tensor(t_ffnT(0, 1))
        blk.vector(v_crow_f(0, 2))
        blk.gpsimd(g_abT(0))
        emit_attn(0)
        if nfst > 1:
            emit_attn(1)
        blk.vector(v_fin(0))

        for st in range(1, nfst):
            if st >= 1:
                blk.tensor(t_Bpass(st, SIGNP, "p"))
            blk.tensor(t_h2(st, 0))
            if st >= 1:
                blk.tensor(t_Bpass(st, SIGNM, "m", half=0))
            blk.tensor(t_h3z4(st, 0))
            if st >= 1:
                blk.tensor(t_Bpass(st, SIGNM, "m", half=1))
                blk.vector(v_Bmev(st))
            blk.tensor(t_cb_f(st, 0))
            if st + 1 < nfst:
                blk.tensor(t_A(st + 1, half=0))
            blk.vector(v_comb(st, 1))
            blk.tensor(t_h2(st, 1))
            blk.tensor(t_h3z4(st, 1))
            blk.vector(v_crow_f(st, 1))
            blk.tensor(t_cb_f(st, 1))
            if st + 1 < nfst:
                blk.tensor(t_A(st + 1, half=1))
            blk.vector(v_comb(st, 2))
            blk.tensor(t_h2(st, 2))
            blk.tensor(t_ffnT(st, 0))
            blk.tensor(t_h3z4(st, 2))
            blk.tensor(t_ffnT(st, 1))
            if st + 1 < nfst:
                emit_attn(st + 1)
            blk.vector(v_crow_f(st, 2))
            blk.gpsimd(g_abT(st))
            blk.vector(v_fin(st))



        def s3_gelu1_c0(e, st):
            if st >= 1:
                e.wait_ge(sem_v3.h, ((st - 1) * 2 + 2) * 8)
                e.wait_ge(sq.h, at[f"t_h2_{st - 1}_2_3"])
            else:
                e.wait_ge(sv.h, at["v_Aev0"])
            for mt in range(8):
                ins = e.activation(h1_bf[:, mt * FST:(mt + 1) * FST],
                                   A_sb[:, mt * FST:(mt + 1) * FST],
                                   AF.Gelu, bias=b1_sb[:, mt:mt + 1])
                ins.then_inc(sem_s3.h, 1)
                sem_s3.n += 1

        def scalar_mega3(e):
            preload(e, SET_GELU)
            e.wait_ge(sem_dwb.h, sem_dwb.n)
            for name, cnt in s3sched:
                p = name.split("_")
                base = sem_s3.n
                if name == "gf1":
                    e.wait_ge(sq.h, at["t_gf1"])
                    last = None
                    for mt in range(4):
                        last = e.activation(gg1_sb[:, mt:mt + 1], pf[4][:, mt:mt + 1], AF.Gelu,
                                            bias=gb1_sb[:, mt:mt + 1], scale=1.0 / float(seq_t))
                    last.then_inc(sem_s3.h, 1)
                    sem_s3.n += 1
                elif name == "gf2":
                    e.wait_ge(sq.h, at["t_gf2"])
                    last = None
                    for mt in range(2):
                        last = e.activation(gg2_sb[:, mt:mt + 1], pf[5][:, mt:mt + 1], AF.Gelu,
                                            bias=gb2_sb[:, mt:mt + 1])
                    last.then_inc(sem_s3.h, 1)
                    sem_s3.n += 1
                elif name == "gf3":
                    e.wait_ge(sq.h, at["t_gf3"])
                    ins = e.activation(sclr[0:1, 8:9], pf[4][0:1, 16:17], AF.Tanh,
                                       bias=sclr[0:1, 5:6], scale=0.5)
                    ins.then_inc(sem_s3.h, 1)
                    sem_s3.n += 1
                elif p[0] == "aev":
                    stp = int(p[1])
                    e.wait_ge(sq.h, at[f"t_h1_{stp}_0_7"])
                    for b in range(4):
                        ins = e.activation(A_sb[:, b * 512:(b + 1) * 512],
                                           pf[b][:, 0:512], AF.Copy)
                        ins.then_inc(sem_s3.h, 1)
                        sem_s3.n += 1
                elif p[0] == "atev":
                    stp, tt = int(p[1]), int(p[2])
                    jj = stp * cpst + tt
                    e.wait_ge(sq.h, at[f"t_o{jj}"])
                    ab = attn2[stp % 2]
                    for half, bank in ((0, pf[2]), (1, pf[3])):
                        ins = e.activation(ab[:, tt * D + half * 512: tt * D + half * 512 + 512],
                                           bank[:, 0:512], AF.Copy)
                        ins.then_inc(sem_s3.h, 1)
                        sem_s3.n += 1
                elif p[0] == "bpev":
                    stp = int(p[1])
                    e.wait_ge(sq.h, at[f"t_Bp_{stp}_7"])
                    for b in range(4):
                        ins = e.activation(comb_sb[:, b * 512:(b + 1) * 512],
                                           pf[b][:, 0:512], AF.Copy)
                        ins.then_inc(sem_s3.h, 1)
                        sem_s3.n += 1
                elif p[0] == "g1":
                    st, cyc = int(p[1]), int(p[2])
                    if cyc == 0:
                        s3_gelu1_c0(e, st)
                    else:
                        e.wait_ge(sq.h, at[f"t_h2_{st}_{cyc - 1}_3"])
                        v3base = (st * 2 + cyc - 1) * 8
                        for mt in range(8):
                            e.wait_ge(sem_v3.h, v3base + mt + 1)
                            ins = e.activation(h1_bf[:, mt * FST:(mt + 1) * FST],
                                               comb_sb[:, mt * FST:(mt + 1) * FST],
                                               AF.Gelu, bias=b1_sb[:, mt:mt + 1])
                            ins.then_inc(sem_s3.h, 1)
                            sem_s3.n += 1
                elif p[0] == "g2":
                    st, cyc = int(p[1]), int(p[2])
                    for mt in range(4):
                        e.wait_ge(sq.h, at[f"t_h2_{st}_{cyc}_{(mt // 2) * 2 + 1}"])
                        ins = e.activation(h2_bf[:, mt * FST:(mt + 1) * FST],
                                           pf[4 + mt // 2][:, (mt % 2) * 256:(mt % 2) * 256 + 256],
                                           AF.Gelu, bias=b2_sb[:, mt:mt + 1])
                        ins.then_inc(sem_s3.h, 1)
                        sem_s3.n += 1
                elif p[0] == "g3":
                    st, cyc = int(p[1]), int(p[2])
                    for mt in range(2):
                        e.wait_ge(sq.h, at[f"t_h3_{st}_{cyc}_1"])
                        ins = e.activation(h3_bf[:, mt * FST:(mt + 1) * FST],
                                           pf[4][:, mt * 256: mt * 256 + 256],
                                           AF.Gelu, bias=b3_sb[:, mt:mt + 1])
                        ins.then_inc(sem_s3.h, 1)
                        sem_s3.n += 1
                elif p[0] == "crow":
                    st = int(p[1])
                    if st == 0:
                        e.wait_ge(sv.h, at["v_A3"])
                    e.wait_ge(sem_s3.h, s3th4(st, 0))
                    ins = e.activation(brow_bf[0:1, 0:FST], th4_row[0:1, :], AF.Identity,
                                       bias=sclr[0:1, 9:10], scale=sclr[0:1, 6:7])
                    ins.then_inc(sem_s3.h, 1)
                    sem_s3.n += 1
                elif p[0] == "th4":
                    st, cyc = int(p[1]), int(p[2])
                    if cyc == 2:
                        e.wait_ge(sv.h, at[f"v_crow{st}_1"])
                    elif cyc == 0 and st >= 1:
                        e.wait_ge(sv.h, at[f"v_crow{st - 1}_2"])
                    e.wait_ge(sq.h, at[f"t_z4_{st}_{cyc}"])
                    ins = e.activation(th4_row[0:1, :], pf[5][0:1, 0:FST], AF.Tanh,
                                       bias=sclr[0:1, 4:5], scale=0.5)
                    ins.then_inc(sem_s3.h, 1)
                    sem_s3.n += 1
                else:
                    raise AssertionError(name)
                assert sem_s3.n == base + cnt == s3pos[name] + cnt, name

        blk.scalar(scalar_mega3)

    return nc


def _consts(rpc):
    nch = rpc // L
    p = np.arange(128, dtype=np.float32)
    f = np.arange(128, dtype=np.float32)
    E1 = np.maximum(p[:, None] - f[None, :] - 1.0, 0.0).astype(np.float32)
    M1 = (p[:, None] > f[None, :]).astype(np.float32)
    EQ = np.ascontiguousarray(np.broadcast_to((127.0 - f)[None, :], (128, 128))).astype(np.float32)
    CK = np.stack([p, np.full(128, 128.0, np.float32)], axis=1).astype(np.float32)
    return dict(
        cpack=np.concatenate([E1, M1, EQ, CK], axis=1).astype(np.float32),
        ident=np.eye(128, dtype=np.float32),
        onesr=np.ones((1, 128), dtype=np.float32),
    )


def _core_consts(rpc, r):
    nch = rpc // L
    rr = np.arange(GRP, dtype=np.float32)
    EJ = (np.float32(L) * (nch - 1 - np.arange(nch, dtype=np.float32))).astype(np.float32)
    E4 = np.maximum(np.float32(rpc) * (rr - r - 1.0), 0.0).astype(np.float32)
    M4 = (rr > r).astype(np.float32)
    rpack = np.zeros(nch + 2 * GRP + 128, np.float32)
    rpack[0:nch] = EJ
    rpack[nch:nch + GRP] = E4
    rpack[nch + GRP:nch + 2 * GRP] = M4
    return dict(rpack=rpack[None, :])


def make_in_maps(inputs, rpc=RPC):
    x = np.ascontiguousarray(inputs["L0_attn_x"], dtype=np.float32)
    f32 = lambda k: np.ascontiguousarray(inputs[k], np.float32)
    s11 = lambda k: np.asarray(inputs[k], np.float32).reshape(1, 1)
    shared = {
        "basis": f32("L0_attn_basis"),
        "w1": f32("L0_ffn_cn_w1"), "b1": f32("L0_ffn_cn_b1"),
        "w2": f32("L0_ffn_cn_w2"), "b2": f32("L0_ffn_cn_b2"),
        "w3": f32("L0_ffn_cn_w3"), "b3": f32("L0_ffn_cn_b3"),
        "w4": f32("L0_ffn_cn_w4"),
        "gw1": f32("L0_ffn_gc_w1"), "gb1": f32("L0_ffn_gc_b1"),
        "gw2": f32("L0_ffn_gc_w2"), "gb2": f32("L0_ffn_gc_b2"),
        "gw3": f32("L0_ffn_gc_w3"),
        "spack": np.concatenate([
            s11("L0_ffn_cn_b4"), s11("L0_ffn_gc_b3"), s11("L0_ffn_adaptive_range"),
            s11("L0_attn_decay_logit"), s11("L0_attn_out_scale")], axis=1),
        "cfs": np.ascontiguousarray(np.stack([
            f32("L0_attn_q_coeffs"), f32("L0_attn_k_coeffs"),
            f32("L0_attn_v_coeffs"), f32("L0_attn_o_coeffs")], axis=0)),
    }
    shared.update(_consts(rpc))
    in_maps = []
    for core in range(NCORES):
        b, r = core // GRP, core % GRP
        m = dict(shared)
        m["x"] = np.ascontiguousarray(x[b, r * rpc:(r + 1) * rpc, :])
        m.update(_core_consts(rpc, r))
        in_maps.append(m)
    return in_maps


def kernel(**inputs):
    in_maps = make_in_maps(inputs)
    nc = build_nc()
    res = run_bass_kernel_spmd(nc, in_maps, core_ids=list(range(NCORES)))
    y = np.empty((B, T, D), dtype=np.float32)
    for core in range(NCORES):
        b, r = core // GRP, core % GRP
        y[b, r * RPC:(r + 1) * RPC, :] = res.results[core]["y"]
    return y
